# version 22
# baseline (speedup 1.0000x reference)
import contextlib
import numpy as np
import concourse.bass as bass
import concourse.mybir as mybir
from concourse.bass_utils import run_bass_kernel_spmd
from concourse.alu_op_type import AluOpType as OP

F32 = mybir.dt.float32
BF16 = mybir.dt.bfloat16
AF = mybir.ActivationFunctionType

MAGIC = float(3 * 2 ** 22)

L = 4096
DM = 1024
DI = 2048
NS = 16
DTR = 64
HID = 4096
SEC = 512
NSEC = L // SEC
SS = 1024
NSS = L // SS
G = DI // 128
FT = DM // 128
HT = HID // 128
EPS = 1e-5


class Tok:
    __slots__ = ("sem", "val")

    def __init__(self, sem, val):
        self.sem = sem
        self.val = val


class Buf:
    __slots__ = ("ap", "ready", "readers", "name")

    def __init__(self, ap, name=""):
        self.ap = ap
        self.ready = []
        self.readers = []
        self.name = name

    def __getitem__(self, idx):
        return self.ap[idx]


class Ctx:
    STREAMS = ("pe", "act", "dve", "pool", "sp", "gp")
    SP_LANES = tuple(f"spd{i}" for i in range(16))
    ACT_LANES = tuple(f"actd{i}" for i in range(8))
    GP_LANES = tuple(f"gpd{i}" for i in range(8))
    SEMS = STREAMS + SP_LANES + ACT_LANES + GP_LANES

    def __init__(self):
        self.q = {s: [] for s in self.STREAMS}
        self.q["gp"] = self.q["pool"]
        self.cnt = {s: 0 for s in self.SEMS}
        self.waited = {s: {} for s in self.STREAMS}
        self.waited["gp"] = self.waited["pool"]
        self.lane_i = {"sp": 0, "act": 0, "gp": 0}

    def _emit(self, stream, builder, waits, inc_sem, inc_n, prewait=None):
        ws = []
        for t in waits:
            if t is None:
                continue
            prev = self.waited[stream].get(t.sem, -1)
            if t.val <= prev:
                continue
            self.waited[stream][t.sem] = t.val
            ws.append((t.sem, t.val))
        if prewait is not None:
            s, v = prewait
            if self.waited[stream].get(s, -1) < v:
                self.waited[stream][s] = v
                ws.append((s, v))
        tok = None
        if inc_sem is not None:
            self.cnt[inc_sem] += inc_n
            tok = Tok(inc_sem, self.cnt[inc_sem])

        def thunk(eng, sems, ws=ws, builder=builder, inc_sem=inc_sem, inc_n=inc_n):
            for s, v in ws:
                eng.wait_ge(sems[s], v)
            inst = builder(eng)
            if inc_sem is not None:
                inst.then_inc(sems[inc_sem], inc_n)

        self.q[stream].append(thunk)
        return tok

    def op(self, stream, builder, reads=(), writes=(), extra=(), inc=True):
        waits = []
        for b in reads:
            if isinstance(b, Buf):
                waits.extend(b.ready)
        for b in writes:
            waits.extend(b.ready)
            waits.extend(b.readers)
        waits.extend(extra)
        sem = stream if inc else None
        tok = self._emit(stream, builder, waits, sem, 1 if inc else 0)
        if tok is not None:
            for b in writes:
                b.ready = [tok]
                b.readers = []
            for b in reads:
                if isinstance(b, Buf):
                    b.readers.append(tok)
        return tok

    def dma(self, stream, out, in_, reads=(), writes=(), extra=()):
        lanes = {"sp": self.SP_LANES, "act": self.ACT_LANES,
                 "gp": self.GP_LANES}[stream]
        lane = lanes[self.lane_i[stream] % len(lanes)]
        self.lane_i[stream] += 1
        if isinstance(out, Buf) and out not in writes:
            writes = tuple(writes) + (out,)
        if isinstance(in_, Buf) and in_ not in reads:
            reads = tuple(reads) + (in_,)
        waits = []
        for b in reads:
            if isinstance(b, Buf):
                waits.extend(b.ready)
        for b in writes:
            waits.extend(b.ready)
            waits.extend(b.readers)
        waits.extend(extra)
        out_ap = out.ap[:] if isinstance(out, Buf) else out
        in_ap = in_.ap[:] if isinstance(in_, Buf) else in_

        def builder(eng, o=out_ap, i=in_ap):
            return eng.dma_start(out=o, in_=i)

        tok = self._emit(stream, builder, waits, lane, 16,
                         prewait=(lane, self.cnt[lane]))
        for b in writes:
            b.ready = [tok]
            b.readers = []
        for b in reads:
            if isinstance(b, Buf):
                b.readers.append(tok)
        return tok

    def dma_multi(self, stream, pairs, wbuf, reads=(), extra=()):
        lanes = {"sp": self.SP_LANES, "act": self.ACT_LANES,
                 "gp": self.GP_LANES}[stream]
        old_waits = list(wbuf.ready) + list(wbuf.readers) + list(extra)
        for b in reads:
            if isinstance(b, Buf):
                old_waits.extend(b.ready)
        toks = []
        for out_ap, in_ap in pairs:
            lane = lanes[self.lane_i[stream] % len(lanes)]
            self.lane_i[stream] += 1

            def builder(eng, o=out_ap, i=in_ap):
                return eng.dma_start(out=o, in_=i)

            toks.append(self._emit(stream, builder, old_waits, lane, 16,
                                   prewait=(lane, self.cnt[lane])))
        wbuf.ready = list(toks)
        wbuf.readers = []
        for b in reads:
            if isinstance(b, Buf):
                b.readers.extend(toks)
        return toks


class Rot:

    def __init__(self, bufs):
        self.bufs = bufs
        self.i = 0

    def get(self):
        b = self.bufs[self.i % len(self.bufs)]
        self.i += 1
        return b


def build_program(taps=False):
    nc = bass.Bass("TRN2", target_bir_lowering=False)

    dram = {}

    def din(name, shape, dtype=F32):
        dram[name] = nc.dram_tensor(name, shape, dtype, kind="ExternalInput").ap()
        return dram[name]

    def dscratch(name, shape, dtype):
        dram[name] = nc.dram_tensor(name, shape, dtype, kind="Internal").ap()
        return dram[name]

    xT = din("xT", [DM, L])
    xT4 = din("xT4", [128, L * FT])
    W_in = din("W_in", [DM, 2 * DI])
    convw = din("convw", [128, G * 4])
    Wx = din("Wx", [DI, 128])
    Wdt = din("Wdt", [DTR, DI])
    A_log_t = din("A_log_t", [128, G * NS])
    D_diag = din("D_diag", [DI, 128])
    W_out = din("W_out", [DI, DM])
    w1T = din("w1T", [DM, HID])
    w2T = din("w2T", [HID, DM])
    wmod = din("wmod", [DM, 6 * DM])
    eye = din("eye", [128, 128])
    v_gamma = din("v_gamma", [128, FT])
    v_beta = din("v_beta", [128, FT])
    v_b_in_u = din("v_b_in_u", [128, G])
    v_b_in_r = din("v_b_in_r", [128, G])
    v_conv_b = din("v_conv_b", [128, G])
    v_b_dt = din("v_b_dt", [128, G])
    v_b_out = din("v_b_out", [128, FT])
    v_b1 = din("v_b1", [128, HT])
    v_b2 = din("v_b2", [128, FT])
    v_c = din("v_c", [128, FT])

    outT = nc.dram_tensor("outT", [DM, L], F32, kind="ExternalOutput").ap()

    win_bf = dscratch("win_bf", [128, FT * 2 * DI], BF16)
    wout_bf = dscratch("wout_bf", [128, G * DM], BF16)
    w1q_bf = dscratch("w1q_bf", [128, FT * HID], BF16)
    w2q_bf = dscratch("w2q_bf", [128, HT * DM], BF16)
    t2_hbm = dscratch("t2_hbm", [DM, L], BF16)
    yg3_hbm = dscratch("yg3_hbm", [128, NSEC * G * SEC], BF16)
    bc_hbm = dscratch("bc_hbm", [4, NS * SS], BF16)
    su_hbm = dscratch("su_hbm", [DI, L], BF16)
    sres_hbm = dscratch("sres_hbm", [DI, L], BF16)
    trk = {nm: Buf(None, nm) for nm in ("win", "wout", "w1q", "w2q")}
    for nm in ("t2", "su", "sres", "yg"):
        for s_ in range(NSEC):
            trk[(nm, s_)] = Buf(None, f"{nm}{s_}")
    for nm in ("bcB", "bcC"):
        for s_ in range(2):
            trk[(nm, s_)] = Buf(None, f"{nm}{s_}")

    taps_out = {}
    if taps:
        for nm, shape, tdt in [("tap_xn", [DM, L], BF16), ("tap_su", [DI, L], BF16),
                               ("tap_delta", [DI, L], BF16), ("tap_B", [NS, L], BF16),
                               ("tap_C", [NS, L], BF16), ("tap_yg", [DI, L], BF16),
                               ("tap_x1", [DM, SEC], BF16), ("tap_t2", [DM, L], BF16),
                               ("tap_x", [DM, L], F32), ("tap_modv", [128, 48], F32),
                               ("tap_sc", [128, FT], F32), ("tap_wq", [128, 128], BF16),
                               ("tap_scbf", [128, FT], BF16), ("tap_bank0", [128, 1], F32),
                               ("tap_snap", [128, FT], BF16)]:
            taps_out[nm] = nc.dram_tensor(nm, shape, tdt, kind="ExternalOutput").ap()

    st = contextlib.ExitStack()
    sbufs_bytes = [0]

    def sb(name, shape, dtype):
        t = st.enter_context(nc.sbuf_tensor(name, shape, dtype))
        esz = 4 if dtype == F32 else 2
        sbufs_bytes[0] += shape[1] * esz
        return Buf(t, name)

    def sbrot(name, n, shape, dtype):
        return Rot([sb(f"{name}{i}", shape, dtype) for i in range(n)])

    c_ones = sb("c_ones", [128, 1], BF16)
    c_eye = sb("c_eye", [128, 128], BF16)
    c_A = sb("c_A", [128, G * NS], F32)
    c_convw = sb("c_convw", [128, G * 4], F32)
    c_Dd = sb("c_Dd", [128, G * 128], BF16)
    c_Wx = sb("c_Wx", [128, G * 128], BF16)
    c_Wdt = sb("c_Wdt", [64, DI], BF16)
    c_gam = sb("c_gam", [128, FT], F32)
    c_bet = sb("c_bet", [128, FT], F32)
    c_biu = sb("c_biu", [128, G], F32)
    c_bir = sb("c_bir", [128, G], F32)
    c_bir_h = sb("c_bir_h", [128, G], F32)
    c_bconv = sb("c_bconv", [128, G], F32)
    c_bdt = sb("c_bdt", [128, G], F32)
    c_bout = sb("c_bout", [128, FT], F32)
    c_b1m = sb("c_b1m", [128, HT], F32)
    c_b2q = sb("c_b2q", [128, FT], F32)
    c_sa = sb("c_sa", [128, FT], F32)
    c_ca1 = sb("c_ca1", [128, FT], F32)
    c_ga = sb("c_ga", [128, FT], F32)
    c_gb = sb("c_gb", [128, FT], F32)
    c_s16cb = sb("c_s16cb", [128, FT], F32)
    c_b16sb = sb("c_b16sb", [128, FT], F32)
    st_h = sb("st_h", [128, G * NS], F32)
    st_halo = sb("st_halo", [128, G * 3], BF16)
    c_cst = sb("c_cst", [128, 7], F32)

    QC = SEC // 2
    r_x = sbrot("r_x", 1, [128, FT * QC], F32)
    r_xbf = sbrot("r_xbf", 2, [128, FT * QC], BF16)
    b_t2 = sb("b_t2", [128, FT * SEC], BF16)
    b_xn = sb("b_xn", [128, FT * SEC], BF16)
    r_uraw = sbrot("r_uraw", 2, [128, SEC + 3], BF16)
    b_su = sb("b_su", [128, G * SEC], BF16)
    r_wsl = sbrot("r_wsl", 2, [128, 2048], BF16)
    r_small = sbrot("r_small", 3, [128, SEC], BF16)
    r_f32 = sbrot("r_f32", 3, [128, SEC], F32)
    r_rep = sbrot("r_rep", 3, [128, QC], BF16)
    r_dt = sbrot("r_dt", 2, [64, SS], BF16)
    r_Bsb = sbrot("r_Bsb", 2, [16, SS], BF16)
    r_Csb = sbrot("r_Csb", 2, [16, SS], BF16)
    b_dl = sb("b_dl", [128, 2 * SS], BF16)
    b_du = sb("b_du", [128, 2 * SS], BF16)
    r_sug = sbrot("r_sug", 2, [128, SS], BF16)
    r_Brep = sbrot("r_Brep", 2, [128, 2 * SS], BF16)
    r_Crep = sbrot("r_Crep", 2, [128, 2 * SS], BF16)
    r_dA = sbrot("r_dA", 2, [128, SS], BF16)
    r_bt = sbrot("r_bt", 2, [128, SS], BF16)
    r_h = sbrot("r_h", 2, [128, SS], F32)
    r_tmp = sbrot("r_tmp", 3, [128, SS], BF16)
    r_yv = sbrot("r_yv", 1, [128, SS], BF16)
    r_sres = sbrot("r_sres", 1, [128, SS], BF16)
    b_x1 = sb("b_x1", [128, FT * SEC], BF16)
    b_xq = sb("b_xq", [128, FT * SEC], BF16)
    b_h2q = sb("b_h2q", [128, HT * SEC], BF16)
    r_wst = sbrot("r_wst", 2, [128, 256], F32)
    r_wbf = sbrot("r_wbf", 2, [128, 256], BF16)
    b_modv = sb("b_modv", [128, 48], F32)
    b_vc = sb("b_vc", [128, FT], F32)
    r_tiny = sbrot("r_tiny", 3, [128, FT], F32)
    r_stat = sbrot("r_stat", 3, [1, QC], F32)
    r_statb = sbrot("r_statb", 2, [1, QC], BF16)

    psum = [Buf(st.enter_context(nc.psum_tensor(f"ps{i}", [128, 512], F32)), f"ps{i}")
            for i in range(8)]
    PS_Y = psum[0:4]
    PS_W = psum[4:8]

    sems = {}
    for s in Ctx.SEMS:
        sems[s] = st.enter_context(nc.semaphore(f"sem_{s}"))

    ctx = Ctx()

    def act(builder, reads=(), writes=(), extra=()):
        return ctx.op("act", builder, reads, writes, extra)

    def dve(builder, reads=(), writes=(), extra=()):
        return ctx.op("dve", builder, reads, writes, extra)

    def pool(builder, reads=(), writes=(), extra=()):
        return ctx.op("pool", builder, reads, writes, extra)

    def pe(builder, reads=(), writes=(), extra=(), inc=True):
        return ctx.op("pe", builder, reads, writes, extra, inc=inc)

    def dma_sp(out, in_, reads=(), writes=(), extra=()):
        return ctx.dma("sp", out, in_, reads, writes, extra)

    def dma_act(out, in_, reads=(), writes=(), extra=()):
        return ctx.dma("act", out, in_, reads, writes, extra)

    def dma_gp(out, in_, reads=(), writes=(), extra=()):
        return ctx.dma("gp", out, in_, reads, writes, extra)

    def bcast(dst_buf, src_ap, cols, reads=()):
        in_b = src_ap.unsqueeze(1).broadcast_to((1, 128, cols))
        return ctx.dma("act", dst_buf.ap[:, 0:cols], in_b, reads=reads,
                       writes=(dst_buf,))

    cst_vals = [EPS, 1.0 / 64.0, 1.0 / 1024.0, -0.5, 0.5, 1.0, MAGIC]
    cst_tok = None
    for i, v in enumerate(cst_vals):
        cst_tok = dve(lambda e, i=i, v=v: e.memset(c_cst.ap[:, i:i + 1], v),
                      writes=(c_cst,))
        try:
            nc.const_aps.register(F32, float(v), c_cst.ap[:, i:i + 1])
        except Exception:
            pass
    ctx._emit("act", lambda e: e.nop(), [cst_tok], None, 0)
    ctx._emit("pool", lambda e: e.nop(), [cst_tok], None, 0)

    dma_sp(c_A, A_log_t)
    dma_sp(c_convw, convw)
    dma_sp(c_gam, v_gamma)
    dma_sp(c_bet, v_beta)
    dma_sp(c_biu, v_b_in_u)
    dma_sp(c_bir, v_b_in_r)
    dma_sp(c_bconv, v_conv_b)
    dma_sp(c_bdt, v_b_dt)
    dma_sp(c_bout, v_b_out)
    dma_sp(b_vc, v_c)

    eye_f = r_wst.get()
    dma_sp(eye_f.ap[:, 0:128], eye, writes=(eye_f,))
    dve(lambda e: e.tensor_copy(c_eye.ap[:], eye_f.ap[:, 0:128]),
        reads=(eye_f,), writes=(c_eye,))
    dve(lambda e: e.memset(c_ones.ap[:], 1.0), writes=(c_ones,))
    dve(lambda e: e.memset(st_h.ap[:], 0.0), writes=(st_h,))
    dve(lambda e: e.memset(st_halo.ap[:], 0.0), writes=(st_halo,))

    tA = r_wst.get()
    act(lambda e: e.activation(tA.ap[:, 0:G * NS], c_A.ap[:], AF.Exp),
        reads=(c_A,), writes=(tA,))
    dve(lambda e: e.tensor_scalar(c_A.ap[:], tA.ap[:, 0:G * NS], -1.0, None, op0=OP.mult),
        reads=(tA,), writes=(c_A,))

    dve(lambda e: e.tensor_scalar(c_bir_h.ap[:], c_bir.ap[:], 0.5, None, op0=OP.mult),
        reads=(c_bir,), writes=(c_bir_h,))

    for g in range(G):
        w = r_wst.get()
        dma_sp(w.ap[:, 0:128], D_diag[g * 128:(g + 1) * 128, :], writes=(w,))
        dve(lambda e, g=g, w=w: e.tensor_scalar(
            c_Dd.ap[:, g * 128:(g + 1) * 128], w.ap[:, 0:128], 1.0, None, op0=OP.mult),
            reads=(w,), writes=(c_Dd,))

    for g in range(G):
        w = r_wst.get()
        dma_sp(w.ap[:, 0:128], Wx[g * 128:(g + 1) * 128, :], writes=(w,))
        dve(lambda e, g=g, w=w: e.tensor_copy(
            c_Wx.ap[:, g * 128:(g + 1) * 128], w.ap[:, 0:128]),
            reads=(w,), writes=(c_Wx,))
    for hw in range(8):
        wdt_f = r_wst.get()
        dma_sp(wdt_f.ap[0:64, 0:256], Wdt[0:64, hw * 256:(hw + 1) * 256],
               writes=(wdt_f,))
        dve(lambda e, hw=hw, wdt_f=wdt_f: e.tensor_copy(
            c_Wdt.ap[0:64, hw * 256:(hw + 1) * 256], wdt_f.ap[0:64, 0:256]),
            reads=(wdt_f,), writes=(c_Wdt,))

    wb = r_wst.get()
    dma_sp(wb.ap[:, 0:HT], v_b1, writes=(wb,))
    q1 = r_f32.get()
    dve(lambda e, q1=q1, wb=wb: e.tensor_scalar(
        q1.ap[:, 0:HT], wb.ap[:, 0:HT], 64.0, MAGIC,
        op0=OP.mult, op1=OP.add), reads=(wb,), writes=(q1,))
    q2 = r_f32.get()
    dve(lambda e, q1=q1, q2=q2: e.tensor_scalar(
        q2.ap[:, 0:HT], q1.ap[:, 0:HT], MAGIC, 127.0,
        op0=OP.subtract, op1=OP.min), reads=(q1,), writes=(q2,))
    q3 = r_f32.get()
    dve(lambda e, q2=q2, q3=q3: e.tensor_scalar(
        q3.ap[:, 0:HT], q2.ap[:, 0:HT], -128.0, None,
        op0=OP.max), reads=(q2,), writes=(q3,))
    dve(lambda e, q3=q3: e.tensor_scalar(
        c_b1m.ap[:], q3.ap[:, 0:HT], 0.25, MAGIC,
        op0=OP.mult, op1=OP.add), reads=(q3,), writes=(c_b1m,))
    wb2 = r_wst.get()
    dma_sp(wb2.ap[:, 0:FT], v_b2, writes=(wb2,))
    q1 = r_tiny.get()
    dve(lambda e, q1=q1, wb2=wb2: e.tensor_scalar(
        q1.ap[:, 0:FT], wb2.ap[:, 0:FT], 64.0, MAGIC,
        op0=OP.mult, op1=OP.add), reads=(wb2,), writes=(q1,))
    q2 = r_tiny.get()
    dve(lambda e, q1=q1, q2=q2: e.tensor_scalar(
        q2.ap[:, 0:FT], q1.ap[:, 0:FT], MAGIC, 127.0,
        op0=OP.subtract, op1=OP.min), reads=(q1,), writes=(q2,))
    q3 = r_tiny.get()
    dve(lambda e, q2=q2, q3=q3: e.tensor_scalar(
        q3.ap[:, 0:FT], q2.ap[:, 0:FT], -128.0, None,
        op0=OP.max), reads=(q2,), writes=(q3,))
    dve(lambda e, q3=q3: e.tensor_scalar(
        c_b2q.ap[:], q3.ap[:, 0:FT], 1.0 / 64.0, None,
        op0=OP.mult), reads=(q3,), writes=(c_b2q,))

    tc = r_tiny.get()
    act(lambda e: e.activation(tc.ap[:, 0:FT], b_vc.ap[:], AF.Tanh, scale=0.5),
        reads=(b_vc,), writes=(tc,))
    scv = r_tiny.get()
    dve(lambda e: e.scalar_tensor_tensor(scv.ap[:, 0:FT], tc.ap[:, 0:FT], 1.0,
                                         b_vc.ap[:], op0=OP.add, op1=OP.mult),
        reads=(tc, b_vc), writes=(scv,))
    sc_bf = r_small.get()
    dve(lambda e: e.tensor_copy(sc_bf.ap[:, 0:FT], scv.ap[:, 0:FT]),
        reads=(scv,), writes=(sc_bf,))
    if taps:
        dma_sp(taps_out["tap_sc"], scv.ap[:, 0:FT], reads=(scv,))
        dma_sp(taps_out["tap_scbf"], sc_bf.ap[:, 0:FT], reads=(sc_bf,))
        snap = sb("dbg_snap", [128, FT], BF16)
        dve(lambda e, snap=snap: e.tensor_copy(snap.ap[:], sc_bf.ap[:, 0:FT]),
            reads=(sc_bf,), writes=(snap,))
        dma_sp(taps_out["tap_snap"], snap.ap[:], reads=(snap,))
    for m in range(48):
        bank = PS_W[m % 2]
        group_bufs = [sc_bf]
        for k in range(FT):
            w = r_wst.get()
            dma_sp(w.ap[:, 0:128], wmod[k * 128:(k + 1) * 128,
                                        m * 128:m * 128 + 128], writes=(w,))
            wq = r_wsl.get()
            dve(lambda e, w=w, wq=wq: e.tensor_copy(wq.ap[:, 0:128], w.ap[:, 0:128]),
                reads=(w,), writes=(wq,))
            group_bufs.append(wq)
            if taps and m == 0 and k == 0:
                dma_sp(taps_out["tap_wq"], wq.ap[:, 0:128], reads=(wq,))
            tok = pe(lambda e, wq=wq, k=k, bank=bank: e.matmul(
                bank.ap[:, 0:1], wq.ap[:, 0:128], sc_bf.ap[:, k:k + 1],
                start=(k == 0), stop=(k == FT - 1)),
                reads=(wq, sc_bf), writes=(bank,) if k == FT - 1 else (),
                extra=tuple(bank.ready) + tuple(bank.readers) if k == 0 else (),
                inc=True)
        if taps and m == 0:
            dbg = r_f32.get()
            act(lambda e, bank=bank, dbg=dbg: e.activation(
                dbg.ap[:, 0:1], bank.ap[:, 0:1], AF.Identity),
                reads=(bank,), writes=(dbg,))
            dma_sp(taps_out["tap_bank0"], dbg.ap[:, 0:1], reads=(dbg,))
        act(lambda e, m=m, bank=bank: e.activation(
            b_modv.ap[:, m:m + 1], bank.ap[:, 0:1], AF.Identity, scale=0.5),
            reads=(bank,), writes=(b_modv,))

    dve(lambda e: e.tensor_copy(c_sa.ap[:], b_modv.ap[:, 0:FT]),
        reads=(b_modv,), writes=(c_sa,))
    dve(lambda e: e.tensor_scalar(c_ca1.ap[:], b_modv.ap[:, FT:2 * FT], 1.0, None,
                                  op0=OP.add), reads=(b_modv,), writes=(c_ca1,))

    def clamp_gate(dst, col0):
        gneg = r_tiny.get()
        dve(lambda e: e.tensor_scalar(gneg.ap[:, 0:FT],
                                      b_modv.ap[:, col0:col0 + FT], -1.0, None,
                                      op0=OP.mult), reads=(b_modv,), writes=(gneg,))
        gabs = r_tiny.get()
        dve(lambda e: e.tensor_tensor(gabs.ap[:, 0:FT], gneg.ap[:, 0:FT],
                                      b_modv.ap[:, col0:col0 + FT], OP.max),
            reads=(gneg, b_modv), writes=(gabs,))
        gmag = r_tiny.get()
        dve(lambda e: e.tensor_scalar(gmag.ap[:, 0:FT], gabs.ap[:, 0:FT], 0.25, None,
                                      op0=OP.max), reads=(gabs,), writes=(gmag,))
        gge = r_tiny.get()
        dve(lambda e: e.tensor_scalar(gge.ap[:, 0:FT],
                                      b_modv.ap[:, col0:col0 + FT], 0.0, None,
                                      op0=OP.is_ge), reads=(b_modv,), writes=(gge,))
        gsgn = r_tiny.get()
        dve(lambda e: e.tensor_scalar(gsgn.ap[:, 0:FT], gge.ap[:, 0:FT], 2.0, -1.0,
                                      op0=OP.mult, op1=OP.add), reads=(gge,), writes=(gsgn,))
        dve(lambda e: e.tensor_tensor(dst.ap[:], gmag.ap[:, 0:FT], gsgn.ap[:, 0:FT],
                                      OP.mult), reads=(gmag, gsgn), writes=(dst,))

    clamp_gate(c_ga, 2 * FT)
    clamp_gate(c_gb, 5 * FT)
    cb1 = r_tiny.get()
    dve(lambda e: e.tensor_scalar(cb1.ap[:, 0:FT], b_modv.ap[:, 4 * FT:5 * FT],
                                  1.0, None, op0=OP.add), reads=(b_modv,), writes=(cb1,))
    dve(lambda e: e.tensor_scalar(c_s16cb.ap[:], cb1.ap[:, 0:FT], 16.0, None,
                                  op0=OP.mult), reads=(cb1,), writes=(c_s16cb,))
    dve(lambda e: e.tensor_scalar(c_b16sb.ap[:], b_modv.ap[:, 3 * FT:4 * FT],
                                  16.0, MAGIC, op0=OP.mult, op1=OP.add),
        reads=(b_modv,), writes=(c_b16sb,))
    if taps:
        dma_sp(taps_out["tap_modv"], b_modv.ap[:], reads=(b_modv,))

    def prep_weight(src, dst, rows, cols, quant, track, do_yield=False):
        nk = rows // 128
        piece = 0
        for r0 in range(0, rows, 128):
            for cc0 in range(0, cols, 256):
                cw = min(256, cols - cc0)
                w = r_wst.get()
                dma_sp(w.ap[:, 0:cw], src[r0:r0 + 128, cc0:cc0 + cw], writes=(w,))
                wbf = r_wbf.get()
                if quant:
                    t1 = r_wst.get()
                    act(lambda e, w=w, t1=t1, cw=cw: e.activation(
                        t1.ap[:, 0:cw], w.ap[:, 0:cw], AF.Identity,
                        scale=64.0, bias=c_cst.ap[:, 6:7]),
                        reads=(w, c_cst), writes=(t1,))
                    dve(lambda e, t1=t1, cw=cw: e.tensor_scalar(
                        t1.ap[:, 0:cw], t1.ap[:, 0:cw], MAGIC, 127.0,
                        op0=OP.subtract, op1=OP.min), reads=(t1,), writes=(t1,))
                    dve(lambda e, t1=t1, wbf=wbf, cw=cw: e.tensor_scalar(
                        wbf.ap[:, 0:cw], t1.ap[:, 0:cw], -128.0, None,
                        op0=OP.max), reads=(t1,), writes=(wbf,))
                else:
                    act(lambda e, w=w, wbf=wbf, cw=cw: e.activation(
                        wbf.ap[:, 0:cw], w.ap[:, 0:cw], AF.Identity),
                        reads=(w,), writes=(wbf,))
                kk = r0 // 128
                wv = cc0 // 512
                dof = wv * (nk * 512) + kk * 512 + (cc0 % 512)
                dma_sp(dst[0:128, dof:dof + cw], wbf.ap[:, 0:cw],
                       reads=(wbf,), writes=(track,))
                piece += 1
                if do_yield and piece % 2 == 0:
                    yield

    for _ in prep_weight(W_in, win_bf, DM, 2 * DI, False, trk["win"]):
        pass

    def layernorm_stats(src_bf, col0, n_feat_tiles, width, sq_buf, mu_rep, r_rep_):
        nfeat = float(n_feat_tiles * 128)
        cols = [col0(k) for k in range(n_feat_tiles)]
        col0 = lambda k, cols=cols: cols[k]
        for k in range(n_feat_tiles):
            dve(lambda e, k=k: e.tensor_tensor(
                sq_buf.ap[:, k * width:(k + 1) * width],
                src_bf.ap[:, col0(k):col0(k) + width],
                src_bf.ap[:, col0(k):col0(k) + width], OP.mult),
                reads=(src_bf,), writes=(sq_buf,) if k == n_feat_tiles - 1 else (),
                extra=tuple(sq_buf.ready) + tuple(sq_buf.readers) if k == 0 else ())
        bank_s = PS_W[0]
        bank_q = PS_W[1]
        for k in range(n_feat_tiles):
            t_s = pe(lambda e, k=k: e.matmul(
                bank_s.ap[0:1, 0:width], c_ones.ap[:],
                src_bf.ap[:, col0(k):col0(k) + width],
                start=(k == 0), stop=(k == n_feat_tiles - 1)),
                reads=(c_ones, src_bf),
                writes=(bank_s,) if k == n_feat_tiles - 1 else (),
                extra=tuple(bank_s.ready) + tuple(bank_s.readers) if k == 0 else (),
                inc=(k == n_feat_tiles - 1))
        src_bf.readers.append(t_s)
        for k in range(n_feat_tiles):
            t_q = pe(lambda e, k=k: e.matmul(
                bank_q.ap[0:1, 0:width], c_ones.ap[:],
                sq_buf.ap[:, k * width:(k + 1) * width],
                start=(k == 0), stop=(k == n_feat_tiles - 1)),
                reads=(c_ones, sq_buf),
                writes=(bank_q,) if k == n_feat_tiles - 1 else (),
                extra=tuple(bank_q.ready) + tuple(bank_q.readers) if k == 0 else (),
                inc=(k == n_feat_tiles - 1))
        sq_buf.readers.append(t_q)
        mu = r_stat.get()
        dve(lambda e: e.tensor_scalar(mu.ap[0:1, 0:width], bank_s.ap[0:1, 0:width],
                                      1.0 / nfeat, None, op0=OP.mult),
            reads=(bank_s,), writes=(mu,))
        mub = r_statb.get()
        dve(lambda e: e.tensor_copy(mub.ap[0:1, 0:width], mu.ap[0:1, 0:width]),
            reads=(mu,), writes=(mub,))
        m2 = r_stat.get()
        dve(lambda e: e.tensor_tensor(m2.ap[0:1, 0:width], mu.ap[0:1, 0:width],
                                      mu.ap[0:1, 0:width], OP.mult),
            reads=(mu,), writes=(m2,))
        var = r_stat.get()
        dve(lambda e: e.scalar_tensor_tensor(var.ap[0:1, 0:width],
                                             bank_q.ap[0:1, 0:width], 1.0 / nfeat,
                                             m2.ap[0:1, 0:width],
                                             op0=OP.mult, op1=OP.subtract),
            reads=(bank_q, m2), writes=(var,))
        lnv = r_stat.get()
        act(lambda e: e.activation(lnv.ap[0:1, 0:width], var.ap[0:1, 0:width],
                                   AF.Ln, bias=EPS), reads=(var,), writes=(lnv,))
        rin = r_stat.get()
        act(lambda e: e.activation(rin.ap[0:1, 0:width], lnv.ap[0:1, 0:width],
                                   AF.Exp, scale=-0.5), reads=(lnv,), writes=(rin,))
        rb = r_statb.get()
        dve(lambda e: e.tensor_copy(rb.ap[0:1, 0:width], rin.ap[0:1, 0:width]),
            reads=(rin,), writes=(rb,))
        bcast(mu_rep, mub.ap[0:1, 0:width], width, reads=(mub,))
        bcast(r_rep_, rb.ap[0:1, 0:width], width, reads=(rb,))

    def phase_a1(s):
        t0 = s * SEC
        for hq in range(2):
            tq = t0 + hq * QC
            xb = r_x.get()
            qi = tq // QC
            dma_sp(xb, xT4[0:128, qi * (FT * QC):(qi + 1) * (FT * QC)])
            if taps:
                dma_sp(taps_out["tap_x"].rearrange("(ft p) t -> p ft t", p=128)[:, :, tq:tq + QC],
                       xb.ap[:].rearrange("p (ft t) -> p ft t", t=QC), reads=(xb,))
            xbf = r_xbf.get()
            dve(lambda e, xb=xb, xbf=xbf: e.tensor_copy(xbf.ap[:], xb.ap[:]),
                reads=(xb,), writes=(xbf,))
            sq = r_xbf.get()
            mu1 = r_rep.get()
            rr1 = r_rep.get()
            layernorm_stats(xbf, lambda k: k * QC, FT, QC, sq, mu1, rr1)
            for ft in range(FT):
                d1 = r_small.get()
                dve(lambda e, ft=ft, d1=d1, xb=xb, mu1=mu1: e.tensor_tensor(
                    d1.ap[:, 0:QC], xb.ap[:, ft * QC:(ft + 1) * QC],
                    mu1.ap[:, 0:QC], OP.subtract), reads=(xb, mu1), writes=(d1,))
                d2 = r_small.get()
                dve(lambda e, d1=d1, d2=d2, rr1=rr1: e.tensor_tensor(
                    d2.ap[:, 0:QC], d1.ap[:, 0:QC], rr1.ap[:, 0:QC], OP.mult),
                    reads=(d1, rr1), writes=(d2,))
                act(lambda e, ft=ft, hq=hq, d2=d2: e.activation(
                    b_t2.ap[:, ft * SEC + hq * QC:ft * SEC + (hq + 1) * QC],
                    d2.ap[:, 0:QC], AF.Identity,
                    scale=c_ca1.ap[:, ft:ft + 1], bias=c_sa.ap[:, ft:ft + 1]),
                    reads=(d2, c_ca1, c_sa), writes=(b_t2,))
            sq2 = r_xbf.get()
            mu2 = r_rep.get()
            rr2 = r_rep.get()
            layernorm_stats(b_t2, lambda k: k * SEC + hq * QC, FT, QC, sq2, mu2, rr2)
            for ft in range(FT):
                e1 = r_small.get()
                dve(lambda e, ft=ft, hq=hq, e1=e1, mu2=mu2: e.tensor_tensor(
                    e1.ap[:, 0:QC],
                    b_t2.ap[:, ft * SEC + hq * QC:ft * SEC + (hq + 1) * QC],
                    mu2.ap[:, 0:QC], OP.subtract), reads=(b_t2, mu2), writes=(e1,))
                e2 = r_small.get()
                dve(lambda e, e1=e1, e2=e2, rr2=rr2: e.tensor_tensor(
                    e2.ap[:, 0:QC], e1.ap[:, 0:QC], rr2.ap[:, 0:QC], OP.mult),
                    reads=(e1, rr2), writes=(e2,))
                act(lambda e, ft=ft, hq=hq, e2=e2: e.activation(
                    b_xn.ap[:, ft * SEC + hq * QC:ft * SEC + (hq + 1) * QC],
                    e2.ap[:, 0:QC], AF.Identity,
                    scale=c_gam.ap[:, ft:ft + 1], bias=c_bet.ap[:, ft:ft + 1]),
                    reads=(e2, c_gam, c_bet), writes=(b_xn,))
            yield
        dma_gp(t2_hbm.rearrange("(ft p) t -> p ft t", p=128)[:, :, t0:t0 + SEC],
               b_t2.ap[:].rearrange("p (ft t) -> p ft t", t=SEC), reads=(b_t2,),
               writes=(trk[("t2", s)],))
        if taps:
            dma_sp(taps_out["tap_t2"].rearrange("(ft p) t -> p ft t", p=128)[:, :, t0:t0 + SEC],
                   b_t2.ap[:].rearrange("p (ft t) -> p ft t", t=SEC), reads=(b_t2,))
        if taps:
            dma_sp(taps_out["tap_xn"].rearrange("(ft p) t -> p ft t", p=128)[:, :, t0:t0 + SEC],
                   b_xn.ap[:].rearrange("p (ft t) -> p ft t", t=SEC), reads=(b_xn,))

        for wave in range(8):
            for kg in range(2):
                wb = r_wsl.get()
                dof = wave * (FT * 512) + kg * 2048
                dma_sp(wb.ap[:, 0:2048], win_bf[0:128, dof:dof + 2048],
                       writes=(wb,), reads=(trk["win"],))
                for k in range(kg * 4, kg * 4 + 4):
                    ko = (k - kg * 4) * 512
                    for mi in range(4):
                        bank = PS_W[mi]
                        tok = pe(lambda e, k=k, ko=ko, mi=mi, bank=bank, wb=wb: e.matmul(
                            bank.ap[:, 0:SEC],
                            wb.ap[:, ko + mi * 128:ko + (mi + 1) * 128],
                            b_xn.ap[:, k * SEC:(k + 1) * SEC],
                            start=(k == 0), stop=(k == FT - 1)),
                           reads=(wb, b_xn),
                           extra=tuple(bank.ready) + tuple(bank.readers) if k == 0 else (),
                           inc=(mi == 3))
            for mi in range(4):
                PS_W[mi].ready = [tok]
                PS_W[mi].readers = []
            for mi in range(4):
                m = wave * 4 + mi
                bank = PS_W[mi]
                if m < G:
                    g = m
                    ub = r_uraw.get()
                    act(lambda e, g=g, bank=bank, ub=ub: e.activation(
                        ub.ap[:, 3:SEC + 3], bank.ap[:, 0:SEC], AF.Identity,
                        bias=c_biu.ap[:, g:g + 1]),
                        reads=(bank, c_biu), writes=(ub,))
                    _htok = dve(lambda e, g=g, ub=ub: e.tensor_copy(
                        ub.ap[:, 0:3], st_halo.ap[:, g * 3:(g + 1) * 3]),
                        reads=(st_halo,), extra=tuple(ub.ready))
                    ub.ready = list(ub.ready) + [_htok]
                    a1b = r_small.get()
                    dve(lambda e, g=g, ub=ub, a1b=a1b: e.tensor_scalar(
                        a1b.ap[:, 0:SEC], ub.ap[:, 0:SEC],
                        c_convw.ap[:, g * 4:g * 4 + 1], c_bconv.ap[:, g:g + 1],
                        op0=OP.mult, op1=OP.add), reads=(ub, c_convw, c_bconv),
                        writes=(a1b,))
                    a2b = r_small.get()
                    dve(lambda e, g=g, ub=ub, a1b=a1b, a2b=a2b: e.scalar_tensor_tensor(
                        a2b.ap[:, 0:SEC], ub.ap[:, 1:1 + SEC],
                        c_convw.ap[:, g * 4 + 1:g * 4 + 2], a1b.ap[:, 0:SEC],
                        op0=OP.mult, op1=OP.add), reads=(ub, c_convw, a1b),
                        writes=(a2b,))
                    a3b = r_small.get()
                    dve(lambda e, g=g, ub=ub, a2b=a2b, a3b=a3b: e.scalar_tensor_tensor(
                        a3b.ap[:, 0:SEC], ub.ap[:, 2:2 + SEC],
                        c_convw.ap[:, g * 4 + 2:g * 4 + 3], a2b.ap[:, 0:SEC],
                        op0=OP.mult, op1=OP.add), reads=(ub, c_convw, a2b),
                        writes=(a3b,))
                    zb = r_small.get()
                    dve(lambda e, g=g, ub=ub, a3b=a3b, zb=zb: e.scalar_tensor_tensor(
                        zb.ap[:, 0:SEC], ub.ap[:, 3:3 + SEC],
                        c_convw.ap[:, g * 4 + 3:g * 4 + 4], a3b.ap[:, 0:SEC],
                        op0=OP.mult, op1=OP.add), reads=(ub, c_convw, a3b),
                        writes=(zb,))
                    dve(lambda e, g=g, ub=ub: e.tensor_copy(
                        st_halo.ap[:, g * 3:(g + 1) * 3], ub.ap[:, SEC:SEC + 3]),
                        reads=(ub,), writes=(st_halo,))
                    act(lambda e, g=g, zb=zb: e.activation(
                        b_su.ap[:, g * SEC:(g + 1) * SEC], zb.ap[:, 0:SEC],
                        AF.Silu), reads=(zb,), writes=(b_su,))
                else:
                    rti = m - G
                    srt = r_small.get()
                    act(lambda e, rti=rti, bank=bank, srt=srt: e.activation(
                        srt.ap[:, 0:SEC], bank.ap[:, 0:SEC], AF.Silu,
                        bias=c_bir.ap[:, rti:rti + 1]),
                        reads=(bank, c_bir), writes=(srt,))
                    dma_gp(sres_hbm[rti * 128:(rti + 1) * 128, t0:t0 + SEC],
                           srt.ap[:, 0:SEC], reads=(srt,), writes=(trk[("sres", s)],))
            yield

        for h in range(4):
            dma_gp(su_hbm.rearrange("(g p) t -> p g t", p=128)[:, h * 4:(h + 1) * 4,
                                                              t0:t0 + SEC],
                   b_su.ap[:, h * 4 * SEC:(h + 1) * 4 * SEC].rearrange(
                       "p (g t) -> p g t", t=SEC), reads=(b_su,),
                   writes=(trk[("su", s)],))
        yield
        if taps:
            dma_sp(taps_out["tap_su"].rearrange("(g p) t -> p g t", p=128)[:, :, t0:t0 + SEC],
                   b_su.ap[:].rearrange("p (g t) -> p g t", t=SEC), reads=(b_su,))

        bank = PS_W[0]
        for k in range(G):
            tok = pe(lambda e, k=k, bank=bank: e.matmul(
                bank.ap[:, 0:SEC], c_Wx.ap[:, k * 128:(k + 1) * 128],
                b_su.ap[:, k * SEC:(k + 1) * SEC], start=(k == 0), stop=(k == G - 1)),
               reads=(c_Wx, b_su), writes=(bank,) if k == G - 1 else (),
               extra=tuple(bank.ready) + tuple(bank.readers) if k == 0 else (),
               inc=(k == G - 1))
        b_su.readers.append(tok)
        ssec = s // 2
        half = s % 2
        dtb = r_dt.bufs[ssec % 2]
        Bsb = r_Bsb.bufs[ssec % 2]
        Csb = r_Csb.bufs[ssec % 2]
        act(lambda e: e.activation(dtb.ap[0:64, half * SEC:(half + 1) * SEC],
                                   bank.ap[0:64, 0:SEC], AF.Identity),
            reads=(bank,), writes=(dtb,) if half == 1 else (),
            extra=tuple(dtb.ready) + tuple(dtb.readers) if half == 0 else ())
        act(lambda e: e.activation(Bsb.ap[0:16, half * SEC:(half + 1) * SEC],
                                   bank.ap[64:80, 0:SEC], AF.Identity),
            reads=(bank,), writes=(Bsb,) if half == 1 else (),
            extra=tuple(Bsb.ready) + tuple(Bsb.readers) if half == 0 else ())
        act(lambda e: e.activation(Csb.ap[0:16, half * SEC:(half + 1) * SEC],
                                   bank.ap[96:112, 0:SEC], AF.Identity),
            reads=(bank,), writes=(Csb,) if half == 1 else (),
            extra=tuple(Csb.ready) + tuple(Csb.readers) if half == 0 else ())
        if half == 1:
            par = ssec % 2
            dma_gp(bc_hbm[2 * par:2 * par + 1, :].rearrange(
                       "r (n t) -> r n t", t=SS),
                   Bsb.ap[0:16, 0:SS], reads=(Bsb,),
                   writes=(trk[("bcB", ssec % 2)],))
            dma_gp(bc_hbm[2 * par + 1:2 * par + 2, :].rearrange(
                       "r (n t) -> r n t", t=SS),
                   Csb.ap[0:16, 0:SS], reads=(Csb,),
                   writes=(trk[("bcC", ssec % 2)],))

    from collections import deque
    pend = deque()
    pend2 = deque()

    def pump(kk):
        for _ in range(kk):
            if pend:
                try:
                    next(pend[0])
                except StopIteration:
                    pend.popleft()
            if pend2:
                try:
                    next(pend2[0])
                except StopIteration:
                    pend2.popleft()

    def phase_a2(ssec):
        T0 = ssec * SS
        dtb = r_dt.bufs[ssec % 2]
        Bsb = r_Bsb.bufs[ssec % 2]
        Csb = r_Csb.bufs[ssec % 2]
        if taps:
            dma_sp(taps_out["tap_B"][0:NS, T0:T0 + SS], Bsb.ap[0:NS, 0:SS],
                   reads=(Bsb,))
            dma_sp(taps_out["tap_C"][0:NS, T0:T0 + SS], Csb.ap[0:NS, 0:SS],
                   reads=(Csb,))
        for gb in range(G // 2):
            gs = [2 * gb, 2 * gb + 1]
            sug_of = {}
            for gi, g in enumerate(gs):
                sug = r_sug.get()
                sug_of[g] = sug
                dma_sp(sug.ap[:], su_hbm[g * 128:(g + 1) * 128, T0:T0 + SS],
                       writes=(sug,), reads=(trk[("su", 2 * ssec)], trk[("su", 2 * ssec + 1)]))
                for hh in range(2):
                    bank = PS_Y[gi * 2 + hh]
                    pe(lambda e, g=g, hh=hh, bank=bank: e.matmul(
                        bank.ap[:, 0:512], c_Wdt.ap[0:64, g * 128:(g + 1) * 128],
                        dtb.ap[0:64, hh * 512:(hh + 1) * 512], start=True, stop=True),
                       reads=(c_Wdt, dtb), writes=(bank,))
                    sp1 = r_f32.get()
                    dve(lambda e, g=g, bank=bank, sp1=sp1: e.tensor_scalar(
                        sp1.ap[:, 0:512], bank.ap[:, 0:512],
                        c_bdt.ap[:, g:g + 1], 44.0, op0=OP.add, op1=OP.min),
                        reads=(bank, c_bdt), writes=(sp1,))
                    sp2 = r_f32.get()
                    act(lambda e, sp1=sp1, sp2=sp2: e.activation(
                        sp2.ap[:, 0:512], sp1.ap[:, 0:512], AF.Exp),
                        reads=(sp1,), writes=(sp2,))
                    act(lambda e, gi=gi, hh=hh, sp2=sp2: e.activation(
                        b_dl.ap[:, gi * SS + hh * 512:gi * SS + (hh + 1) * 512],
                        sp2.ap[:, 0:512], AF.Ln, bias=1.0),
                        reads=(sp2,), writes=(b_dl,))
                dve(lambda e, gi=gi, sug=sug: e.tensor_tensor(
                    b_du.ap[:, gi * SS:(gi + 1) * SS], sug.ap[:],
                    b_dl.ap[:, gi * SS:(gi + 1) * SS], OP.mult),
                    reads=(sug, b_dl), writes=(b_du,))
            if taps:
                for gi, g in enumerate(gs):
                    dma_sp(taps_out["tap_delta"][g * 128:(g + 1) * 128, T0:T0 + SS],
                           b_dl.ap[:, gi * SS:(gi + 1) * SS], reads=(b_dl,))

            par = ssec % 2
            for n in range(NS):
                if n % 2 == 0:
                    Brep = r_Brep.get()
                    ctx.dma("act", Brep.ap[:, 0:2 * SS],
                            bc_hbm[2 * par:2 * par + 1,
                                   n * SS:(n + 2) * SS].unsqueeze(1)
                            .broadcast_to((1, 128, 2 * SS)),
                            reads=(trk[("bcB", par)],), writes=(Brep,))
                    Crep = r_Crep.get()
                    ctx.dma("act", Crep.ap[:, 0:2 * SS],
                            bc_hbm[2 * par + 1:2 * par + 2,
                                   n * SS:(n + 2) * SS].unsqueeze(1)
                            .broadcast_to((1, 128, 2 * SS)),
                            reads=(trk[("bcC", par)],), writes=(Crep,))
                no = (n % 2) * SS
                for gi, g in enumerate(gs):
                    idx = g * NS + n
                    dA = r_dA.get()
                    act(lambda e, gi=gi, idx=idx, dA=dA: e.activation(
                        dA.ap[:], b_dl.ap[:, gi * SS:(gi + 1) * SS], AF.Exp,
                        scale=c_A.ap[:, idx:idx + 1]),
                        reads=(b_dl, c_A), writes=(dA,))
                    bt = r_bt.get()
                    dve(lambda e, gi=gi, bt=bt, Brep=Brep, no=no: e.tensor_tensor(
                        bt.ap[:], b_du.ap[:, gi * SS:(gi + 1) * SS],
                        Brep.ap[:, no:no + SS],
                        OP.mult), reads=(b_du, Brep), writes=(bt,))
                    hb = r_h.get()
                    dve(lambda e, dA=dA, bt=bt, hb=hb, idx=idx: e.tensor_tensor_scan(
                        hb.ap[:], dA.ap[:], bt.ap[:], st_h.ap[:, idx:idx + 1],
                        op0=OP.mult, op1=OP.add),
                        reads=(dA, bt, st_h), writes=(hb,))
                    dve(lambda e, hb=hb, idx=idx: e.tensor_copy(
                        st_h.ap[:, idx:idx + 1], hb.ap[:, SS - 1:SS]),
                        reads=(hb,), writes=(st_h,))
                    tmp = r_tmp.get()
                    dve(lambda e, hb=hb, tmp=tmp, Crep=Crep, no=no: e.tensor_tensor(
                        tmp.ap[:], hb.ap[:], Crep.ap[:, no:no + SS], OP.mult),
                        reads=(hb, Crep), writes=(tmp,))
                    for hh in range(2):
                        bank = PS_Y[gi * 2 + hh]
                        pe(lambda e, bank=bank, tmp=tmp, hh=hh, n=n: e.matmul(
                            bank.ap[:, 0:512], c_eye.ap[:],
                            tmp.ap[:, hh * 512:(hh + 1) * 512],
                            start=(n == 0), stop=False),
                            reads=(c_eye, tmp),
                            extra=tuple(bank.ready) + tuple(bank.readers) if n == 0 else (),
                            inc=True)
                    pump(1)
            for gi, g in enumerate(gs):
                for hh in range(2):
                    bank = PS_Y[gi * 2 + hh]
                    pe(lambda e, g=g, hh=hh, bank=bank, sug=sug_of[g]: e.matmul(
                        bank.ap[:, 0:512], c_Dd.ap[:, g * 128:(g + 1) * 128],
                        sug.ap[:, hh * 512:(hh + 1) * 512],
                        start=False, stop=True),
                       reads=(c_Dd, sug_of[g]), writes=(bank,))
            for gi, g in enumerate(gs):
                yv = r_yv.get()
                for hh in range(2):
                    bank = PS_Y[gi * 2 + hh]
                    act(lambda e, yv=yv, hh=hh, bank=bank: e.activation(
                        yv.ap[:, hh * 512:(hh + 1) * 512], bank.ap[:, 0:512],
                        AF.Identity),
                        reads=(bank,), writes=(yv,) if hh == 1 else (),
                        extra=tuple(yv.ready) + tuple(yv.readers) if hh == 0 else ())
                sres = r_sres.get()
                dma_sp(sres.ap[:], sres_hbm[g * 128:(g + 1) * 128, T0:T0 + SS],
                       writes=(sres,), reads=(trk[("sres", 2 * ssec)], trk[("sres", 2 * ssec + 1)]))
                yg = r_tmp.get()
                dve(lambda e, yv=yv, sres=sres, yg=yg: e.tensor_tensor(
                    yg.ap[:], yv.ap[:], sres.ap[:], OP.mult),
                    reads=(yv, sres), writes=(yg,))
                for hh in range(2):
                    sidx = 2 * ssec + hh
                    dma_gp(yg3_hbm[0:128, sidx * (G * SEC) + g * SEC:
                                   sidx * (G * SEC) + (g + 1) * SEC],
                           yg.ap[:, hh * SEC:(hh + 1) * SEC],
                           reads=(yg,), writes=(trk[("yg", sidx)],))
                if taps:
                    dma_sp(taps_out["tap_yg"][g * 128:(g + 1) * 128, T0:T0 + SS],
                           yg.ap[:], reads=(yg,))

    def phase_a3_ffn(s):
        t0 = s * SEC
        yield
        ygbufs = {}
        for wave in range(2):
            for k in range(G):
                kg = k // 4
                if k % 4 == 0:
                    wsl = r_wsl.get()
                    dof = wave * (G * 512) + kg * 2048
                    dma_sp(wsl.ap[:, 0:2048], wout_bf[0:128, dof:dof + 2048],
                           writes=(wsl,), reads=(trk["wout"],))
                    wcur = wsl
                    ygb = r_wsl.get()
                    yof = s * (G * SEC) + kg * 4 * SEC
                    dma_sp(ygb.ap[:, 0:2048], yg3_hbm[0:128, yof:yof + 2048],
                           writes=(ygb,), reads=(trk[("yg", s)],))
                    ygbufs[kg] = ygb
                ko = (k - kg * 4) * 512
                ygc = ygbufs[kg]
                for mi in range(4):
                    bank = PS_W[mi]
                    tok = pe(lambda e, bank=bank, w=wcur, mi=mi, ko=ko, k=k,
                             yg=ygc: e.matmul(
                        bank.ap[:, 0:SEC],
                        w.ap[:, ko + mi * 128:ko + (mi + 1) * 128],
                        yg.ap[:, ko:ko + SEC], start=(k == 0), stop=(k == G - 1)),
                       reads=(wcur, ygc),
                       extra=tuple(bank.ready) + tuple(bank.readers) if k == 0 else (),
                       inc=(mi == 3))
                if k % 4 == 3:
                    yield
            for mi in range(4):
                PS_W[mi].ready = [tok]
                PS_W[mi].readers = []
            for mi in range(4):
                m = wave * 4 + mi
                bank = PS_W[mi]
                ob = r_f32.get()
                act(lambda e, m=m, bank=bank, ob=ob: e.activation(
                    ob.ap[:, 0:SEC], bank.ap[:, 0:SEC], AF.Identity,
                    bias=c_bout.ap[:, m:m + 1]), reads=(bank, c_bout), writes=(ob,))
                t2r = r_small.get()
                dma_sp(t2r.ap[:, 0:SEC], t2_hbm[m * 128:(m + 1) * 128, t0:t0 + SEC],
                       writes=(t2r,), reads=(trk[("t2", s)],))
                e1 = r_f32.get()
                dve(lambda e, ob=ob, t2r=t2r, e1=e1: e.tensor_tensor(
                    e1.ap[:, 0:SEC], ob.ap[:, 0:SEC], t2r.ap[:, 0:SEC], OP.add),
                    reads=(ob, t2r), writes=(e1,))
                x3 = r_f32.get()
                dma_sp(x3.ap[:, 0:SEC], xT[m * 128:(m + 1) * 128, t0:t0 + SEC],
                       writes=(x3,))
                dve(lambda e, m=m, e1=e1, x3=x3: e.scalar_tensor_tensor(
                    b_x1.ap[:, m * SEC:(m + 1) * SEC], e1.ap[:, 0:SEC],
                    c_ga.ap[:, m:m + 1], x3.ap[:, 0:SEC],
                    op0=OP.mult, op1=OP.add), reads=(e1, c_ga, x3), writes=(b_x1,))
                yield
        if taps and s == 0:
            dma_sp(taps_out["tap_x1"].rearrange("(ft p) t -> p ft t", p=128)[:, :, :],
                   b_x1.ap[:].rearrange("p (ft t) -> p ft t", t=SEC), reads=(b_x1,))

        for hq in range(2):
            x1bf = r_xbf.get()
            dve(lambda e, hq=hq, x1bf=x1bf: e.tensor_copy(
                x1bf.ap[:].rearrange("p (ft t) -> p ft t", t=QC),
                b_x1.ap[:].rearrange("p (ft t) -> p ft t", t=QC)[:, hq::2, :]),
                reads=(b_x1,), writes=(x1bf,))
            sq4 = r_xbf.get()
            mu4 = r_rep.get()
            rr4 = r_rep.get()
            layernorm_stats(x1bf, lambda k: k * QC, FT, QC, sq4, mu4, rr4)
            for ft in range(FT):
                f1 = r_small.get()
                dve(lambda e, ft=ft, f1=f1, x1bf=x1bf, mu4=mu4: e.tensor_tensor(
                    f1.ap[:, 0:QC], x1bf.ap[:, ft * QC:(ft + 1) * QC],
                    mu4.ap[:, 0:QC], OP.subtract), reads=(x1bf, mu4), writes=(f1,))
                f2 = r_small.get()
                dve(lambda e, f1=f1, f2=f2, rr4=rr4: e.tensor_tensor(
                    f2.ap[:, 0:QC], f1.ap[:, 0:QC], rr4.ap[:, 0:QC], OP.mult),
                    reads=(f1, rr4), writes=(f2,))
                qb1 = r_f32.get()
                act(lambda e, ft=ft, f2=f2, qb1=qb1: e.activation(
                    qb1.ap[:, 0:QC], f2.ap[:, 0:QC], AF.Identity,
                    scale=c_s16cb.ap[:, ft:ft + 1], bias=c_b16sb.ap[:, ft:ft + 1]),
                    reads=(f2, c_s16cb, c_b16sb), writes=(qb1,))
                qb2 = r_f32.get()
                dve(lambda e, qb1=qb1, qb2=qb2: e.tensor_scalar(
                    qb2.ap[:, 0:QC], qb1.ap[:, 0:QC], MAGIC, 127.0,
                    op0=OP.subtract, op1=OP.min), reads=(qb1,), writes=(qb2,))
                dve(lambda e, ft=ft, hq=hq, qb2=qb2: e.tensor_scalar(
                    b_xq.ap[:, ft * SEC + hq * QC:ft * SEC + (hq + 1) * QC],
                    qb2.ap[:, 0:QC], -128.0, None, op0=OP.max),
                    reads=(qb2,), writes=(b_xq,))
                if ft % 2 == 1:
                    yield
        HC = SEC
        for hc in range(1):
            for wave in range(8):
                wbs = {}
                for k in range(FT):
                    kg = k // 4
                    if kg not in wbs:
                        wb = r_wsl.get()
                        dof = wave * (FT * 512) + kg * 2048
                        dma_sp(wb.ap[:, 0:2048], w1q_bf[0:128, dof:dof + 2048],
                               writes=(wb,), reads=(trk["w1q"],))
                        wbs[kg] = wb
                    wb = wbs[kg]
                    ko = (k - kg * 4) * 512
                    for mi in range(4):
                        bank = PS_W[mi]
                        tok = pe(lambda e, k=k, ko=ko, mi=mi, bank=bank, wb=wb,
                                 hc=hc: e.matmul(
                            bank.ap[:, 0:HC],
                            wb.ap[:, ko + mi * 128:ko + (mi + 1) * 128],
                            b_xq.ap[:, k * SEC + hc * HC:k * SEC + (hc + 1) * HC],
                            start=(k == 0), stop=(k == FT - 1)),
                           reads=(wb, b_xq),
                           extra=tuple(bank.ready) + tuple(bank.readers) if k == 0 else (),
                           inc=(mi == 3))
                    if k % 4 == 3:
                        yield
                for mi in range(4):
                    PS_W[mi].ready = [tok]
                    PS_W[mi].readers = []
                for mi in range(4):
                    m = wave * 4 + mi
                    bank = PS_W[mi]
                    tb = r_f32.get()
                    act(lambda e, m=m, hc=hc, bank=bank, tb=tb: e.activation(
                        tb.ap[:, 0:HC], bank.ap[:, 0:HC], AF.Identity,
                        scale=1.0 / 64.0, bias=c_b1m.ap[:, m:m + 1]),
                        reads=(bank, c_b1m), writes=(tb,))
                    tb2 = r_f32.get()
                    dve(lambda e, tb=tb, tb2=tb2: e.tensor_scalar(
                        tb2.ap[:, 0:HC], tb.ap[:, 0:HC], MAGIC, 127.0,
                        op0=OP.subtract, op1=OP.min), reads=(tb,), writes=(tb2,))
                    act(lambda e, m=m, hc=hc, tb2=tb2: e.activation(
                        b_h2q.ap[:, m * HC:(m + 1) * HC], tb2.ap[:, 0:HC],
                        AF.Relu), reads=(tb2,), writes=(b_h2q,))
                yield
            for wave in range(2):
                for k in range(HT):
                    kg = k // 4
                    if k % 4 == 0:
                        w2s = r_wsl.get()
                        dof = wave * (HT * 512) + kg * 2048
                        dma_sp(w2s.ap[:, 0:2048], w2q_bf[0:128, dof:dof + 2048],
                               writes=(w2s,), reads=(trk["w2q"],))
                        w2cur = w2s
                    ko = (k - kg * 4) * 512
                    for mi in range(4):
                        bank = PS_W[mi]
                        tok = pe(lambda e, k=k, ko=ko, mi=mi, bank=bank,
                                 w2s=w2cur: e.matmul(
                            bank.ap[:, 0:HC],
                            w2s.ap[:, ko + mi * 128:ko + (mi + 1) * 128],
                            b_h2q.ap[:, k * HC:(k + 1) * HC],
                            start=(k == 0), stop=(k == HT - 1)),
                           reads=(w2cur, b_h2q),
                           extra=tuple(bank.ready) + tuple(bank.readers) if k == 0 else (),
                           inc=(mi == 3))
                    if k % 4 == 3:
                        yield
                for mi in range(4):
                    PS_W[mi].ready = [tok]
                    PS_W[mi].readers = []
                for mi in range(4):
                    m = wave * 4 + mi
                    bank = PS_W[mi]
                    o2 = r_f32.get()
                    act(lambda e, m=m, bank=bank, o2=o2: e.activation(
                        o2.ap[:, 0:HC], bank.ap[:, 0:HC], AF.Identity,
                        scale=1.0 / 1024.0, bias=c_b2q.ap[:, m:m + 1]),
                        reads=(bank, c_b2q), writes=(o2,))
                    ot = r_f32.get()
                    dve(lambda e, m=m, o2=o2, ot=ot, hc=hc: e.scalar_tensor_tensor(
                        ot.ap[:, 0:HC], o2.ap[:, 0:HC], c_gb.ap[:, m:m + 1],
                        b_x1.ap[:, m * SEC + hc * HC:m * SEC + (hc + 1) * HC],
                        op0=OP.mult, op1=OP.add),
                        reads=(o2, c_gb, b_x1), writes=(ot,))
                    dma_gp(outT[m * 128:(m + 1) * 128, t0 + hc * HC:t0 + (hc + 1) * HC],
                           ot.ap[:, 0:HC], reads=(ot,))
                    yield

    def drain():
        while pend or pend2:
            pump(1)

    pend2.append(prep_weight(W_out, wout_bf, DI, DM, False, trk["wout"], True))
    pend2.append(prep_weight(w1T, w1q_bf, DM, HID, True, trk["w1q"], True))
    pend2.append(prep_weight(w2T, w2q_bf, HID, DM, True, trk["w2q"], True))
    for _ in phase_a1(0):
        pass
    for _ in phase_a1(1):
        pass
    pend.append(phase_a1(2))
    pend.append(phase_a1(3))
    phase_a2(0)
    for ssec in range(1, NSS):
        drain()
        if 2 * ssec + 2 < NSEC:
            pend.append(phase_a1(2 * ssec + 2))
        if 2 * ssec + 3 < NSEC:
            pend.append(phase_a1(2 * ssec + 3))
        pend.append(phase_a3_ffn(2 * ssec - 2))
        pend.append(phase_a3_ffn(2 * ssec - 1))
        phase_a2(ssec)
    drain()
    pend.append(phase_a3_ffn(NSEC - 2))
    pend.append(phase_a3_ffn(NSEC - 1))
    drain()

    finals = [Tok(s, ctx.cnt[s]) for s in
              Ctx.SP_LANES + Ctx.ACT_LANES + Ctx.GP_LANES if ctx.cnt[s] > 0]
    ctx._emit("sp", lambda e: e.nop(), finals, None, 0)
    ctx._emit("act", lambda e: e.nop(),
              [Tok(s, ctx.cnt[s]) for s in Ctx.ACT_LANES if ctx.cnt[s] > 0],
              None, 0)

    semmap = {k: sems[k] for k in sems}

    with st, nc.Block() as block:

        @block.sync
        def _(sync):
            for thunk in ctx.q["sp"]:
                thunk(sync, semmap)

        @block.vector
        def _(vector):
            for thunk in ctx.q["dve"]:
                thunk(vector, semmap)

        @block.scalar
        def _(scalar):
            for thunk in ctx.q["act"]:
                thunk(scalar, semmap)

        @block.tensor
        def _(tensor):
            for thunk in ctx.q["pe"]:
                thunk(tensor, semmap)

        @block.gpsimd
        def _(gpsimd):
            for thunk in ctx.q["pool"]:
                thunk(gpsimd, semmap)

    return nc, sbufs_bytes[0]


def _vec_tiles(v, ntiles):
    return np.ascontiguousarray(v.reshape(ntiles, 128).T.astype(np.float32))


_prog_cache = {}


def _get_prog(taps=False):
    key = taps
    if key not in _prog_cache:
        _prog_cache[key] = build_program(taps=taps)
    return _prog_cache[key]


def make_inputs(args, taps=False):
    (x, ln_g, ln_b, W_in, b_in, conv_w, conv_b, W_x, W_dt, b_dt, A_log, Dv,
     W_out, b_out, w1, b1, w2, b2, c, wmod_w) = args
    n = 8
    f32 = np.float32
    W_in_h = np.ascontiguousarray(W_in.astype(f32))
    convw_h = np.ascontiguousarray(conv_w.T.reshape(G, 128, 4).transpose(1, 0, 2)
                                   .reshape(128, G * 4).astype(f32))
    Wx_h = np.zeros((DI, 128), f32)
    Wx_h[:, 0:64] = W_x[:, 0:64]
    Wx_h[:, 64:80] = W_x[:, 64:80]
    Wx_h[:, 96:112] = W_x[:, 80:96]
    Wdt_h = np.ascontiguousarray(W_dt.astype(f32))
    A_h = np.ascontiguousarray(
        A_log.reshape(G, 128, NS).transpose(1, 0, 2).reshape(128, G * NS).astype(f32))
    Dd_h = np.zeros((DI, 128), f32)
    idx = np.arange(DI)
    Dd_h[idx, idx % 128] = Dv.astype(f32)
    Wout_h = np.ascontiguousarray(W_out.astype(f32))
    w1T_h = np.ascontiguousarray(w1.T.astype(f32))
    w2T_h = np.ascontiguousarray(w2.T.astype(f32))
    wmod_h = np.ascontiguousarray(wmod_w.astype(f32))
    eye_h = np.eye(128, dtype=f32)
    shared = {
        "W_in": W_in_h, "convw": convw_h, "Wx": Wx_h, "Wdt": Wdt_h,
        "A_log_t": A_h, "D_diag": Dd_h, "W_out": Wout_h, "w1T": w1T_h,
        "w2T": w2T_h, "wmod": wmod_h, "eye": eye_h,
        "v_gamma": _vec_tiles(ln_g, FT), "v_beta": _vec_tiles(ln_b, FT),
        "v_b_in_u": _vec_tiles(b_in[0:DI], G),
        "v_b_in_r": _vec_tiles(b_in[DI:2 * DI], G),
        "v_conv_b": _vec_tiles(conv_b, G), "v_b_dt": _vec_tiles(b_dt, G),
        "v_b_out": _vec_tiles(b_out, FT), "v_b1": _vec_tiles(b1, HT),
        "v_b2": _vec_tiles(b2, FT),
    }
    in_maps = []
    for b in range(n):
        m = dict(shared)
        xt = np.ascontiguousarray(x[b].T.astype(f32))
        m["xT"] = xt
        m["xT4"] = np.ascontiguousarray(
            xt.reshape(FT, 128, L // 256, 256).transpose(1, 2, 0, 3)
            .reshape(128, L * FT))
        m["v_c"] = _vec_tiles(c[b], FT)
        in_maps.append(m)
    return in_maps


def kernel(L0_attn_x, L0_attn_ln_gamma, L0_attn_ln_beta, L0_attn_W_in,
           L0_attn_b_in, L0_attn_conv_w, L0_attn_conv_b, L0_attn_W_x,
           L0_attn_W_dt, L0_attn_b_dt, L0_attn_A_log, L0_attn_D,
           L0_attn_W_out, L0_attn_b_out, L0_ffn_w1, L0_ffn_b1, L0_ffn_w2,
           L0_ffn_b2, L0_c, L0_wmod):
    args = (np.asarray(L0_attn_x), np.asarray(L0_attn_ln_gamma),
            np.asarray(L0_attn_ln_beta), np.asarray(L0_attn_W_in),
            np.asarray(L0_attn_b_in), np.asarray(L0_attn_conv_w),
            np.asarray(L0_attn_conv_b), np.asarray(L0_attn_W_x),
            np.asarray(L0_attn_W_dt), np.asarray(L0_attn_b_dt),
            np.asarray(L0_attn_A_log), np.asarray(L0_attn_D),
            np.asarray(L0_attn_W_out), np.asarray(L0_attn_b_out),
            np.asarray(L0_ffn_w1), np.asarray(L0_ffn_b1),
            np.asarray(L0_ffn_w2), np.asarray(L0_ffn_b2),
            np.asarray(L0_c), np.asarray(L0_wmod))
    in_maps = make_inputs(args)
    nc, _ = _get_prog(taps=False)
    res = run_bass_kernel_spmd(nc, in_maps, core_ids=list(range(8)))
    out = np.stack([np.ascontiguousarray(r["outT"].T) for r in res.results])
    return out.astype(np.float32)


if __name__ == "__main__":
    nc, nbytes = build_program()
    print("built ok, sbuf bytes/partition:", nbytes)


# version 23
# speedup vs baseline: 1.0445x; 1.0445x over previous
import contextlib
import numpy as np
import concourse.bass as bass
import concourse.mybir as mybir
from concourse.bass_utils import run_bass_kernel_spmd
from concourse.alu_op_type import AluOpType as OP

F32 = mybir.dt.float32
BF16 = mybir.dt.bfloat16
AF = mybir.ActivationFunctionType

MAGIC = float(3 * 2 ** 22)

L = 4096
DM = 1024
DI = 2048
NS = 16
DTR = 64
HID = 4096
SEC = 512
NSEC = L // SEC
SS = 1024
NSS = L // SS
G = DI // 128
FT = DM // 128
HT = HID // 128
EPS = 1e-5


class Tok:
    __slots__ = ("sem", "val")

    def __init__(self, sem, val):
        self.sem = sem
        self.val = val


class Buf:
    __slots__ = ("ap", "ready", "readers", "name")

    def __init__(self, ap, name=""):
        self.ap = ap
        self.ready = []
        self.readers = []
        self.name = name

    def __getitem__(self, idx):
        return self.ap[idx]


class Ctx:
    STREAMS = ("pe", "act", "dve", "pool", "sp", "gp")
    SP_LANES = tuple(f"spd{i}" for i in range(16))
    ACT_LANES = tuple(f"actd{i}" for i in range(8))
    GP_LANES = tuple(f"gpd{i}" for i in range(8))
    SEMS = STREAMS + SP_LANES + ACT_LANES + GP_LANES

    def __init__(self):
        self.q = {s: [] for s in self.STREAMS}
        self.q["gp"] = self.q["pool"]
        self.cnt = {s: 0 for s in self.SEMS}
        self.waited = {s: {} for s in self.STREAMS}
        self.waited["gp"] = self.waited["pool"]
        self.lane_i = {"sp": 0, "act": 0, "gp": 0}

    def _emit(self, stream, builder, waits, inc_sem, inc_n, prewait=None):
        ws = []
        for t in waits:
            if t is None:
                continue
            prev = self.waited[stream].get(t.sem, -1)
            if t.val <= prev:
                continue
            self.waited[stream][t.sem] = t.val
            ws.append((t.sem, t.val))
        if prewait is not None:
            s, v = prewait
            if self.waited[stream].get(s, -1) < v:
                self.waited[stream][s] = v
                ws.append((s, v))
        tok = None
        if inc_sem is not None:
            self.cnt[inc_sem] += inc_n
            tok = Tok(inc_sem, self.cnt[inc_sem])

        def thunk(eng, sems, ws=ws, builder=builder, inc_sem=inc_sem, inc_n=inc_n):
            for s, v in ws:
                eng.wait_ge(sems[s], v)
            inst = builder(eng)
            if inc_sem is not None:
                inst.then_inc(sems[inc_sem], inc_n)

        self.q[stream].append(thunk)
        return tok

    def op(self, stream, builder, reads=(), writes=(), extra=(), inc=True):
        waits = []
        for b in reads:
            if isinstance(b, Buf):
                waits.extend(b.ready)
        for b in writes:
            waits.extend(b.ready)
            waits.extend(b.readers)
        waits.extend(extra)
        sem = stream if inc else None
        tok = self._emit(stream, builder, waits, sem, 1 if inc else 0)
        if tok is not None:
            for b in writes:
                b.ready = [tok]
                b.readers = []
            for b in reads:
                if isinstance(b, Buf):
                    b.readers.append(tok)
        return tok

    def dma(self, stream, out, in_, reads=(), writes=(), extra=()):
        lanes = {"sp": self.SP_LANES, "act": self.ACT_LANES,
                 "gp": self.GP_LANES}[stream]
        lane = lanes[self.lane_i[stream] % len(lanes)]
        self.lane_i[stream] += 1
        if isinstance(out, Buf) and out not in writes:
            writes = tuple(writes) + (out,)
        if isinstance(in_, Buf) and in_ not in reads:
            reads = tuple(reads) + (in_,)
        waits = []
        for b in reads:
            if isinstance(b, Buf):
                waits.extend(b.ready)
        for b in writes:
            waits.extend(b.ready)
            waits.extend(b.readers)
        waits.extend(extra)
        out_ap = out.ap[:] if isinstance(out, Buf) else out
        in_ap = in_.ap[:] if isinstance(in_, Buf) else in_

        def builder(eng, o=out_ap, i=in_ap):
            return eng.dma_start(out=o, in_=i)

        tok = self._emit(stream, builder, waits, lane, 16,
                         prewait=(lane, self.cnt[lane]))
        for b in writes:
            b.ready = [tok]
            b.readers = []
        for b in reads:
            if isinstance(b, Buf):
                b.readers.append(tok)
        return tok

    def dma_multi(self, stream, pairs, wbuf, reads=(), extra=()):
        lanes = {"sp": self.SP_LANES, "act": self.ACT_LANES,
                 "gp": self.GP_LANES}[stream]
        old_waits = list(wbuf.ready) + list(wbuf.readers) + list(extra)
        for b in reads:
            if isinstance(b, Buf):
                old_waits.extend(b.ready)
        toks = []
        for out_ap, in_ap in pairs:
            lane = lanes[self.lane_i[stream] % len(lanes)]
            self.lane_i[stream] += 1

            def builder(eng, o=out_ap, i=in_ap):
                return eng.dma_start(out=o, in_=i)

            toks.append(self._emit(stream, builder, old_waits, lane, 16,
                                   prewait=(lane, self.cnt[lane])))
        wbuf.ready = list(toks)
        wbuf.readers = []
        for b in reads:
            if isinstance(b, Buf):
                b.readers.extend(toks)
        return toks


class Rot:

    def __init__(self, bufs):
        self.bufs = bufs
        self.i = 0

    def get(self):
        b = self.bufs[self.i % len(self.bufs)]
        self.i += 1
        return b


def build_program(taps=False):
    nc = bass.Bass("TRN2", target_bir_lowering=False)

    dram = {}

    def din(name, shape, dtype=F32):
        dram[name] = nc.dram_tensor(name, shape, dtype, kind="ExternalInput").ap()
        return dram[name]

    def dscratch(name, shape, dtype):
        dram[name] = nc.dram_tensor(name, shape, dtype, kind="Internal").ap()
        return dram[name]

    xT = din("xT", [DM, L])
    xT4 = din("xT4", [128, L * FT])
    W_in = din("W_in", [DM, 2 * DI])
    convw = din("convw", [128, G * 4])
    Wx = din("Wx", [DI, 128])
    Wdt = din("Wdt", [DTR, DI])
    A_log_t = din("A_log_t", [128, G * NS])
    D_diag = din("D_diag", [DI, 128])
    W_out = din("W_out", [DI, DM])
    w1T = din("w1T", [DM, HID])
    w2T = din("w2T", [HID, DM])
    wmod = din("wmod", [DM, 6 * DM])
    eye = din("eye", [128, 128])
    v_gamma = din("v_gamma", [128, FT])
    v_beta = din("v_beta", [128, FT])
    v_b_in_u = din("v_b_in_u", [128, G])
    v_b_in_r = din("v_b_in_r", [128, G])
    v_conv_b = din("v_conv_b", [128, G])
    v_b_dt = din("v_b_dt", [128, G])
    v_b_out = din("v_b_out", [128, FT])
    v_b1 = din("v_b1", [128, HT])
    v_b2 = din("v_b2", [128, FT])
    v_c = din("v_c", [128, FT])

    outT = nc.dram_tensor("outT", [DM, L], F32, kind="ExternalOutput").ap()

    win_bf = dscratch("win_bf", [128, FT * 2 * DI], BF16)
    wout_bf = dscratch("wout_bf", [128, G * DM], BF16)
    w1q_bf = dscratch("w1q_bf", [128, FT * HID], BF16)
    w2q_bf = dscratch("w2q_bf", [128, HT * DM], BF16)
    t2_hbm = dscratch("t2_hbm", [DM, L], BF16)
    yg3_hbm = dscratch("yg3_hbm", [128, NSEC * G * SEC], BF16)
    bc_hbm = dscratch("bc_hbm", [4, NS * SS], BF16)
    su_hbm = dscratch("su_hbm", [DI, L], BF16)
    sres_hbm = dscratch("sres_hbm", [DI, L], BF16)
    trk = {nm: Buf(None, nm) for nm in ("win", "wout", "w1q", "w2q")}
    for nm in ("t2", "su", "sres", "yg"):
        for s_ in range(NSEC):
            trk[(nm, s_)] = Buf(None, f"{nm}{s_}")
    for nm in ("bcB", "bcC"):
        for s_ in range(2):
            trk[(nm, s_)] = Buf(None, f"{nm}{s_}")

    taps_out = {}
    if taps:
        for nm, shape, tdt in [("tap_xn", [DM, L], BF16), ("tap_su", [DI, L], BF16),
                               ("tap_delta", [DI, L], BF16), ("tap_B", [NS, L], BF16),
                               ("tap_C", [NS, L], BF16), ("tap_yg", [DI, L], BF16),
                               ("tap_x1", [DM, SEC], BF16), ("tap_t2", [DM, L], BF16),
                               ("tap_x", [DM, L], F32), ("tap_modv", [128, 48], F32),
                               ("tap_sc", [128, FT], F32), ("tap_wq", [128, 128], BF16),
                               ("tap_scbf", [128, FT], BF16), ("tap_bank0", [128, 1], F32),
                               ("tap_snap", [128, FT], BF16)]:
            taps_out[nm] = nc.dram_tensor(nm, shape, tdt, kind="ExternalOutput").ap()

    st = contextlib.ExitStack()
    sbufs_bytes = [0]

    def sb(name, shape, dtype):
        t = st.enter_context(nc.sbuf_tensor(name, shape, dtype))
        esz = 4 if dtype == F32 else 2
        sbufs_bytes[0] += shape[1] * esz
        return Buf(t, name)

    def sbrot(name, n, shape, dtype):
        return Rot([sb(f"{name}{i}", shape, dtype) for i in range(n)])

    c_ones = sb("c_ones", [128, 1], BF16)
    c_eye = sb("c_eye", [128, 128], BF16)
    c_A = sb("c_A", [128, G * NS], F32)
    c_convw = sb("c_convw", [128, G * 4], F32)
    c_Dd = sb("c_Dd", [128, G * 128], BF16)
    c_Wx = sb("c_Wx", [128, G * 128], BF16)
    c_Wdt = sb("c_Wdt", [64, DI], BF16)
    c_gam = sb("c_gam", [128, FT], F32)
    c_bet = sb("c_bet", [128, FT], F32)
    c_biu = sb("c_biu", [128, G], F32)
    c_bir = sb("c_bir", [128, G], F32)
    c_bir_h = sb("c_bir_h", [128, G], F32)
    c_bconv = sb("c_bconv", [128, G], F32)
    c_bdt = sb("c_bdt", [128, G], F32)
    c_bout = sb("c_bout", [128, FT], F32)
    c_b1m = sb("c_b1m", [128, HT], F32)
    c_b2q = sb("c_b2q", [128, FT], F32)
    c_sa = sb("c_sa", [128, FT], F32)
    c_ca1 = sb("c_ca1", [128, FT], F32)
    c_ga = sb("c_ga", [128, FT], F32)
    c_gb = sb("c_gb", [128, FT], F32)
    c_s16cb = sb("c_s16cb", [128, FT], F32)
    c_b16sb = sb("c_b16sb", [128, FT], F32)
    st_h = sb("st_h", [128, G * NS], F32)
    st_halo = sb("st_halo", [128, G * 3], BF16)
    c_cst = sb("c_cst", [128, 7], F32)

    QC = SEC // 2
    r_x = sbrot("r_x", 1, [128, FT * QC], F32)
    r_xbf = sbrot("r_xbf", 2, [128, FT * QC], BF16)
    b_t2 = sb("b_t2", [128, FT * SEC], BF16)
    b_xn = sb("b_xn", [128, FT * SEC], BF16)
    r_uraw = sbrot("r_uraw", 2, [128, SEC + 3], BF16)
    b_su = sb("b_su", [128, G * SEC], BF16)
    r_wsl = sbrot("r_wsl", 2, [128, 2048], BF16)
    r_small = sbrot("r_small", 3, [128, SEC], BF16)
    r_f32 = sbrot("r_f32", 3, [128, SEC], F32)
    r_rep = sbrot("r_rep", 3, [128, QC], BF16)
    r_dt = sbrot("r_dt", 2, [64, SS], BF16)
    r_Bsb = sbrot("r_Bsb", 2, [16, SS], BF16)
    r_Csb = sbrot("r_Csb", 2, [16, SS], BF16)
    b_dl = sb("b_dl", [128, 2 * SS], BF16)
    b_du = sb("b_du", [128, 2 * SS], BF16)
    r_sug = sbrot("r_sug", 2, [128, SS], BF16)
    r_Brep = sbrot("r_Brep", 2, [128, 2 * SS], BF16)
    r_Crep = sbrot("r_Crep", 2, [128, 2 * SS], BF16)
    r_dA = sbrot("r_dA", 2, [128, SS], BF16)
    r_bt = sbrot("r_bt", 2, [128, SS], BF16)
    r_h = sbrot("r_h", 2, [128, SS], F32)
    r_tmp = sbrot("r_tmp", 3, [128, SS], BF16)
    r_yv = sbrot("r_yv", 1, [128, SS], BF16)
    r_sres = sbrot("r_sres", 1, [128, SS], BF16)
    b_x1 = sb("b_x1", [128, FT * SEC], BF16)
    b_xq = sb("b_xq", [128, FT * SEC], BF16)
    b_h2q = sb("b_h2q", [128, HT * SEC], BF16)
    r_wst = sbrot("r_wst", 4, [128, 256], F32)
    r_wbf = sbrot("r_wbf", 3, [128, 256], BF16)
    b_modv = sb("b_modv", [128, 48], F32)
    b_vc = sb("b_vc", [128, FT], F32)
    r_tiny = sbrot("r_tiny", 3, [128, FT], F32)
    r_stat = sbrot("r_stat", 3, [1, QC], F32)
    r_statb = sbrot("r_statb", 2, [1, QC], BF16)

    psum = [Buf(st.enter_context(nc.psum_tensor(f"ps{i}", [128, 512], F32)), f"ps{i}")
            for i in range(8)]
    PS_Y = psum[0:4]
    PS_W = psum[4:8]

    sems = {}
    for s in Ctx.SEMS:
        sems[s] = st.enter_context(nc.semaphore(f"sem_{s}"))

    ctx = Ctx()

    def act(builder, reads=(), writes=(), extra=()):
        return ctx.op("act", builder, reads, writes, extra)

    def dve(builder, reads=(), writes=(), extra=()):
        return ctx.op("dve", builder, reads, writes, extra)

    def pool(builder, reads=(), writes=(), extra=()):
        return ctx.op("pool", builder, reads, writes, extra)

    def pe(builder, reads=(), writes=(), extra=(), inc=True):
        return ctx.op("pe", builder, reads, writes, extra, inc=inc)

    def dma_sp(out, in_, reads=(), writes=(), extra=()):
        return ctx.dma("sp", out, in_, reads, writes, extra)

    def dma_act(out, in_, reads=(), writes=(), extra=()):
        return ctx.dma("act", out, in_, reads, writes, extra)

    def dma_gp(out, in_, reads=(), writes=(), extra=()):
        return ctx.dma("gp", out, in_, reads, writes, extra)

    def bcast(dst_buf, src_ap, cols, reads=()):
        in_b = src_ap.unsqueeze(1).broadcast_to((1, 128, cols))
        return ctx.dma("act", dst_buf.ap[:, 0:cols], in_b, reads=reads,
                       writes=(dst_buf,))

    cst_vals = [EPS, 1.0 / 64.0, 1.0 / 1024.0, -0.5, 0.5, 1.0, MAGIC]
    cst_tok = None
    for i, v in enumerate(cst_vals):
        cst_tok = dve(lambda e, i=i, v=v: e.memset(c_cst.ap[:, i:i + 1], v),
                      writes=(c_cst,))
        try:
            nc.const_aps.register(F32, float(v), c_cst.ap[:, i:i + 1])
        except Exception:
            pass
    ctx._emit("act", lambda e: e.nop(), [cst_tok], None, 0)
    ctx._emit("pool", lambda e: e.nop(), [cst_tok], None, 0)

    dma_sp(c_A, A_log_t)
    dma_sp(c_convw, convw)
    dma_sp(c_gam, v_gamma)
    dma_sp(c_bet, v_beta)
    dma_sp(c_biu, v_b_in_u)
    dma_sp(c_bir, v_b_in_r)
    dma_sp(c_bconv, v_conv_b)
    dma_sp(c_bdt, v_b_dt)
    dma_sp(c_bout, v_b_out)
    dma_sp(b_vc, v_c)

    eye_f = r_wst.get()
    dma_sp(eye_f.ap[:, 0:128], eye, writes=(eye_f,))
    dve(lambda e: e.tensor_copy(c_eye.ap[:], eye_f.ap[:, 0:128]),
        reads=(eye_f,), writes=(c_eye,))
    dve(lambda e: e.memset(c_ones.ap[:], 1.0), writes=(c_ones,))
    dve(lambda e: e.memset(st_h.ap[:], 0.0), writes=(st_h,))
    dve(lambda e: e.memset(st_halo.ap[:], 0.0), writes=(st_halo,))

    tA = r_wst.get()
    act(lambda e: e.activation(tA.ap[:, 0:G * NS], c_A.ap[:], AF.Exp),
        reads=(c_A,), writes=(tA,))
    dve(lambda e: e.tensor_scalar(c_A.ap[:], tA.ap[:, 0:G * NS], -1.0, None, op0=OP.mult),
        reads=(tA,), writes=(c_A,))

    dve(lambda e: e.tensor_scalar(c_bir_h.ap[:], c_bir.ap[:], 0.5, None, op0=OP.mult),
        reads=(c_bir,), writes=(c_bir_h,))

    for g in range(G):
        w = r_wst.get()
        dma_sp(w.ap[:, 0:128], D_diag[g * 128:(g + 1) * 128, :], writes=(w,))
        dve(lambda e, g=g, w=w: e.tensor_scalar(
            c_Dd.ap[:, g * 128:(g + 1) * 128], w.ap[:, 0:128], 1.0, None, op0=OP.mult),
            reads=(w,), writes=(c_Dd,))

    for g in range(G):
        w = r_wst.get()
        dma_sp(w.ap[:, 0:128], Wx[g * 128:(g + 1) * 128, :], writes=(w,))
        dve(lambda e, g=g, w=w: e.tensor_copy(
            c_Wx.ap[:, g * 128:(g + 1) * 128], w.ap[:, 0:128]),
            reads=(w,), writes=(c_Wx,))
    for hw in range(8):
        wdt_f = r_wst.get()
        dma_sp(wdt_f.ap[0:64, 0:256], Wdt[0:64, hw * 256:(hw + 1) * 256],
               writes=(wdt_f,))
        dve(lambda e, hw=hw, wdt_f=wdt_f: e.tensor_copy(
            c_Wdt.ap[0:64, hw * 256:(hw + 1) * 256], wdt_f.ap[0:64, 0:256]),
            reads=(wdt_f,), writes=(c_Wdt,))

    wb = r_wst.get()
    dma_sp(wb.ap[:, 0:HT], v_b1, writes=(wb,))
    q1 = r_f32.get()
    dve(lambda e, q1=q1, wb=wb: e.tensor_scalar(
        q1.ap[:, 0:HT], wb.ap[:, 0:HT], 64.0, MAGIC,
        op0=OP.mult, op1=OP.add), reads=(wb,), writes=(q1,))
    q2 = r_f32.get()
    dve(lambda e, q1=q1, q2=q2: e.tensor_scalar(
        q2.ap[:, 0:HT], q1.ap[:, 0:HT], MAGIC, 127.0,
        op0=OP.subtract, op1=OP.min), reads=(q1,), writes=(q2,))
    q3 = r_f32.get()
    dve(lambda e, q2=q2, q3=q3: e.tensor_scalar(
        q3.ap[:, 0:HT], q2.ap[:, 0:HT], -128.0, None,
        op0=OP.max), reads=(q2,), writes=(q3,))
    dve(lambda e, q3=q3: e.tensor_scalar(
        c_b1m.ap[:], q3.ap[:, 0:HT], 0.25, MAGIC,
        op0=OP.mult, op1=OP.add), reads=(q3,), writes=(c_b1m,))
    wb2 = r_wst.get()
    dma_sp(wb2.ap[:, 0:FT], v_b2, writes=(wb2,))
    q1 = r_tiny.get()
    dve(lambda e, q1=q1, wb2=wb2: e.tensor_scalar(
        q1.ap[:, 0:FT], wb2.ap[:, 0:FT], 64.0, MAGIC,
        op0=OP.mult, op1=OP.add), reads=(wb2,), writes=(q1,))
    q2 = r_tiny.get()
    dve(lambda e, q1=q1, q2=q2: e.tensor_scalar(
        q2.ap[:, 0:FT], q1.ap[:, 0:FT], MAGIC, 127.0,
        op0=OP.subtract, op1=OP.min), reads=(q1,), writes=(q2,))
    q3 = r_tiny.get()
    dve(lambda e, q2=q2, q3=q3: e.tensor_scalar(
        q3.ap[:, 0:FT], q2.ap[:, 0:FT], -128.0, None,
        op0=OP.max), reads=(q2,), writes=(q3,))
    dve(lambda e, q3=q3: e.tensor_scalar(
        c_b2q.ap[:], q3.ap[:, 0:FT], 1.0 / 64.0, None,
        op0=OP.mult), reads=(q3,), writes=(c_b2q,))

    tc = r_tiny.get()
    act(lambda e: e.activation(tc.ap[:, 0:FT], b_vc.ap[:], AF.Tanh, scale=0.5),
        reads=(b_vc,), writes=(tc,))
    scv = r_tiny.get()
    dve(lambda e: e.scalar_tensor_tensor(scv.ap[:, 0:FT], tc.ap[:, 0:FT], 1.0,
                                         b_vc.ap[:], op0=OP.add, op1=OP.mult),
        reads=(tc, b_vc), writes=(scv,))
    sc_bf = r_small.get()
    dve(lambda e: e.tensor_copy(sc_bf.ap[:, 0:FT], scv.ap[:, 0:FT]),
        reads=(scv,), writes=(sc_bf,))
    if taps:
        dma_sp(taps_out["tap_sc"], scv.ap[:, 0:FT], reads=(scv,))
        dma_sp(taps_out["tap_scbf"], sc_bf.ap[:, 0:FT], reads=(sc_bf,))
        snap = sb("dbg_snap", [128, FT], BF16)
        dve(lambda e, snap=snap: e.tensor_copy(snap.ap[:], sc_bf.ap[:, 0:FT]),
            reads=(sc_bf,), writes=(snap,))
        dma_sp(taps_out["tap_snap"], snap.ap[:], reads=(snap,))
    for m in range(48):
        bank = PS_W[m % 2]
        group_bufs = [sc_bf]
        for k in range(FT):
            w = r_wst.get()
            dma_sp(w.ap[:, 0:128], wmod[k * 128:(k + 1) * 128,
                                        m * 128:m * 128 + 128], writes=(w,))
            wq = r_wsl.get()
            dve(lambda e, w=w, wq=wq: e.tensor_copy(wq.ap[:, 0:128], w.ap[:, 0:128]),
                reads=(w,), writes=(wq,))
            group_bufs.append(wq)
            if taps and m == 0 and k == 0:
                dma_sp(taps_out["tap_wq"], wq.ap[:, 0:128], reads=(wq,))
            tok = pe(lambda e, wq=wq, k=k, bank=bank: e.matmul(
                bank.ap[:, 0:1], wq.ap[:, 0:128], sc_bf.ap[:, k:k + 1],
                start=(k == 0), stop=(k == FT - 1)),
                reads=(wq, sc_bf), writes=(bank,) if k == FT - 1 else (),
                extra=tuple(bank.ready) + tuple(bank.readers) if k == 0 else (),
                inc=True)
        if taps and m == 0:
            dbg = r_f32.get()
            act(lambda e, bank=bank, dbg=dbg: e.activation(
                dbg.ap[:, 0:1], bank.ap[:, 0:1], AF.Identity),
                reads=(bank,), writes=(dbg,))
            dma_sp(taps_out["tap_bank0"], dbg.ap[:, 0:1], reads=(dbg,))
        act(lambda e, m=m, bank=bank: e.activation(
            b_modv.ap[:, m:m + 1], bank.ap[:, 0:1], AF.Identity, scale=0.5),
            reads=(bank,), writes=(b_modv,))

    dve(lambda e: e.tensor_copy(c_sa.ap[:], b_modv.ap[:, 0:FT]),
        reads=(b_modv,), writes=(c_sa,))
    dve(lambda e: e.tensor_scalar(c_ca1.ap[:], b_modv.ap[:, FT:2 * FT], 1.0, None,
                                  op0=OP.add), reads=(b_modv,), writes=(c_ca1,))

    def clamp_gate(dst, col0):
        gneg = r_tiny.get()
        dve(lambda e: e.tensor_scalar(gneg.ap[:, 0:FT],
                                      b_modv.ap[:, col0:col0 + FT], -1.0, None,
                                      op0=OP.mult), reads=(b_modv,), writes=(gneg,))
        gabs = r_tiny.get()
        dve(lambda e: e.tensor_tensor(gabs.ap[:, 0:FT], gneg.ap[:, 0:FT],
                                      b_modv.ap[:, col0:col0 + FT], OP.max),
            reads=(gneg, b_modv), writes=(gabs,))
        gmag = r_tiny.get()
        dve(lambda e: e.tensor_scalar(gmag.ap[:, 0:FT], gabs.ap[:, 0:FT], 0.25, None,
                                      op0=OP.max), reads=(gabs,), writes=(gmag,))
        gge = r_tiny.get()
        dve(lambda e: e.tensor_scalar(gge.ap[:, 0:FT],
                                      b_modv.ap[:, col0:col0 + FT], 0.0, None,
                                      op0=OP.is_ge), reads=(b_modv,), writes=(gge,))
        gsgn = r_tiny.get()
        dve(lambda e: e.tensor_scalar(gsgn.ap[:, 0:FT], gge.ap[:, 0:FT], 2.0, -1.0,
                                      op0=OP.mult, op1=OP.add), reads=(gge,), writes=(gsgn,))
        dve(lambda e: e.tensor_tensor(dst.ap[:], gmag.ap[:, 0:FT], gsgn.ap[:, 0:FT],
                                      OP.mult), reads=(gmag, gsgn), writes=(dst,))

    clamp_gate(c_ga, 2 * FT)
    clamp_gate(c_gb, 5 * FT)
    cb1 = r_tiny.get()
    dve(lambda e: e.tensor_scalar(cb1.ap[:, 0:FT], b_modv.ap[:, 4 * FT:5 * FT],
                                  1.0, None, op0=OP.add), reads=(b_modv,), writes=(cb1,))
    dve(lambda e: e.tensor_scalar(c_s16cb.ap[:], cb1.ap[:, 0:FT], 16.0, None,
                                  op0=OP.mult), reads=(cb1,), writes=(c_s16cb,))
    dve(lambda e: e.tensor_scalar(c_b16sb.ap[:], b_modv.ap[:, 3 * FT:4 * FT],
                                  16.0, MAGIC, op0=OP.mult, op1=OP.add),
        reads=(b_modv,), writes=(c_b16sb,))
    if taps:
        dma_sp(taps_out["tap_modv"], b_modv.ap[:], reads=(b_modv,))

    def prep_weight(src, dst, rows, cols, quant, track, do_yield=False):
        nk = rows // 128
        piece = 0
        for r0 in range(0, rows, 128):
            for cc0 in range(0, cols, 256):
                cw = min(256, cols - cc0)
                w = r_wst.get()
                dma_sp(w.ap[:, 0:cw], src[r0:r0 + 128, cc0:cc0 + cw], writes=(w,))
                wbf = r_wbf.get()
                if quant:
                    t1 = r_wst.get()
                    act(lambda e, w=w, t1=t1, cw=cw: e.activation(
                        t1.ap[:, 0:cw], w.ap[:, 0:cw], AF.Identity,
                        scale=64.0, bias=c_cst.ap[:, 6:7]),
                        reads=(w, c_cst), writes=(t1,))
                    dve(lambda e, t1=t1, cw=cw: e.tensor_scalar(
                        t1.ap[:, 0:cw], t1.ap[:, 0:cw], MAGIC, 127.0,
                        op0=OP.subtract, op1=OP.min), reads=(t1,), writes=(t1,))
                    dve(lambda e, t1=t1, wbf=wbf, cw=cw: e.tensor_scalar(
                        wbf.ap[:, 0:cw], t1.ap[:, 0:cw], -128.0, None,
                        op0=OP.max), reads=(t1,), writes=(wbf,))
                else:
                    act(lambda e, w=w, wbf=wbf, cw=cw: e.activation(
                        wbf.ap[:, 0:cw], w.ap[:, 0:cw], AF.Identity),
                        reads=(w,), writes=(wbf,))
                kk = r0 // 128
                wv = cc0 // 512
                dof = wv * (nk * 512) + kk * 512 + (cc0 % 512)
                dma_sp(dst[0:128, dof:dof + cw], wbf.ap[:, 0:cw],
                       reads=(wbf,), writes=(track,))
                piece += 1
                if do_yield and piece % 2 == 0:
                    yield

    for _ in prep_weight(W_in, win_bf, DM, 2 * DI, False, trk["win"]):
        pass

    def layernorm_stats(src_bf, col0, n_feat_tiles, width, sq_buf, mu_rep, r_rep_):
        nfeat = float(n_feat_tiles * 128)
        cols = [col0(k) for k in range(n_feat_tiles)]
        col0 = lambda k, cols=cols: cols[k]
        for k in range(n_feat_tiles):
            dve(lambda e, k=k: e.tensor_tensor(
                sq_buf.ap[:, k * width:(k + 1) * width],
                src_bf.ap[:, col0(k):col0(k) + width],
                src_bf.ap[:, col0(k):col0(k) + width], OP.mult),
                reads=(src_bf,), writes=(sq_buf,) if k == n_feat_tiles - 1 else (),
                extra=tuple(sq_buf.ready) + tuple(sq_buf.readers) if k == 0 else ())
        bank_s = PS_W[0]
        bank_q = PS_W[1]
        for k in range(n_feat_tiles):
            t_s = pe(lambda e, k=k: e.matmul(
                bank_s.ap[0:1, 0:width], c_ones.ap[:],
                src_bf.ap[:, col0(k):col0(k) + width],
                start=(k == 0), stop=(k == n_feat_tiles - 1)),
                reads=(c_ones, src_bf),
                writes=(bank_s,) if k == n_feat_tiles - 1 else (),
                extra=tuple(bank_s.ready) + tuple(bank_s.readers) if k == 0 else (),
                inc=(k == n_feat_tiles - 1))
        src_bf.readers.append(t_s)
        for k in range(n_feat_tiles):
            t_q = pe(lambda e, k=k: e.matmul(
                bank_q.ap[0:1, 0:width], c_ones.ap[:],
                sq_buf.ap[:, k * width:(k + 1) * width],
                start=(k == 0), stop=(k == n_feat_tiles - 1)),
                reads=(c_ones, sq_buf),
                writes=(bank_q,) if k == n_feat_tiles - 1 else (),
                extra=tuple(bank_q.ready) + tuple(bank_q.readers) if k == 0 else (),
                inc=(k == n_feat_tiles - 1))
        sq_buf.readers.append(t_q)
        mu = r_stat.get()
        dve(lambda e: e.tensor_scalar(mu.ap[0:1, 0:width], bank_s.ap[0:1, 0:width],
                                      1.0 / nfeat, None, op0=OP.mult),
            reads=(bank_s,), writes=(mu,))
        mub = r_statb.get()
        dve(lambda e: e.tensor_copy(mub.ap[0:1, 0:width], mu.ap[0:1, 0:width]),
            reads=(mu,), writes=(mub,))
        m2 = r_stat.get()
        dve(lambda e: e.tensor_tensor(m2.ap[0:1, 0:width], mu.ap[0:1, 0:width],
                                      mu.ap[0:1, 0:width], OP.mult),
            reads=(mu,), writes=(m2,))
        var = r_stat.get()
        dve(lambda e: e.scalar_tensor_tensor(var.ap[0:1, 0:width],
                                             bank_q.ap[0:1, 0:width], 1.0 / nfeat,
                                             m2.ap[0:1, 0:width],
                                             op0=OP.mult, op1=OP.subtract),
            reads=(bank_q, m2), writes=(var,))
        lnv = r_stat.get()
        act(lambda e: e.activation(lnv.ap[0:1, 0:width], var.ap[0:1, 0:width],
                                   AF.Ln, bias=EPS), reads=(var,), writes=(lnv,))
        rin = r_stat.get()
        act(lambda e: e.activation(rin.ap[0:1, 0:width], lnv.ap[0:1, 0:width],
                                   AF.Exp, scale=-0.5), reads=(lnv,), writes=(rin,))
        rb = r_statb.get()
        dve(lambda e: e.tensor_copy(rb.ap[0:1, 0:width], rin.ap[0:1, 0:width]),
            reads=(rin,), writes=(rb,))
        bcast(mu_rep, mub.ap[0:1, 0:width], width, reads=(mub,))
        bcast(r_rep_, rb.ap[0:1, 0:width], width, reads=(rb,))

    def phase_a1(s):
        t0 = s * SEC
        for hq in range(2):
            tq = t0 + hq * QC
            xb = r_x.get()
            qi = tq // QC
            dma_sp(xb, xT4[0:128, qi * (FT * QC):(qi + 1) * (FT * QC)])
            if taps:
                dma_sp(taps_out["tap_x"].rearrange("(ft p) t -> p ft t", p=128)[:, :, tq:tq + QC],
                       xb.ap[:].rearrange("p (ft t) -> p ft t", t=QC), reads=(xb,))
            xbf = r_xbf.get()
            dve(lambda e, xb=xb, xbf=xbf: e.tensor_copy(xbf.ap[:], xb.ap[:]),
                reads=(xb,), writes=(xbf,))
            sq = r_xbf.get()
            mu1 = r_rep.get()
            rr1 = r_rep.get()
            layernorm_stats(xbf, lambda k: k * QC, FT, QC, sq, mu1, rr1)
            for ft in range(FT):
                d1 = r_small.get()
                dve(lambda e, ft=ft, d1=d1, xb=xb, mu1=mu1: e.tensor_tensor(
                    d1.ap[:, 0:QC], xb.ap[:, ft * QC:(ft + 1) * QC],
                    mu1.ap[:, 0:QC], OP.subtract), reads=(xb, mu1), writes=(d1,))
                d2 = r_small.get()
                dve(lambda e, d1=d1, d2=d2, rr1=rr1: e.tensor_tensor(
                    d2.ap[:, 0:QC], d1.ap[:, 0:QC], rr1.ap[:, 0:QC], OP.mult),
                    reads=(d1, rr1), writes=(d2,))
                act(lambda e, ft=ft, hq=hq, d2=d2: e.activation(
                    b_t2.ap[:, ft * SEC + hq * QC:ft * SEC + (hq + 1) * QC],
                    d2.ap[:, 0:QC], AF.Identity,
                    scale=c_ca1.ap[:, ft:ft + 1], bias=c_sa.ap[:, ft:ft + 1]),
                    reads=(d2, c_ca1, c_sa), writes=(b_t2,))
            sq2 = r_xbf.get()
            mu2 = r_rep.get()
            rr2 = r_rep.get()
            layernorm_stats(b_t2, lambda k: k * SEC + hq * QC, FT, QC, sq2, mu2, rr2)
            for ft in range(FT):
                e1 = r_small.get()
                dve(lambda e, ft=ft, hq=hq, e1=e1, mu2=mu2: e.tensor_tensor(
                    e1.ap[:, 0:QC],
                    b_t2.ap[:, ft * SEC + hq * QC:ft * SEC + (hq + 1) * QC],
                    mu2.ap[:, 0:QC], OP.subtract), reads=(b_t2, mu2), writes=(e1,))
                e2 = r_small.get()
                dve(lambda e, e1=e1, e2=e2, rr2=rr2: e.tensor_tensor(
                    e2.ap[:, 0:QC], e1.ap[:, 0:QC], rr2.ap[:, 0:QC], OP.mult),
                    reads=(e1, rr2), writes=(e2,))
                act(lambda e, ft=ft, hq=hq, e2=e2: e.activation(
                    b_xn.ap[:, ft * SEC + hq * QC:ft * SEC + (hq + 1) * QC],
                    e2.ap[:, 0:QC], AF.Identity,
                    scale=c_gam.ap[:, ft:ft + 1], bias=c_bet.ap[:, ft:ft + 1]),
                    reads=(e2, c_gam, c_bet), writes=(b_xn,))
            yield
        dma_gp(t2_hbm.rearrange("(ft p) t -> p ft t", p=128)[:, :, t0:t0 + SEC],
               b_t2.ap[:].rearrange("p (ft t) -> p ft t", t=SEC), reads=(b_t2,),
               writes=(trk[("t2", s)],))
        if taps:
            dma_sp(taps_out["tap_t2"].rearrange("(ft p) t -> p ft t", p=128)[:, :, t0:t0 + SEC],
                   b_t2.ap[:].rearrange("p (ft t) -> p ft t", t=SEC), reads=(b_t2,))
        if taps:
            dma_sp(taps_out["tap_xn"].rearrange("(ft p) t -> p ft t", p=128)[:, :, t0:t0 + SEC],
                   b_xn.ap[:].rearrange("p (ft t) -> p ft t", t=SEC), reads=(b_xn,))

        for wave in range(8):
            for kg in range(2):
                wb = r_wsl.get()
                dof = wave * (FT * 512) + kg * 2048
                dma_sp(wb.ap[:, 0:2048], win_bf[0:128, dof:dof + 2048],
                       writes=(wb,), reads=(trk["win"],))
                for k in range(kg * 4, kg * 4 + 4):
                    ko = (k - kg * 4) * 512
                    for mi in range(4):
                        bank = PS_W[mi]
                        tok = pe(lambda e, k=k, ko=ko, mi=mi, bank=bank, wb=wb: e.matmul(
                            bank.ap[:, 0:SEC],
                            wb.ap[:, ko + mi * 128:ko + (mi + 1) * 128],
                            b_xn.ap[:, k * SEC:(k + 1) * SEC],
                            start=(k == 0), stop=(k == FT - 1)),
                           reads=(wb, b_xn),
                           extra=tuple(bank.ready) + tuple(bank.readers) if k == 0 else (),
                           inc=(mi == 3))
            for mi in range(4):
                PS_W[mi].ready = [tok]
                PS_W[mi].readers = []
            for mi in range(4):
                m = wave * 4 + mi
                bank = PS_W[mi]
                if m < G:
                    g = m
                    ub = r_uraw.get()
                    act(lambda e, g=g, bank=bank, ub=ub: e.activation(
                        ub.ap[:, 3:SEC + 3], bank.ap[:, 0:SEC], AF.Identity,
                        bias=c_biu.ap[:, g:g + 1]),
                        reads=(bank, c_biu), writes=(ub,))
                    _htok = dve(lambda e, g=g, ub=ub: e.tensor_copy(
                        ub.ap[:, 0:3], st_halo.ap[:, g * 3:(g + 1) * 3]),
                        reads=(st_halo,), extra=tuple(ub.ready))
                    ub.ready = list(ub.ready) + [_htok]
                    a1b = r_small.get()
                    dve(lambda e, g=g, ub=ub, a1b=a1b: e.tensor_scalar(
                        a1b.ap[:, 0:SEC], ub.ap[:, 0:SEC],
                        c_convw.ap[:, g * 4:g * 4 + 1], c_bconv.ap[:, g:g + 1],
                        op0=OP.mult, op1=OP.add), reads=(ub, c_convw, c_bconv),
                        writes=(a1b,))
                    a2b = r_small.get()
                    dve(lambda e, g=g, ub=ub, a1b=a1b, a2b=a2b: e.scalar_tensor_tensor(
                        a2b.ap[:, 0:SEC], ub.ap[:, 1:1 + SEC],
                        c_convw.ap[:, g * 4 + 1:g * 4 + 2], a1b.ap[:, 0:SEC],
                        op0=OP.mult, op1=OP.add), reads=(ub, c_convw, a1b),
                        writes=(a2b,))
                    a3b = r_small.get()
                    dve(lambda e, g=g, ub=ub, a2b=a2b, a3b=a3b: e.scalar_tensor_tensor(
                        a3b.ap[:, 0:SEC], ub.ap[:, 2:2 + SEC],
                        c_convw.ap[:, g * 4 + 2:g * 4 + 3], a2b.ap[:, 0:SEC],
                        op0=OP.mult, op1=OP.add), reads=(ub, c_convw, a2b),
                        writes=(a3b,))
                    zb = r_small.get()
                    dve(lambda e, g=g, ub=ub, a3b=a3b, zb=zb: e.scalar_tensor_tensor(
                        zb.ap[:, 0:SEC], ub.ap[:, 3:3 + SEC],
                        c_convw.ap[:, g * 4 + 3:g * 4 + 4], a3b.ap[:, 0:SEC],
                        op0=OP.mult, op1=OP.add), reads=(ub, c_convw, a3b),
                        writes=(zb,))
                    dve(lambda e, g=g, ub=ub: e.tensor_copy(
                        st_halo.ap[:, g * 3:(g + 1) * 3], ub.ap[:, SEC:SEC + 3]),
                        reads=(ub,), writes=(st_halo,))
                    act(lambda e, g=g, zb=zb: e.activation(
                        b_su.ap[:, g * SEC:(g + 1) * SEC], zb.ap[:, 0:SEC],
                        AF.Silu), reads=(zb,), writes=(b_su,))
                else:
                    rti = m - G
                    srt = r_small.get()
                    act(lambda e, rti=rti, bank=bank, srt=srt: e.activation(
                        srt.ap[:, 0:SEC], bank.ap[:, 0:SEC], AF.Silu,
                        bias=c_bir.ap[:, rti:rti + 1]),
                        reads=(bank, c_bir), writes=(srt,))
                    dma_gp(sres_hbm[rti * 128:(rti + 1) * 128, t0:t0 + SEC],
                           srt.ap[:, 0:SEC], reads=(srt,), writes=(trk[("sres", s)],))
            yield

        for h in range(4):
            dma_gp(su_hbm.rearrange("(g p) t -> p g t", p=128)[:, h * 4:(h + 1) * 4,
                                                              t0:t0 + SEC],
                   b_su.ap[:, h * 4 * SEC:(h + 1) * 4 * SEC].rearrange(
                       "p (g t) -> p g t", t=SEC), reads=(b_su,),
                   writes=(trk[("su", s)],))
        yield
        if taps:
            dma_sp(taps_out["tap_su"].rearrange("(g p) t -> p g t", p=128)[:, :, t0:t0 + SEC],
                   b_su.ap[:].rearrange("p (g t) -> p g t", t=SEC), reads=(b_su,))

        bank = PS_W[0]
        for k in range(G):
            tok = pe(lambda e, k=k, bank=bank: e.matmul(
                bank.ap[:, 0:SEC], c_Wx.ap[:, k * 128:(k + 1) * 128],
                b_su.ap[:, k * SEC:(k + 1) * SEC], start=(k == 0), stop=(k == G - 1)),
               reads=(c_Wx, b_su), writes=(bank,) if k == G - 1 else (),
               extra=tuple(bank.ready) + tuple(bank.readers) if k == 0 else (),
               inc=(k == G - 1))
        b_su.readers.append(tok)
        ssec = s // 2
        half = s % 2
        dtb = r_dt.bufs[ssec % 2]
        Bsb = r_Bsb.bufs[ssec % 2]
        Csb = r_Csb.bufs[ssec % 2]
        act(lambda e: e.activation(dtb.ap[0:64, half * SEC:(half + 1) * SEC],
                                   bank.ap[0:64, 0:SEC], AF.Identity),
            reads=(bank,), writes=(dtb,) if half == 1 else (),
            extra=tuple(dtb.ready) + tuple(dtb.readers) if half == 0 else ())
        act(lambda e: e.activation(Bsb.ap[0:16, half * SEC:(half + 1) * SEC],
                                   bank.ap[64:80, 0:SEC], AF.Identity),
            reads=(bank,), writes=(Bsb,) if half == 1 else (),
            extra=tuple(Bsb.ready) + tuple(Bsb.readers) if half == 0 else ())
        act(lambda e: e.activation(Csb.ap[0:16, half * SEC:(half + 1) * SEC],
                                   bank.ap[96:112, 0:SEC], AF.Identity),
            reads=(bank,), writes=(Csb,) if half == 1 else (),
            extra=tuple(Csb.ready) + tuple(Csb.readers) if half == 0 else ())
        if half == 1:
            par = ssec % 2
            dma_gp(bc_hbm[2 * par:2 * par + 1, :].rearrange(
                       "r (n t) -> r n t", t=SS),
                   Bsb.ap[0:16, 0:SS], reads=(Bsb,),
                   writes=(trk[("bcB", ssec % 2)],))
            dma_gp(bc_hbm[2 * par + 1:2 * par + 2, :].rearrange(
                       "r (n t) -> r n t", t=SS),
                   Csb.ap[0:16, 0:SS], reads=(Csb,),
                   writes=(trk[("bcC", ssec % 2)],))

    from collections import deque
    pend = deque()
    pend2 = deque()

    def pump(kk):
        for _ in range(kk):
            if pend:
                try:
                    next(pend[0])
                except StopIteration:
                    pend.popleft()
            if pend2:
                try:
                    next(pend2[0])
                except StopIteration:
                    pend2.popleft()

    def phase_a2(ssec):
        T0 = ssec * SS
        dtb = r_dt.bufs[ssec % 2]
        Bsb = r_Bsb.bufs[ssec % 2]
        Csb = r_Csb.bufs[ssec % 2]
        if taps:
            dma_sp(taps_out["tap_B"][0:NS, T0:T0 + SS], Bsb.ap[0:NS, 0:SS],
                   reads=(Bsb,))
            dma_sp(taps_out["tap_C"][0:NS, T0:T0 + SS], Csb.ap[0:NS, 0:SS],
                   reads=(Csb,))
        for gb in range(G // 2):
            gs = [2 * gb, 2 * gb + 1]
            sug_of = {}
            for gi, g in enumerate(gs):
                sug = r_sug.get()
                sug_of[g] = sug
                dma_sp(sug.ap[:], su_hbm[g * 128:(g + 1) * 128, T0:T0 + SS],
                       writes=(sug,), reads=(trk[("su", 2 * ssec)], trk[("su", 2 * ssec + 1)]))
                for hh in range(2):
                    bank = PS_Y[gi * 2 + hh]
                    pe(lambda e, g=g, hh=hh, bank=bank: e.matmul(
                        bank.ap[:, 0:512], c_Wdt.ap[0:64, g * 128:(g + 1) * 128],
                        dtb.ap[0:64, hh * 512:(hh + 1) * 512], start=True, stop=True),
                       reads=(c_Wdt, dtb), writes=(bank,))
                    sp1 = r_f32.get()
                    dve(lambda e, g=g, bank=bank, sp1=sp1: e.tensor_scalar(
                        sp1.ap[:, 0:512], bank.ap[:, 0:512],
                        c_bdt.ap[:, g:g + 1], 44.0, op0=OP.add, op1=OP.min),
                        reads=(bank, c_bdt), writes=(sp1,))
                    sp2 = r_f32.get()
                    act(lambda e, sp1=sp1, sp2=sp2: e.activation(
                        sp2.ap[:, 0:512], sp1.ap[:, 0:512], AF.Exp),
                        reads=(sp1,), writes=(sp2,))
                    act(lambda e, gi=gi, hh=hh, sp2=sp2: e.activation(
                        b_dl.ap[:, gi * SS + hh * 512:gi * SS + (hh + 1) * 512],
                        sp2.ap[:, 0:512], AF.Ln, bias=1.0),
                        reads=(sp2,), writes=(b_dl,))
                dve(lambda e, gi=gi, sug=sug: e.tensor_tensor(
                    b_du.ap[:, gi * SS:(gi + 1) * SS], sug.ap[:],
                    b_dl.ap[:, gi * SS:(gi + 1) * SS], OP.mult),
                    reads=(sug, b_dl), writes=(b_du,))
            if taps:
                for gi, g in enumerate(gs):
                    dma_sp(taps_out["tap_delta"][g * 128:(g + 1) * 128, T0:T0 + SS],
                           b_dl.ap[:, gi * SS:(gi + 1) * SS], reads=(b_dl,))

            par = ssec % 2
            for n in range(NS):
                if n % 2 == 0:
                    Brep = r_Brep.get()
                    ctx.dma("act", Brep.ap[:, 0:2 * SS],
                            bc_hbm[2 * par:2 * par + 1,
                                   n * SS:(n + 2) * SS].unsqueeze(1)
                            .broadcast_to((1, 128, 2 * SS)),
                            reads=(trk[("bcB", par)],), writes=(Brep,))
                    Crep = r_Crep.get()
                    ctx.dma("act", Crep.ap[:, 0:2 * SS],
                            bc_hbm[2 * par + 1:2 * par + 2,
                                   n * SS:(n + 2) * SS].unsqueeze(1)
                            .broadcast_to((1, 128, 2 * SS)),
                            reads=(trk[("bcC", par)],), writes=(Crep,))
                no = (n % 2) * SS
                for gi, g in enumerate(gs):
                    idx = g * NS + n
                    dA = r_dA.get()
                    act(lambda e, gi=gi, idx=idx, dA=dA: e.activation(
                        dA.ap[:], b_dl.ap[:, gi * SS:(gi + 1) * SS], AF.Exp,
                        scale=c_A.ap[:, idx:idx + 1]),
                        reads=(b_dl, c_A), writes=(dA,))
                    bt = r_bt.get()
                    dve(lambda e, gi=gi, bt=bt, Brep=Brep, no=no: e.tensor_tensor(
                        bt.ap[:], b_du.ap[:, gi * SS:(gi + 1) * SS],
                        Brep.ap[:, no:no + SS],
                        OP.mult), reads=(b_du, Brep), writes=(bt,))
                    hb = r_h.get()
                    dve(lambda e, dA=dA, bt=bt, hb=hb, idx=idx: e.tensor_tensor_scan(
                        hb.ap[:], dA.ap[:], bt.ap[:], st_h.ap[:, idx:idx + 1],
                        op0=OP.mult, op1=OP.add),
                        reads=(dA, bt, st_h), writes=(hb,))
                    dve(lambda e, hb=hb, idx=idx: e.tensor_copy(
                        st_h.ap[:, idx:idx + 1], hb.ap[:, SS - 1:SS]),
                        reads=(hb,), writes=(st_h,))
                    tmp = r_tmp.get()
                    dve(lambda e, hb=hb, tmp=tmp, Crep=Crep, no=no: e.tensor_tensor(
                        tmp.ap[:], hb.ap[:], Crep.ap[:, no:no + SS], OP.mult),
                        reads=(hb, Crep), writes=(tmp,))
                    for hh in range(2):
                        bank = PS_Y[gi * 2 + hh]
                        pe(lambda e, bank=bank, tmp=tmp, hh=hh, n=n: e.matmul(
                            bank.ap[:, 0:512], c_eye.ap[:],
                            tmp.ap[:, hh * 512:(hh + 1) * 512],
                            start=(n == 0), stop=False),
                            reads=(c_eye, tmp),
                            extra=tuple(bank.ready) + tuple(bank.readers) if n == 0 else (),
                            inc=True)
                    pump(1)
            for gi, g in enumerate(gs):
                for hh in range(2):
                    bank = PS_Y[gi * 2 + hh]
                    pe(lambda e, g=g, hh=hh, bank=bank, sug=sug_of[g]: e.matmul(
                        bank.ap[:, 0:512], c_Dd.ap[:, g * 128:(g + 1) * 128],
                        sug.ap[:, hh * 512:(hh + 1) * 512],
                        start=False, stop=True),
                       reads=(c_Dd, sug_of[g]), writes=(bank,))
            for gi, g in enumerate(gs):
                yv = r_yv.get()
                for hh in range(2):
                    bank = PS_Y[gi * 2 + hh]
                    act(lambda e, yv=yv, hh=hh, bank=bank: e.activation(
                        yv.ap[:, hh * 512:(hh + 1) * 512], bank.ap[:, 0:512],
                        AF.Identity),
                        reads=(bank,), writes=(yv,) if hh == 1 else (),
                        extra=tuple(yv.ready) + tuple(yv.readers) if hh == 0 else ())
                sres = r_sres.get()
                dma_sp(sres.ap[:], sres_hbm[g * 128:(g + 1) * 128, T0:T0 + SS],
                       writes=(sres,), reads=(trk[("sres", 2 * ssec)], trk[("sres", 2 * ssec + 1)]))
                yg = r_tmp.get()
                dve(lambda e, yv=yv, sres=sres, yg=yg: e.tensor_tensor(
                    yg.ap[:], yv.ap[:], sres.ap[:], OP.mult),
                    reads=(yv, sres), writes=(yg,))
                for hh in range(2):
                    sidx = 2 * ssec + hh
                    dma_gp(yg3_hbm[0:128, sidx * (G * SEC) + g * SEC:
                                   sidx * (G * SEC) + (g + 1) * SEC],
                           yg.ap[:, hh * SEC:(hh + 1) * SEC],
                           reads=(yg,), writes=(trk[("yg", sidx)],))
                if taps:
                    dma_sp(taps_out["tap_yg"][g * 128:(g + 1) * 128, T0:T0 + SS],
                           yg.ap[:], reads=(yg,))

    def phase_a3_ffn(s):
        t0 = s * SEC
        yield
        ygbufs = {}
        for wave in range(2):
            for k in range(G):
                kg = k // 4
                if k % 4 == 0:
                    wsl = r_wsl.get()
                    dof = wave * (G * 512) + kg * 2048
                    dma_sp(wsl.ap[:, 0:2048], wout_bf[0:128, dof:dof + 2048],
                           writes=(wsl,), reads=(trk["wout"],))
                    wcur = wsl
                    ygb = r_wsl.get()
                    yof = s * (G * SEC) + kg * 4 * SEC
                    dma_sp(ygb.ap[:, 0:2048], yg3_hbm[0:128, yof:yof + 2048],
                           writes=(ygb,), reads=(trk[("yg", s)],))
                    ygbufs[kg] = ygb
                ko = (k - kg * 4) * 512
                ygc = ygbufs[kg]
                for mi in range(4):
                    bank = PS_W[mi]
                    tok = pe(lambda e, bank=bank, w=wcur, mi=mi, ko=ko, k=k,
                             yg=ygc: e.matmul(
                        bank.ap[:, 0:SEC],
                        w.ap[:, ko + mi * 128:ko + (mi + 1) * 128],
                        yg.ap[:, ko:ko + SEC], start=(k == 0), stop=(k == G - 1)),
                       reads=(wcur, ygc),
                       extra=tuple(bank.ready) + tuple(bank.readers) if k == 0 else (),
                       inc=(mi == 3))
                if k % 4 == 3:
                    yield
            for mi in range(4):
                PS_W[mi].ready = [tok]
                PS_W[mi].readers = []
            for mi in range(4):
                m = wave * 4 + mi
                bank = PS_W[mi]
                ob = r_f32.get()
                act(lambda e, m=m, bank=bank, ob=ob: e.activation(
                    ob.ap[:, 0:SEC], bank.ap[:, 0:SEC], AF.Identity,
                    bias=c_bout.ap[:, m:m + 1]), reads=(bank, c_bout), writes=(ob,))
                t2r = r_small.get()
                dma_sp(t2r.ap[:, 0:SEC], t2_hbm[m * 128:(m + 1) * 128, t0:t0 + SEC],
                       writes=(t2r,), reads=(trk[("t2", s)],))
                e1 = r_f32.get()
                dve(lambda e, ob=ob, t2r=t2r, e1=e1: e.tensor_tensor(
                    e1.ap[:, 0:SEC], ob.ap[:, 0:SEC], t2r.ap[:, 0:SEC], OP.add),
                    reads=(ob, t2r), writes=(e1,))
                x3 = r_f32.get()
                dma_sp(x3.ap[:, 0:SEC], xT[m * 128:(m + 1) * 128, t0:t0 + SEC],
                       writes=(x3,))
                dve(lambda e, m=m, e1=e1, x3=x3: e.scalar_tensor_tensor(
                    b_x1.ap[:, m * SEC:(m + 1) * SEC], e1.ap[:, 0:SEC],
                    c_ga.ap[:, m:m + 1], x3.ap[:, 0:SEC],
                    op0=OP.mult, op1=OP.add), reads=(e1, c_ga, x3), writes=(b_x1,))
                yield
        if taps and s == 0:
            dma_sp(taps_out["tap_x1"].rearrange("(ft p) t -> p ft t", p=128)[:, :, :],
                   b_x1.ap[:].rearrange("p (ft t) -> p ft t", t=SEC), reads=(b_x1,))

        for hq in range(2):
            x1bf = r_xbf.get()
            dve(lambda e, hq=hq, x1bf=x1bf: e.tensor_copy(
                x1bf.ap[:].rearrange("p (ft t) -> p ft t", t=QC),
                b_x1.ap[:].rearrange("p (ft t) -> p ft t", t=QC)[:, hq::2, :]),
                reads=(b_x1,), writes=(x1bf,))
            sq4 = r_xbf.get()
            mu4 = r_rep.get()
            rr4 = r_rep.get()
            layernorm_stats(x1bf, lambda k: k * QC, FT, QC, sq4, mu4, rr4)
            for ft in range(FT):
                f1 = r_small.get()
                dve(lambda e, ft=ft, f1=f1, x1bf=x1bf, mu4=mu4: e.tensor_tensor(
                    f1.ap[:, 0:QC], x1bf.ap[:, ft * QC:(ft + 1) * QC],
                    mu4.ap[:, 0:QC], OP.subtract), reads=(x1bf, mu4), writes=(f1,))
                f2 = r_small.get()
                dve(lambda e, f1=f1, f2=f2, rr4=rr4: e.tensor_tensor(
                    f2.ap[:, 0:QC], f1.ap[:, 0:QC], rr4.ap[:, 0:QC], OP.mult),
                    reads=(f1, rr4), writes=(f2,))
                qb1 = r_f32.get()
                act(lambda e, ft=ft, f2=f2, qb1=qb1: e.activation(
                    qb1.ap[:, 0:QC], f2.ap[:, 0:QC], AF.Identity,
                    scale=c_s16cb.ap[:, ft:ft + 1], bias=c_b16sb.ap[:, ft:ft + 1]),
                    reads=(f2, c_s16cb, c_b16sb), writes=(qb1,))
                qb2 = r_f32.get()
                dve(lambda e, qb1=qb1, qb2=qb2: e.tensor_scalar(
                    qb2.ap[:, 0:QC], qb1.ap[:, 0:QC], MAGIC, 127.0,
                    op0=OP.subtract, op1=OP.min), reads=(qb1,), writes=(qb2,))
                dve(lambda e, ft=ft, hq=hq, qb2=qb2: e.tensor_scalar(
                    b_xq.ap[:, ft * SEC + hq * QC:ft * SEC + (hq + 1) * QC],
                    qb2.ap[:, 0:QC], -128.0, None, op0=OP.max),
                    reads=(qb2,), writes=(b_xq,))
                if ft % 2 == 1:
                    yield
        HC = SEC
        for hc in range(1):
            for wave in range(8):
                wbs = {}
                for k in range(FT):
                    kg = k // 4
                    if kg not in wbs:
                        wb = r_wsl.get()
                        dof = wave * (FT * 512) + kg * 2048
                        dma_sp(wb.ap[:, 0:2048], w1q_bf[0:128, dof:dof + 2048],
                               writes=(wb,), reads=(trk["w1q"],))
                        wbs[kg] = wb
                    wb = wbs[kg]
                    ko = (k - kg * 4) * 512
                    for mi in range(4):
                        bank = PS_W[mi]
                        tok = pe(lambda e, k=k, ko=ko, mi=mi, bank=bank, wb=wb,
                                 hc=hc: e.matmul(
                            bank.ap[:, 0:HC],
                            wb.ap[:, ko + mi * 128:ko + (mi + 1) * 128],
                            b_xq.ap[:, k * SEC + hc * HC:k * SEC + (hc + 1) * HC],
                            start=(k == 0), stop=(k == FT - 1)),
                           reads=(wb, b_xq),
                           extra=tuple(bank.ready) + tuple(bank.readers) if k == 0 else (),
                           inc=(mi == 3))
                    if k % 4 == 3:
                        yield
                for mi in range(4):
                    PS_W[mi].ready = [tok]
                    PS_W[mi].readers = []
                for mi in range(4):
                    m = wave * 4 + mi
                    bank = PS_W[mi]
                    tb = r_f32.get()
                    act(lambda e, m=m, hc=hc, bank=bank, tb=tb: e.activation(
                        tb.ap[:, 0:HC], bank.ap[:, 0:HC], AF.Identity,
                        scale=1.0 / 64.0, bias=c_b1m.ap[:, m:m + 1]),
                        reads=(bank, c_b1m), writes=(tb,))
                    tb2 = r_f32.get()
                    dve(lambda e, tb=tb, tb2=tb2: e.tensor_scalar(
                        tb2.ap[:, 0:HC], tb.ap[:, 0:HC], MAGIC, 127.0,
                        op0=OP.subtract, op1=OP.min), reads=(tb,), writes=(tb2,))
                    act(lambda e, m=m, hc=hc, tb2=tb2: e.activation(
                        b_h2q.ap[:, m * HC:(m + 1) * HC], tb2.ap[:, 0:HC],
                        AF.Relu), reads=(tb2,), writes=(b_h2q,))
                yield
            for wave in range(2):
                for k in range(HT):
                    kg = k // 4
                    if k % 4 == 0:
                        w2s = r_wsl.get()
                        dof = wave * (HT * 512) + kg * 2048
                        dma_sp(w2s.ap[:, 0:2048], w2q_bf[0:128, dof:dof + 2048],
                               writes=(w2s,), reads=(trk["w2q"],))
                        w2cur = w2s
                    ko = (k - kg * 4) * 512
                    for mi in range(4):
                        bank = PS_W[mi]
                        tok = pe(lambda e, k=k, ko=ko, mi=mi, bank=bank,
                                 w2s=w2cur: e.matmul(
                            bank.ap[:, 0:HC],
                            w2s.ap[:, ko + mi * 128:ko + (mi + 1) * 128],
                            b_h2q.ap[:, k * HC:(k + 1) * HC],
                            start=(k == 0), stop=(k == HT - 1)),
                           reads=(w2cur, b_h2q),
                           extra=tuple(bank.ready) + tuple(bank.readers) if k == 0 else (),
                           inc=(mi == 3))
                    if k % 4 == 3:
                        yield
                for mi in range(4):
                    PS_W[mi].ready = [tok]
                    PS_W[mi].readers = []
                for mi in range(4):
                    m = wave * 4 + mi
                    bank = PS_W[mi]
                    o2 = r_f32.get()
                    act(lambda e, m=m, bank=bank, o2=o2: e.activation(
                        o2.ap[:, 0:HC], bank.ap[:, 0:HC], AF.Identity,
                        scale=1.0 / 1024.0, bias=c_b2q.ap[:, m:m + 1]),
                        reads=(bank, c_b2q), writes=(o2,))
                    ot = r_f32.get()
                    dve(lambda e, m=m, o2=o2, ot=ot, hc=hc: e.scalar_tensor_tensor(
                        ot.ap[:, 0:HC], o2.ap[:, 0:HC], c_gb.ap[:, m:m + 1],
                        b_x1.ap[:, m * SEC + hc * HC:m * SEC + (hc + 1) * HC],
                        op0=OP.mult, op1=OP.add),
                        reads=(o2, c_gb, b_x1), writes=(ot,))
                    dma_gp(outT[m * 128:(m + 1) * 128, t0 + hc * HC:t0 + (hc + 1) * HC],
                           ot.ap[:, 0:HC], reads=(ot,))
                    yield

    def drain():
        while pend or pend2:
            pump(1)

    pend2.append(prep_weight(W_out, wout_bf, DI, DM, False, trk["wout"], True))
    pend2.append(prep_weight(w1T, w1q_bf, DM, HID, True, trk["w1q"], True))
    pend2.append(prep_weight(w2T, w2q_bf, HID, DM, True, trk["w2q"], True))
    for _ in phase_a1(0):
        pass
    for _ in phase_a1(1):
        pass
    pend.append(phase_a1(2))
    pend.append(phase_a1(3))
    phase_a2(0)
    for ssec in range(1, NSS):
        drain()
        if 2 * ssec + 2 < NSEC:
            pend.append(phase_a1(2 * ssec + 2))
        if 2 * ssec + 3 < NSEC:
            pend.append(phase_a1(2 * ssec + 3))
        pend.append(phase_a3_ffn(2 * ssec - 2))
        pend.append(phase_a3_ffn(2 * ssec - 1))
        phase_a2(ssec)
    drain()
    pend.append(phase_a3_ffn(NSEC - 2))
    pend.append(phase_a3_ffn(NSEC - 1))
    drain()

    finals = [Tok(s, ctx.cnt[s]) for s in
              Ctx.SP_LANES + Ctx.ACT_LANES + Ctx.GP_LANES if ctx.cnt[s] > 0]
    ctx._emit("sp", lambda e: e.nop(), finals, None, 0)
    ctx._emit("act", lambda e: e.nop(),
              [Tok(s, ctx.cnt[s]) for s in Ctx.ACT_LANES if ctx.cnt[s] > 0],
              None, 0)

    semmap = {k: sems[k] for k in sems}

    with st, nc.Block() as block:

        @block.sync
        def _(sync):
            for thunk in ctx.q["sp"]:
                thunk(sync, semmap)

        @block.vector
        def _(vector):
            for thunk in ctx.q["dve"]:
                thunk(vector, semmap)

        @block.scalar
        def _(scalar):
            for thunk in ctx.q["act"]:
                thunk(scalar, semmap)

        @block.tensor
        def _(tensor):
            for thunk in ctx.q["pe"]:
                thunk(tensor, semmap)

        @block.gpsimd
        def _(gpsimd):
            for thunk in ctx.q["pool"]:
                thunk(gpsimd, semmap)

    return nc, sbufs_bytes[0]


def _vec_tiles(v, ntiles):
    return np.ascontiguousarray(v.reshape(ntiles, 128).T.astype(np.float32))


_prog_cache = {}


def _get_prog(taps=False):
    key = taps
    if key not in _prog_cache:
        _prog_cache[key] = build_program(taps=taps)
    return _prog_cache[key]


def make_inputs(args, taps=False):
    (x, ln_g, ln_b, W_in, b_in, conv_w, conv_b, W_x, W_dt, b_dt, A_log, Dv,
     W_out, b_out, w1, b1, w2, b2, c, wmod_w) = args
    n = 8
    f32 = np.float32
    W_in_h = np.ascontiguousarray(W_in.astype(f32))
    convw_h = np.ascontiguousarray(conv_w.T.reshape(G, 128, 4).transpose(1, 0, 2)
                                   .reshape(128, G * 4).astype(f32))
    Wx_h = np.zeros((DI, 128), f32)
    Wx_h[:, 0:64] = W_x[:, 0:64]
    Wx_h[:, 64:80] = W_x[:, 64:80]
    Wx_h[:, 96:112] = W_x[:, 80:96]
    Wdt_h = np.ascontiguousarray(W_dt.astype(f32))
    A_h = np.ascontiguousarray(
        A_log.reshape(G, 128, NS).transpose(1, 0, 2).reshape(128, G * NS).astype(f32))
    Dd_h = np.zeros((DI, 128), f32)
    idx = np.arange(DI)
    Dd_h[idx, idx % 128] = Dv.astype(f32)
    Wout_h = np.ascontiguousarray(W_out.astype(f32))
    w1T_h = np.ascontiguousarray(w1.T.astype(f32))
    w2T_h = np.ascontiguousarray(w2.T.astype(f32))
    wmod_h = np.ascontiguousarray(wmod_w.astype(f32))
    eye_h = np.eye(128, dtype=f32)
    shared = {
        "W_in": W_in_h, "convw": convw_h, "Wx": Wx_h, "Wdt": Wdt_h,
        "A_log_t": A_h, "D_diag": Dd_h, "W_out": Wout_h, "w1T": w1T_h,
        "w2T": w2T_h, "wmod": wmod_h, "eye": eye_h,
        "v_gamma": _vec_tiles(ln_g, FT), "v_beta": _vec_tiles(ln_b, FT),
        "v_b_in_u": _vec_tiles(b_in[0:DI], G),
        "v_b_in_r": _vec_tiles(b_in[DI:2 * DI], G),
        "v_conv_b": _vec_tiles(conv_b, G), "v_b_dt": _vec_tiles(b_dt, G),
        "v_b_out": _vec_tiles(b_out, FT), "v_b1": _vec_tiles(b1, HT),
        "v_b2": _vec_tiles(b2, FT),
    }
    in_maps = []
    for b in range(n):
        m = dict(shared)
        xt = np.ascontiguousarray(x[b].T.astype(f32))
        m["xT"] = xt
        m["xT4"] = np.ascontiguousarray(
            xt.reshape(FT, 128, L // 256, 256).transpose(1, 2, 0, 3)
            .reshape(128, L * FT))
        m["v_c"] = _vec_tiles(c[b], FT)
        in_maps.append(m)
    return in_maps


def kernel(L0_attn_x, L0_attn_ln_gamma, L0_attn_ln_beta, L0_attn_W_in,
           L0_attn_b_in, L0_attn_conv_w, L0_attn_conv_b, L0_attn_W_x,
           L0_attn_W_dt, L0_attn_b_dt, L0_attn_A_log, L0_attn_D,
           L0_attn_W_out, L0_attn_b_out, L0_ffn_w1, L0_ffn_b1, L0_ffn_w2,
           L0_ffn_b2, L0_c, L0_wmod):
    args = (np.asarray(L0_attn_x), np.asarray(L0_attn_ln_gamma),
            np.asarray(L0_attn_ln_beta), np.asarray(L0_attn_W_in),
            np.asarray(L0_attn_b_in), np.asarray(L0_attn_conv_w),
            np.asarray(L0_attn_conv_b), np.asarray(L0_attn_W_x),
            np.asarray(L0_attn_W_dt), np.asarray(L0_attn_b_dt),
            np.asarray(L0_attn_A_log), np.asarray(L0_attn_D),
            np.asarray(L0_attn_W_out), np.asarray(L0_attn_b_out),
            np.asarray(L0_ffn_w1), np.asarray(L0_ffn_b1),
            np.asarray(L0_ffn_w2), np.asarray(L0_ffn_b2),
            np.asarray(L0_c), np.asarray(L0_wmod))
    in_maps = make_inputs(args)
    nc, _ = _get_prog(taps=False)
    res = run_bass_kernel_spmd(nc, in_maps, core_ids=list(range(8)))
    out = np.stack([np.ascontiguousarray(r["outT"].T) for r in res.results])
    return out.astype(np.float32)


if __name__ == "__main__":
    nc, nbytes = build_program()
    print("built ok, sbuf bytes/partition:", nbytes)


# version 24
# speedup vs baseline: 1.0642x; 1.0189x over previous
import contextlib
import numpy as np
import concourse.bass as bass
import concourse.mybir as mybir
from concourse.bass_utils import run_bass_kernel_spmd
from concourse.alu_op_type import AluOpType as OP

F32 = mybir.dt.float32
BF16 = mybir.dt.bfloat16
AF = mybir.ActivationFunctionType

MAGIC = float(3 * 2 ** 22)

L = 4096
DM = 1024
DI = 2048
NS = 16
DTR = 64
HID = 4096
SEC = 512
NSEC = L // SEC
SS = 1024
NSS = L // SS
G = DI // 128
FT = DM // 128
HT = HID // 128
EPS = 1e-5


class Tok:
    __slots__ = ("sem", "val")

    def __init__(self, sem, val):
        self.sem = sem
        self.val = val


class Buf:
    __slots__ = ("ap", "ready", "readers", "name")

    def __init__(self, ap, name=""):
        self.ap = ap
        self.ready = []
        self.readers = []
        self.name = name

    def __getitem__(self, idx):
        return self.ap[idx]


class Ctx:
    STREAMS = ("pe", "act", "dve", "pool", "sp", "gp")
    SP_LANES = tuple(f"spd{i}" for i in range(16))
    ACT_LANES = tuple(f"actd{i}" for i in range(8))
    GP_LANES = tuple(f"gpd{i}" for i in range(8))
    SEMS = STREAMS + SP_LANES + ACT_LANES + GP_LANES

    def __init__(self):
        self.q = {s: [] for s in self.STREAMS}
        self.q["gp"] = self.q["pool"]
        self.cnt = {s: 0 for s in self.SEMS}
        self.waited = {s: {} for s in self.STREAMS}
        self.waited["gp"] = self.waited["pool"]
        self.lane_i = {"sp": 0, "act": 0, "gp": 0}

    def _emit(self, stream, builder, waits, inc_sem, inc_n, prewait=None):
        ws = []
        for t in waits:
            if t is None:
                continue
            prev = self.waited[stream].get(t.sem, -1)
            if t.val <= prev:
                continue
            self.waited[stream][t.sem] = t.val
            ws.append((t.sem, t.val))
        if prewait is not None:
            s, v = prewait
            if self.waited[stream].get(s, -1) < v:
                self.waited[stream][s] = v
                ws.append((s, v))
        tok = None
        if inc_sem is not None:
            self.cnt[inc_sem] += inc_n
            tok = Tok(inc_sem, self.cnt[inc_sem])

        def thunk(eng, sems, ws=ws, builder=builder, inc_sem=inc_sem, inc_n=inc_n):
            for s, v in ws:
                eng.wait_ge(sems[s], v)
            inst = builder(eng)
            if inc_sem is not None:
                inst.then_inc(sems[inc_sem], inc_n)

        self.q[stream].append(thunk)
        return tok

    def op(self, stream, builder, reads=(), writes=(), extra=(), inc=True):
        waits = []
        for b in reads:
            if isinstance(b, Buf):
                waits.extend(b.ready)
        for b in writes:
            waits.extend(b.ready)
            waits.extend(b.readers)
        waits.extend(extra)
        sem = stream if inc else None
        tok = self._emit(stream, builder, waits, sem, 1 if inc else 0)
        if tok is not None:
            for b in writes:
                b.ready = [tok]
                b.readers = []
            for b in reads:
                if isinstance(b, Buf):
                    b.readers.append(tok)
        return tok

    def dma(self, stream, out, in_, reads=(), writes=(), extra=()):
        lanes = {"sp": self.SP_LANES, "act": self.ACT_LANES,
                 "gp": self.GP_LANES}[stream]
        lane = lanes[self.lane_i[stream] % len(lanes)]
        self.lane_i[stream] += 1
        if isinstance(out, Buf) and out not in writes:
            writes = tuple(writes) + (out,)
        if isinstance(in_, Buf) and in_ not in reads:
            reads = tuple(reads) + (in_,)
        waits = []
        for b in reads:
            if isinstance(b, Buf):
                waits.extend(b.ready)
        for b in writes:
            waits.extend(b.ready)
            waits.extend(b.readers)
        waits.extend(extra)
        out_ap = out.ap[:] if isinstance(out, Buf) else out
        in_ap = in_.ap[:] if isinstance(in_, Buf) else in_

        def builder(eng, o=out_ap, i=in_ap):
            return eng.dma_start(out=o, in_=i)

        tok = self._emit(stream, builder, waits, lane, 16,
                         prewait=(lane, self.cnt[lane]))
        for b in writes:
            b.ready = [tok]
            b.readers = []
        for b in reads:
            if isinstance(b, Buf):
                b.readers.append(tok)
        return tok

    def dma_multi(self, stream, pairs, wbuf, reads=(), extra=()):
        lanes = {"sp": self.SP_LANES, "act": self.ACT_LANES,
                 "gp": self.GP_LANES}[stream]
        old_waits = list(wbuf.ready) + list(wbuf.readers) + list(extra)
        for b in reads:
            if isinstance(b, Buf):
                old_waits.extend(b.ready)
        toks = []
        for out_ap, in_ap in pairs:
            lane = lanes[self.lane_i[stream] % len(lanes)]
            self.lane_i[stream] += 1

            def builder(eng, o=out_ap, i=in_ap):
                return eng.dma_start(out=o, in_=i)

            toks.append(self._emit(stream, builder, old_waits, lane, 16,
                                   prewait=(lane, self.cnt[lane])))
        wbuf.ready = list(toks)
        wbuf.readers = []
        for b in reads:
            if isinstance(b, Buf):
                b.readers.extend(toks)
        return toks


class Rot:

    def __init__(self, bufs):
        self.bufs = bufs
        self.i = 0

    def get(self):
        b = self.bufs[self.i % len(self.bufs)]
        self.i += 1
        return b


def build_program(taps=False):
    nc = bass.Bass("TRN2", target_bir_lowering=False)

    dram = {}

    def din(name, shape, dtype=F32):
        dram[name] = nc.dram_tensor(name, shape, dtype, kind="ExternalInput").ap()
        return dram[name]

    def dscratch(name, shape, dtype):
        dram[name] = nc.dram_tensor(name, shape, dtype, kind="Internal").ap()
        return dram[name]

    xT = din("xT", [DM, L])
    xT4 = din("xT4", [128, L * FT])
    W_in = din("W_in", [128, DM * 2 * DI // 128])
    convw = din("convw", [128, G * 4])
    Wx = din("Wx", [DI, 128])
    Wdt = din("Wdt", [DTR, DI])
    A_log_t = din("A_log_t", [128, G * NS])
    D_diag = din("D_diag", [DI, 128])
    W_out = din("W_out", [128, DI * DM // 128])
    w1T = din("w1T", [128, DM * HID // 128])
    w2T = din("w2T", [128, HID * DM // 128])
    wmod = din("wmod", [DM, 6 * DM])
    eye = din("eye", [128, 128])
    v_gamma = din("v_gamma", [128, FT])
    v_beta = din("v_beta", [128, FT])
    v_b_in_u = din("v_b_in_u", [128, G])
    v_b_in_r = din("v_b_in_r", [128, G])
    v_conv_b = din("v_conv_b", [128, G])
    v_b_dt = din("v_b_dt", [128, G])
    v_b_out = din("v_b_out", [128, FT])
    v_b1 = din("v_b1", [128, HT])
    v_b2 = din("v_b2", [128, FT])
    v_c = din("v_c", [128, FT])

    outT = nc.dram_tensor("outT", [DM, L], F32, kind="ExternalOutput").ap()

    win_bf = dscratch("win_bf", [128, FT * 2 * DI], BF16)
    wout_bf = dscratch("wout_bf", [128, G * DM], BF16)
    w1q_bf = dscratch("w1q_bf", [128, FT * HID], BF16)
    w2q_bf = dscratch("w2q_bf", [128, HT * DM], BF16)
    t2_hbm = dscratch("t2_hbm", [DM, L], BF16)
    yg3_hbm = dscratch("yg3_hbm", [128, NSEC * G * SEC], BF16)
    bc_hbm = dscratch("bc_hbm", [4, NS * SS], BF16)
    su_hbm = dscratch("su_hbm", [DI, L], BF16)
    sres_hbm = dscratch("sres_hbm", [DI, L], BF16)
    trk = {nm: Buf(None, nm) for nm in ("win", "wout", "w1q", "w2q")}
    for nm in ("t2", "su", "sres", "yg"):
        for s_ in range(NSEC):
            trk[(nm, s_)] = Buf(None, f"{nm}{s_}")
    for nm in ("bcB", "bcC"):
        for s_ in range(2):
            trk[(nm, s_)] = Buf(None, f"{nm}{s_}")

    taps_out = {}
    if taps:
        for nm, shape, tdt in [("tap_xn", [DM, L], BF16), ("tap_su", [DI, L], BF16),
                               ("tap_delta", [DI, L], BF16), ("tap_B", [NS, L], BF16),
                               ("tap_C", [NS, L], BF16), ("tap_yg", [DI, L], BF16),
                               ("tap_x1", [DM, SEC], BF16), ("tap_t2", [DM, L], BF16),
                               ("tap_x", [DM, L], F32), ("tap_modv", [128, 48], F32),
                               ("tap_sc", [128, FT], F32), ("tap_wq", [128, 128], BF16),
                               ("tap_scbf", [128, FT], BF16), ("tap_bank0", [128, 1], F32),
                               ("tap_snap", [128, FT], BF16)]:
            taps_out[nm] = nc.dram_tensor(nm, shape, tdt, kind="ExternalOutput").ap()

    st = contextlib.ExitStack()
    sbufs_bytes = [0]

    def sb(name, shape, dtype):
        t = st.enter_context(nc.sbuf_tensor(name, shape, dtype))
        esz = 4 if dtype == F32 else 2
        sbufs_bytes[0] += shape[1] * esz
        return Buf(t, name)

    def sbrot(name, n, shape, dtype):
        return Rot([sb(f"{name}{i}", shape, dtype) for i in range(n)])

    c_ones = sb("c_ones", [128, 1], BF16)
    c_eye = sb("c_eye", [128, 128], BF16)
    c_A = sb("c_A", [128, G * NS], F32)
    c_convw = sb("c_convw", [128, G * 4], F32)
    c_Dd = sb("c_Dd", [128, G * 128], BF16)
    c_Wx = sb("c_Wx", [128, G * 128], BF16)
    c_Wdt = sb("c_Wdt", [64, DI], BF16)
    c_gam = sb("c_gam", [128, FT], F32)
    c_bet = sb("c_bet", [128, FT], F32)
    c_biu = sb("c_biu", [128, G], F32)
    c_bir = sb("c_bir", [128, G], F32)
    c_bir_h = sb("c_bir_h", [128, G], F32)
    c_bconv = sb("c_bconv", [128, G], F32)
    c_bdt = sb("c_bdt", [128, G], F32)
    c_bout = sb("c_bout", [128, FT], F32)
    c_b1m = sb("c_b1m", [128, HT], F32)
    c_b2q = sb("c_b2q", [128, FT], F32)
    c_sa = sb("c_sa", [128, FT], F32)
    c_ca1 = sb("c_ca1", [128, FT], F32)
    c_ga = sb("c_ga", [128, FT], F32)
    c_gb = sb("c_gb", [128, FT], F32)
    c_s16cb = sb("c_s16cb", [128, FT], F32)
    c_b16sb = sb("c_b16sb", [128, FT], F32)
    st_h = sb("st_h", [128, G * NS], F32)
    st_halo = sb("st_halo", [128, G * 3], BF16)
    c_cst = sb("c_cst", [128, 7], F32)

    QC = SEC // 2
    r_x = sbrot("r_x", 1, [128, FT * QC], F32)
    r_xbf = sbrot("r_xbf", 2, [128, FT * QC], BF16)
    b_t2 = sb("b_t2", [128, FT * SEC], BF16)
    b_xn = sb("b_xn", [128, FT * SEC], BF16)
    r_uraw = sbrot("r_uraw", 2, [128, SEC + 3], BF16)
    b_su = sb("b_su", [128, G * SEC], BF16)
    r_wsl = sbrot("r_wsl", 2, [128, 2048], BF16)
    r_small = sbrot("r_small", 3, [128, SEC], BF16)
    r_f32 = sbrot("r_f32", 3, [128, SEC], F32)
    r_rep = sbrot("r_rep", 2, [128, QC], BF16)
    r_dt = sbrot("r_dt", 2, [64, SS], BF16)
    r_Bsb = sbrot("r_Bsb", 2, [16, SS], BF16)
    r_Csb = sbrot("r_Csb", 2, [16, SS], BF16)
    b_dl = sb("b_dl", [128, 2 * SS], BF16)
    b_du = sb("b_du", [128, 2 * SS], BF16)
    r_sug = sbrot("r_sug", 2, [128, SS], BF16)
    r_Brep = sbrot("r_Brep", 2, [128, 2 * SS], BF16)
    r_Crep = sbrot("r_Crep", 2, [128, 2 * SS], BF16)
    r_dA = sbrot("r_dA", 2, [128, SS], BF16)
    r_bt = sbrot("r_bt", 2, [128, SS], BF16)
    r_h = sbrot("r_h", 2, [128, SS], F32)
    r_tmp = sbrot("r_tmp", 3, [128, SS], BF16)
    r_yv = sbrot("r_yv", 1, [128, SS], BF16)
    r_sres = sbrot("r_sres", 1, [128, SS], BF16)
    b_x1 = sb("b_x1", [128, FT * SEC], BF16)
    b_xq = sb("b_xq", [128, FT * SEC], BF16)
    b_h2q = sb("b_h2q", [128, HT * SEC], BF16)
    r_wst = sbrot("r_wst", 2, [128, 512], F32)
    r_wbf = sbrot("r_wbf", 2, [128, 512], BF16)
    b_modv = sb("b_modv", [128, 48], F32)
    b_vc = sb("b_vc", [128, FT], F32)
    r_tiny = sbrot("r_tiny", 3, [128, FT], F32)
    r_stat = sbrot("r_stat", 3, [1, QC], F32)
    r_statb = sbrot("r_statb", 2, [1, QC], BF16)

    psum = [Buf(st.enter_context(nc.psum_tensor(f"ps{i}", [128, 512], F32)), f"ps{i}")
            for i in range(8)]
    PS_Y = psum[0:4]
    PS_W = psum[4:8]

    sems = {}
    for s in Ctx.SEMS:
        sems[s] = st.enter_context(nc.semaphore(f"sem_{s}"))

    ctx = Ctx()

    def act(builder, reads=(), writes=(), extra=()):
        return ctx.op("act", builder, reads, writes, extra)

    def dve(builder, reads=(), writes=(), extra=()):
        return ctx.op("dve", builder, reads, writes, extra)

    def pool(builder, reads=(), writes=(), extra=()):
        return ctx.op("pool", builder, reads, writes, extra)

    def pe(builder, reads=(), writes=(), extra=(), inc=True):
        return ctx.op("pe", builder, reads, writes, extra, inc=inc)

    def dma_sp(out, in_, reads=(), writes=(), extra=()):
        return ctx.dma("sp", out, in_, reads, writes, extra)

    def dma_act(out, in_, reads=(), writes=(), extra=()):
        return ctx.dma("act", out, in_, reads, writes, extra)

    def dma_gp(out, in_, reads=(), writes=(), extra=()):
        return ctx.dma("gp", out, in_, reads, writes, extra)

    def bcast(dst_buf, src_ap, cols, reads=()):
        in_b = src_ap.unsqueeze(1).broadcast_to((1, 128, cols))
        return ctx.dma("act", dst_buf.ap[:, 0:cols], in_b, reads=reads,
                       writes=(dst_buf,))

    cst_vals = [EPS, 1.0 / 64.0, 1.0 / 1024.0, -0.5, 0.5, 1.0, MAGIC]
    cst_tok = None
    for i, v in enumerate(cst_vals):
        cst_tok = dve(lambda e, i=i, v=v: e.memset(c_cst.ap[:, i:i + 1], v),
                      writes=(c_cst,))
        try:
            nc.const_aps.register(F32, float(v), c_cst.ap[:, i:i + 1])
        except Exception:
            pass
    ctx._emit("act", lambda e: e.nop(), [cst_tok], None, 0)
    ctx._emit("pool", lambda e: e.nop(), [cst_tok], None, 0)

    dma_sp(c_A, A_log_t)
    dma_sp(c_convw, convw)
    dma_sp(c_gam, v_gamma)
    dma_sp(c_bet, v_beta)
    dma_sp(c_biu, v_b_in_u)
    dma_sp(c_bir, v_b_in_r)
    dma_sp(c_bconv, v_conv_b)
    dma_sp(c_bdt, v_b_dt)
    dma_sp(c_bout, v_b_out)
    dma_sp(b_vc, v_c)

    eye_f = r_wst.get()
    dma_sp(eye_f.ap[:, 0:128], eye, writes=(eye_f,))
    dve(lambda e: e.tensor_copy(c_eye.ap[:], eye_f.ap[:, 0:128]),
        reads=(eye_f,), writes=(c_eye,))
    dve(lambda e: e.memset(c_ones.ap[:], 1.0), writes=(c_ones,))
    dve(lambda e: e.memset(st_h.ap[:], 0.0), writes=(st_h,))
    dve(lambda e: e.memset(st_halo.ap[:], 0.0), writes=(st_halo,))

    tA = r_wst.get()
    act(lambda e: e.activation(tA.ap[:, 0:G * NS], c_A.ap[:], AF.Exp),
        reads=(c_A,), writes=(tA,))
    dve(lambda e: e.tensor_scalar(c_A.ap[:], tA.ap[:, 0:G * NS], -1.0, None, op0=OP.mult),
        reads=(tA,), writes=(c_A,))

    dve(lambda e: e.tensor_scalar(c_bir_h.ap[:], c_bir.ap[:], 0.5, None, op0=OP.mult),
        reads=(c_bir,), writes=(c_bir_h,))

    for g in range(G):
        w = r_wst.get()
        dma_sp(w.ap[:, 0:128], D_diag[g * 128:(g + 1) * 128, :], writes=(w,))
        dve(lambda e, g=g, w=w: e.tensor_scalar(
            c_Dd.ap[:, g * 128:(g + 1) * 128], w.ap[:, 0:128], 1.0, None, op0=OP.mult),
            reads=(w,), writes=(c_Dd,))

    for g in range(G):
        w = r_wst.get()
        dma_sp(w.ap[:, 0:128], Wx[g * 128:(g + 1) * 128, :], writes=(w,))
        dve(lambda e, g=g, w=w: e.tensor_copy(
            c_Wx.ap[:, g * 128:(g + 1) * 128], w.ap[:, 0:128]),
            reads=(w,), writes=(c_Wx,))
    for hw in range(8):
        wdt_f = r_wst.get()
        dma_sp(wdt_f.ap[0:64, 0:256], Wdt[0:64, hw * 256:(hw + 1) * 256],
               writes=(wdt_f,))
        dve(lambda e, hw=hw, wdt_f=wdt_f: e.tensor_copy(
            c_Wdt.ap[0:64, hw * 256:(hw + 1) * 256], wdt_f.ap[0:64, 0:256]),
            reads=(wdt_f,), writes=(c_Wdt,))

    wb = r_wst.get()
    dma_sp(wb.ap[:, 0:HT], v_b1, writes=(wb,))
    q1 = r_f32.get()
    dve(lambda e, q1=q1, wb=wb: e.tensor_scalar(
        q1.ap[:, 0:HT], wb.ap[:, 0:HT], 64.0, MAGIC,
        op0=OP.mult, op1=OP.add), reads=(wb,), writes=(q1,))
    q2 = r_f32.get()
    dve(lambda e, q1=q1, q2=q2: e.tensor_scalar(
        q2.ap[:, 0:HT], q1.ap[:, 0:HT], MAGIC, 127.0,
        op0=OP.subtract, op1=OP.min), reads=(q1,), writes=(q2,))
    q3 = r_f32.get()
    dve(lambda e, q2=q2, q3=q3: e.tensor_scalar(
        q3.ap[:, 0:HT], q2.ap[:, 0:HT], -128.0, None,
        op0=OP.max), reads=(q2,), writes=(q3,))
    dve(lambda e, q3=q3: e.tensor_scalar(
        c_b1m.ap[:], q3.ap[:, 0:HT], 0.25, MAGIC,
        op0=OP.mult, op1=OP.add), reads=(q3,), writes=(c_b1m,))
    wb2 = r_wst.get()
    dma_sp(wb2.ap[:, 0:FT], v_b2, writes=(wb2,))
    q1 = r_tiny.get()
    dve(lambda e, q1=q1, wb2=wb2: e.tensor_scalar(
        q1.ap[:, 0:FT], wb2.ap[:, 0:FT], 64.0, MAGIC,
        op0=OP.mult, op1=OP.add), reads=(wb2,), writes=(q1,))
    q2 = r_tiny.get()
    dve(lambda e, q1=q1, q2=q2: e.tensor_scalar(
        q2.ap[:, 0:FT], q1.ap[:, 0:FT], MAGIC, 127.0,
        op0=OP.subtract, op1=OP.min), reads=(q1,), writes=(q2,))
    q3 = r_tiny.get()
    dve(lambda e, q2=q2, q3=q3: e.tensor_scalar(
        q3.ap[:, 0:FT], q2.ap[:, 0:FT], -128.0, None,
        op0=OP.max), reads=(q2,), writes=(q3,))
    dve(lambda e, q3=q3: e.tensor_scalar(
        c_b2q.ap[:], q3.ap[:, 0:FT], 1.0 / 64.0, None,
        op0=OP.mult), reads=(q3,), writes=(c_b2q,))

    tc = r_tiny.get()
    act(lambda e: e.activation(tc.ap[:, 0:FT], b_vc.ap[:], AF.Tanh, scale=0.5),
        reads=(b_vc,), writes=(tc,))
    scv = r_tiny.get()
    dve(lambda e: e.scalar_tensor_tensor(scv.ap[:, 0:FT], tc.ap[:, 0:FT], 1.0,
                                         b_vc.ap[:], op0=OP.add, op1=OP.mult),
        reads=(tc, b_vc), writes=(scv,))
    sc_bf = r_small.get()
    dve(lambda e: e.tensor_copy(sc_bf.ap[:, 0:FT], scv.ap[:, 0:FT]),
        reads=(scv,), writes=(sc_bf,))
    if taps:
        dma_sp(taps_out["tap_sc"], scv.ap[:, 0:FT], reads=(scv,))
        dma_sp(taps_out["tap_scbf"], sc_bf.ap[:, 0:FT], reads=(sc_bf,))
        snap = sb("dbg_snap", [128, FT], BF16)
        dve(lambda e, snap=snap: e.tensor_copy(snap.ap[:], sc_bf.ap[:, 0:FT]),
            reads=(sc_bf,), writes=(snap,))
        dma_sp(taps_out["tap_snap"], snap.ap[:], reads=(snap,))
    for m in range(48):
        bank = PS_W[m % 2]
        group_bufs = [sc_bf]
        for k in range(FT):
            w = r_wst.get()
            dma_sp(w.ap[:, 0:128], wmod[k * 128:(k + 1) * 128,
                                        m * 128:m * 128 + 128], writes=(w,))
            wq = r_wsl.get()
            dve(lambda e, w=w, wq=wq: e.tensor_copy(wq.ap[:, 0:128], w.ap[:, 0:128]),
                reads=(w,), writes=(wq,))
            group_bufs.append(wq)
            if taps and m == 0 and k == 0:
                dma_sp(taps_out["tap_wq"], wq.ap[:, 0:128], reads=(wq,))
            tok = pe(lambda e, wq=wq, k=k, bank=bank: e.matmul(
                bank.ap[:, 0:1], wq.ap[:, 0:128], sc_bf.ap[:, k:k + 1],
                start=(k == 0), stop=(k == FT - 1)),
                reads=(wq, sc_bf), writes=(bank,) if k == FT - 1 else (),
                extra=tuple(bank.ready) + tuple(bank.readers) if k == 0 else (),
                inc=True)
        if taps and m == 0:
            dbg = r_f32.get()
            act(lambda e, bank=bank, dbg=dbg: e.activation(
                dbg.ap[:, 0:1], bank.ap[:, 0:1], AF.Identity),
                reads=(bank,), writes=(dbg,))
            dma_sp(taps_out["tap_bank0"], dbg.ap[:, 0:1], reads=(dbg,))
        act(lambda e, m=m, bank=bank: e.activation(
            b_modv.ap[:, m:m + 1], bank.ap[:, 0:1], AF.Identity, scale=0.5),
            reads=(bank,), writes=(b_modv,))

    dve(lambda e: e.tensor_copy(c_sa.ap[:], b_modv.ap[:, 0:FT]),
        reads=(b_modv,), writes=(c_sa,))
    dve(lambda e: e.tensor_scalar(c_ca1.ap[:], b_modv.ap[:, FT:2 * FT], 1.0, None,
                                  op0=OP.add), reads=(b_modv,), writes=(c_ca1,))

    def clamp_gate(dst, col0):
        gneg = r_tiny.get()
        dve(lambda e: e.tensor_scalar(gneg.ap[:, 0:FT],
                                      b_modv.ap[:, col0:col0 + FT], -1.0, None,
                                      op0=OP.mult), reads=(b_modv,), writes=(gneg,))
        gabs = r_tiny.get()
        dve(lambda e: e.tensor_tensor(gabs.ap[:, 0:FT], gneg.ap[:, 0:FT],
                                      b_modv.ap[:, col0:col0 + FT], OP.max),
            reads=(gneg, b_modv), writes=(gabs,))
        gmag = r_tiny.get()
        dve(lambda e: e.tensor_scalar(gmag.ap[:, 0:FT], gabs.ap[:, 0:FT], 0.25, None,
                                      op0=OP.max), reads=(gabs,), writes=(gmag,))
        gge = r_tiny.get()
        dve(lambda e: e.tensor_scalar(gge.ap[:, 0:FT],
                                      b_modv.ap[:, col0:col0 + FT], 0.0, None,
                                      op0=OP.is_ge), reads=(b_modv,), writes=(gge,))
        gsgn = r_tiny.get()
        dve(lambda e: e.tensor_scalar(gsgn.ap[:, 0:FT], gge.ap[:, 0:FT], 2.0, -1.0,
                                      op0=OP.mult, op1=OP.add), reads=(gge,), writes=(gsgn,))
        dve(lambda e: e.tensor_tensor(dst.ap[:], gmag.ap[:, 0:FT], gsgn.ap[:, 0:FT],
                                      OP.mult), reads=(gmag, gsgn), writes=(dst,))

    clamp_gate(c_ga, 2 * FT)
    clamp_gate(c_gb, 5 * FT)
    cb1 = r_tiny.get()
    dve(lambda e: e.tensor_scalar(cb1.ap[:, 0:FT], b_modv.ap[:, 4 * FT:5 * FT],
                                  1.0, None, op0=OP.add), reads=(b_modv,), writes=(cb1,))
    dve(lambda e: e.tensor_scalar(c_s16cb.ap[:], cb1.ap[:, 0:FT], 16.0, None,
                                  op0=OP.mult), reads=(cb1,), writes=(c_s16cb,))
    dve(lambda e: e.tensor_scalar(c_b16sb.ap[:], b_modv.ap[:, 3 * FT:4 * FT],
                                  16.0, MAGIC, op0=OP.mult, op1=OP.add),
        reads=(b_modv,), writes=(c_b16sb,))
    if taps:
        dma_sp(taps_out["tap_modv"], b_modv.ap[:], reads=(b_modv,))

    def prep_weight(src_t, dst, total, quant, track, do_yield=False):
        CW = 512
        piece = 0
        for c0 in range(0, total, CW):
            w = r_wst.get()
            dma_sp(w.ap[:, 0:CW], src_t[0:128, c0:c0 + CW], writes=(w,))
            wbf = r_wbf.get()
            if quant:
                t1 = r_wst.get()
                act(lambda e, w=w, t1=t1: e.activation(
                    t1.ap[:, 0:CW], w.ap[:, 0:CW], AF.Identity,
                    scale=64.0, bias=c_cst.ap[:, 6:7]),
                    reads=(w, c_cst), writes=(t1,))
                dve(lambda e, t1=t1: e.tensor_scalar(
                    t1.ap[:, 0:CW], t1.ap[:, 0:CW], MAGIC, 127.0,
                    op0=OP.subtract, op1=OP.min), reads=(t1,), writes=(t1,))
                dve(lambda e, t1=t1, wbf=wbf: e.tensor_scalar(
                    wbf.ap[:, 0:CW], t1.ap[:, 0:CW], -128.0, None,
                    op0=OP.max), reads=(t1,), writes=(wbf,))
            else:
                act(lambda e, w=w, wbf=wbf: e.activation(
                    wbf.ap[:, 0:CW], w.ap[:, 0:CW], AF.Identity),
                    reads=(w,), writes=(wbf,))
            dma_sp(dst[0:128, c0:c0 + CW], wbf.ap[:, 0:CW],
                   reads=(wbf,), writes=(track,))
            piece += 1
            if do_yield and piece % 2 == 0:
                yield

    for _ in prep_weight(W_in, win_bf, DM * 2 * DI // 128, False, trk["win"]):
        pass

    def layernorm_stats(src_bf, col0, n_feat_tiles, width, sq_buf, mu_rep, r_rep_):
        nfeat = float(n_feat_tiles * 128)
        cols = [col0(k) for k in range(n_feat_tiles)]
        col0 = lambda k, cols=cols: cols[k]
        for k in range(n_feat_tiles):
            dve(lambda e, k=k: e.tensor_tensor(
                sq_buf.ap[:, k * width:(k + 1) * width],
                src_bf.ap[:, col0(k):col0(k) + width],
                src_bf.ap[:, col0(k):col0(k) + width], OP.mult),
                reads=(src_bf,), writes=(sq_buf,) if k == n_feat_tiles - 1 else (),
                extra=tuple(sq_buf.ready) + tuple(sq_buf.readers) if k == 0 else ())
        bank_s = PS_W[0]
        bank_q = PS_W[1]
        for k in range(n_feat_tiles):
            t_s = pe(lambda e, k=k: e.matmul(
                bank_s.ap[0:1, 0:width], c_ones.ap[:],
                src_bf.ap[:, col0(k):col0(k) + width],
                start=(k == 0), stop=(k == n_feat_tiles - 1)),
                reads=(c_ones, src_bf),
                writes=(bank_s,) if k == n_feat_tiles - 1 else (),
                extra=tuple(bank_s.ready) + tuple(bank_s.readers) if k == 0 else (),
                inc=(k == n_feat_tiles - 1))
        src_bf.readers.append(t_s)
        for k in range(n_feat_tiles):
            t_q = pe(lambda e, k=k: e.matmul(
                bank_q.ap[0:1, 0:width], c_ones.ap[:],
                sq_buf.ap[:, k * width:(k + 1) * width],
                start=(k == 0), stop=(k == n_feat_tiles - 1)),
                reads=(c_ones, sq_buf),
                writes=(bank_q,) if k == n_feat_tiles - 1 else (),
                extra=tuple(bank_q.ready) + tuple(bank_q.readers) if k == 0 else (),
                inc=(k == n_feat_tiles - 1))
        sq_buf.readers.append(t_q)
        mu = r_stat.get()
        dve(lambda e: e.tensor_scalar(mu.ap[0:1, 0:width], bank_s.ap[0:1, 0:width],
                                      1.0 / nfeat, None, op0=OP.mult),
            reads=(bank_s,), writes=(mu,))
        mub = r_statb.get()
        dve(lambda e: e.tensor_copy(mub.ap[0:1, 0:width], mu.ap[0:1, 0:width]),
            reads=(mu,), writes=(mub,))
        m2 = r_stat.get()
        dve(lambda e: e.tensor_tensor(m2.ap[0:1, 0:width], mu.ap[0:1, 0:width],
                                      mu.ap[0:1, 0:width], OP.mult),
            reads=(mu,), writes=(m2,))
        var = r_stat.get()
        dve(lambda e: e.scalar_tensor_tensor(var.ap[0:1, 0:width],
                                             bank_q.ap[0:1, 0:width], 1.0 / nfeat,
                                             m2.ap[0:1, 0:width],
                                             op0=OP.mult, op1=OP.subtract),
            reads=(bank_q, m2), writes=(var,))
        lnv = r_stat.get()
        act(lambda e: e.activation(lnv.ap[0:1, 0:width], var.ap[0:1, 0:width],
                                   AF.Ln, bias=EPS), reads=(var,), writes=(lnv,))
        rin = r_stat.get()
        act(lambda e: e.activation(rin.ap[0:1, 0:width], lnv.ap[0:1, 0:width],
                                   AF.Exp, scale=-0.5), reads=(lnv,), writes=(rin,))
        rb = r_statb.get()
        dve(lambda e: e.tensor_copy(rb.ap[0:1, 0:width], rin.ap[0:1, 0:width]),
            reads=(rin,), writes=(rb,))
        bcast(mu_rep, mub.ap[0:1, 0:width], width, reads=(mub,))
        bcast(r_rep_, rb.ap[0:1, 0:width], width, reads=(rb,))

    def phase_a1(s):
        t0 = s * SEC
        for hq in range(2):
            tq = t0 + hq * QC
            xb = r_x.get()
            qi = tq // QC
            dma_sp(xb, xT4[0:128, qi * (FT * QC):(qi + 1) * (FT * QC)])
            if taps:
                dma_sp(taps_out["tap_x"].rearrange("(ft p) t -> p ft t", p=128)[:, :, tq:tq + QC],
                       xb.ap[:].rearrange("p (ft t) -> p ft t", t=QC), reads=(xb,))
            xbf = r_xbf.get()
            dve(lambda e, xb=xb, xbf=xbf: e.tensor_copy(xbf.ap[:], xb.ap[:]),
                reads=(xb,), writes=(xbf,))
            sq = r_xbf.get()
            mu1 = r_rep.get()
            rr1 = r_rep.get()
            layernorm_stats(xbf, lambda k: k * QC, FT, QC, sq, mu1, rr1)
            for ft in range(FT):
                d1 = r_small.get()
                dve(lambda e, ft=ft, d1=d1, xb=xb, mu1=mu1: e.tensor_tensor(
                    d1.ap[:, 0:QC], xb.ap[:, ft * QC:(ft + 1) * QC],
                    mu1.ap[:, 0:QC], OP.subtract), reads=(xb, mu1), writes=(d1,))
                d2 = r_small.get()
                dve(lambda e, d1=d1, d2=d2, rr1=rr1: e.tensor_tensor(
                    d2.ap[:, 0:QC], d1.ap[:, 0:QC], rr1.ap[:, 0:QC], OP.mult),
                    reads=(d1, rr1), writes=(d2,))
                act(lambda e, ft=ft, hq=hq, d2=d2: e.activation(
                    b_t2.ap[:, ft * SEC + hq * QC:ft * SEC + (hq + 1) * QC],
                    d2.ap[:, 0:QC], AF.Identity,
                    scale=c_ca1.ap[:, ft:ft + 1], bias=c_sa.ap[:, ft:ft + 1]),
                    reads=(d2, c_ca1, c_sa), writes=(b_t2,))
            sq2 = r_xbf.get()
            mu2 = r_rep.get()
            rr2 = r_rep.get()
            layernorm_stats(b_t2, lambda k: k * SEC + hq * QC, FT, QC, sq2, mu2, rr2)
            for ft in range(FT):
                e1 = r_small.get()
                dve(lambda e, ft=ft, hq=hq, e1=e1, mu2=mu2: e.tensor_tensor(
                    e1.ap[:, 0:QC],
                    b_t2.ap[:, ft * SEC + hq * QC:ft * SEC + (hq + 1) * QC],
                    mu2.ap[:, 0:QC], OP.subtract), reads=(b_t2, mu2), writes=(e1,))
                e2 = r_small.get()
                dve(lambda e, e1=e1, e2=e2, rr2=rr2: e.tensor_tensor(
                    e2.ap[:, 0:QC], e1.ap[:, 0:QC], rr2.ap[:, 0:QC], OP.mult),
                    reads=(e1, rr2), writes=(e2,))
                act(lambda e, ft=ft, hq=hq, e2=e2: e.activation(
                    b_xn.ap[:, ft * SEC + hq * QC:ft * SEC + (hq + 1) * QC],
                    e2.ap[:, 0:QC], AF.Identity,
                    scale=c_gam.ap[:, ft:ft + 1], bias=c_bet.ap[:, ft:ft + 1]),
                    reads=(e2, c_gam, c_bet), writes=(b_xn,))
            yield
        dma_gp(t2_hbm.rearrange("(ft p) t -> p ft t", p=128)[:, :, t0:t0 + SEC],
               b_t2.ap[:].rearrange("p (ft t) -> p ft t", t=SEC), reads=(b_t2,),
               writes=(trk[("t2", s)],))
        if taps:
            dma_sp(taps_out["tap_t2"].rearrange("(ft p) t -> p ft t", p=128)[:, :, t0:t0 + SEC],
                   b_t2.ap[:].rearrange("p (ft t) -> p ft t", t=SEC), reads=(b_t2,))
        if taps:
            dma_sp(taps_out["tap_xn"].rearrange("(ft p) t -> p ft t", p=128)[:, :, t0:t0 + SEC],
                   b_xn.ap[:].rearrange("p (ft t) -> p ft t", t=SEC), reads=(b_xn,))

        for wave in range(8):
            for kg in range(2):
                wb = r_wsl.get()
                dof = wave * (FT * 512) + kg * 2048
                dma_sp(wb.ap[:, 0:2048], win_bf[0:128, dof:dof + 2048],
                       writes=(wb,), reads=(trk["win"],))
                for k in range(kg * 4, kg * 4 + 4):
                    ko = (k - kg * 4) * 512
                    for mi in range(4):
                        bank = PS_W[mi]
                        tok = pe(lambda e, k=k, ko=ko, mi=mi, bank=bank, wb=wb: e.matmul(
                            bank.ap[:, 0:SEC],
                            wb.ap[:, ko + mi * 128:ko + (mi + 1) * 128],
                            b_xn.ap[:, k * SEC:(k + 1) * SEC],
                            start=(k == 0), stop=(k == FT - 1)),
                           reads=(wb, b_xn),
                           extra=tuple(bank.ready) + tuple(bank.readers) if k == 0 else (),
                           inc=(mi == 3))
            for mi in range(4):
                PS_W[mi].ready = [tok]
                PS_W[mi].readers = []
            for mi in range(4):
                m = wave * 4 + mi
                bank = PS_W[mi]
                if m < G:
                    g = m
                    ub = r_uraw.get()
                    act(lambda e, g=g, bank=bank, ub=ub: e.activation(
                        ub.ap[:, 3:SEC + 3], bank.ap[:, 0:SEC], AF.Identity,
                        bias=c_biu.ap[:, g:g + 1]),
                        reads=(bank, c_biu), writes=(ub,))
                    _htok = dve(lambda e, g=g, ub=ub: e.tensor_copy(
                        ub.ap[:, 0:3], st_halo.ap[:, g * 3:(g + 1) * 3]),
                        reads=(st_halo,), extra=tuple(ub.ready))
                    ub.ready = list(ub.ready) + [_htok]
                    a1b = r_small.get()
                    dve(lambda e, g=g, ub=ub, a1b=a1b: e.tensor_scalar(
                        a1b.ap[:, 0:SEC], ub.ap[:, 0:SEC],
                        c_convw.ap[:, g * 4:g * 4 + 1], c_bconv.ap[:, g:g + 1],
                        op0=OP.mult, op1=OP.add), reads=(ub, c_convw, c_bconv),
                        writes=(a1b,))
                    a2b = r_small.get()
                    dve(lambda e, g=g, ub=ub, a1b=a1b, a2b=a2b: e.scalar_tensor_tensor(
                        a2b.ap[:, 0:SEC], ub.ap[:, 1:1 + SEC],
                        c_convw.ap[:, g * 4 + 1:g * 4 + 2], a1b.ap[:, 0:SEC],
                        op0=OP.mult, op1=OP.add), reads=(ub, c_convw, a1b),
                        writes=(a2b,))
                    a3b = r_small.get()
                    dve(lambda e, g=g, ub=ub, a2b=a2b, a3b=a3b: e.scalar_tensor_tensor(
                        a3b.ap[:, 0:SEC], ub.ap[:, 2:2 + SEC],
                        c_convw.ap[:, g * 4 + 2:g * 4 + 3], a2b.ap[:, 0:SEC],
                        op0=OP.mult, op1=OP.add), reads=(ub, c_convw, a2b),
                        writes=(a3b,))
                    zb = r_small.get()
                    dve(lambda e, g=g, ub=ub, a3b=a3b, zb=zb: e.scalar_tensor_tensor(
                        zb.ap[:, 0:SEC], ub.ap[:, 3:3 + SEC],
                        c_convw.ap[:, g * 4 + 3:g * 4 + 4], a3b.ap[:, 0:SEC],
                        op0=OP.mult, op1=OP.add), reads=(ub, c_convw, a3b),
                        writes=(zb,))
                    dve(lambda e, g=g, ub=ub: e.tensor_copy(
                        st_halo.ap[:, g * 3:(g + 1) * 3], ub.ap[:, SEC:SEC + 3]),
                        reads=(ub,), writes=(st_halo,))
                    act(lambda e, g=g, zb=zb: e.activation(
                        b_su.ap[:, g * SEC:(g + 1) * SEC], zb.ap[:, 0:SEC],
                        AF.Silu), reads=(zb,), writes=(b_su,))
                else:
                    rti = m - G
                    srt = r_small.get()
                    act(lambda e, rti=rti, bank=bank, srt=srt: e.activation(
                        srt.ap[:, 0:SEC], bank.ap[:, 0:SEC], AF.Silu,
                        bias=c_bir.ap[:, rti:rti + 1]),
                        reads=(bank, c_bir), writes=(srt,))
                    dma_gp(sres_hbm[rti * 128:(rti + 1) * 128, t0:t0 + SEC],
                           srt.ap[:, 0:SEC], reads=(srt,), writes=(trk[("sres", s)],))
            yield

        for h in range(4):
            dma_gp(su_hbm.rearrange("(g p) t -> p g t", p=128)[:, h * 4:(h + 1) * 4,
                                                              t0:t0 + SEC],
                   b_su.ap[:, h * 4 * SEC:(h + 1) * 4 * SEC].rearrange(
                       "p (g t) -> p g t", t=SEC), reads=(b_su,),
                   writes=(trk[("su", s)],))
        yield
        if taps:
            dma_sp(taps_out["tap_su"].rearrange("(g p) t -> p g t", p=128)[:, :, t0:t0 + SEC],
                   b_su.ap[:].rearrange("p (g t) -> p g t", t=SEC), reads=(b_su,))

        bank = PS_W[0]
        for k in range(G):
            tok = pe(lambda e, k=k, bank=bank: e.matmul(
                bank.ap[:, 0:SEC], c_Wx.ap[:, k * 128:(k + 1) * 128],
                b_su.ap[:, k * SEC:(k + 1) * SEC], start=(k == 0), stop=(k == G - 1)),
               reads=(c_Wx, b_su), writes=(bank,) if k == G - 1 else (),
               extra=tuple(bank.ready) + tuple(bank.readers) if k == 0 else (),
               inc=(k == G - 1))
        b_su.readers.append(tok)
        ssec = s // 2
        half = s % 2
        dtb = r_dt.bufs[ssec % 2]
        Bsb = r_Bsb.bufs[ssec % 2]
        Csb = r_Csb.bufs[ssec % 2]
        act(lambda e: e.activation(dtb.ap[0:64, half * SEC:(half + 1) * SEC],
                                   bank.ap[0:64, 0:SEC], AF.Identity),
            reads=(bank,), writes=(dtb,) if half == 1 else (),
            extra=tuple(dtb.ready) + tuple(dtb.readers) if half == 0 else ())
        act(lambda e: e.activation(Bsb.ap[0:16, half * SEC:(half + 1) * SEC],
                                   bank.ap[64:80, 0:SEC], AF.Identity),
            reads=(bank,), writes=(Bsb,) if half == 1 else (),
            extra=tuple(Bsb.ready) + tuple(Bsb.readers) if half == 0 else ())
        act(lambda e: e.activation(Csb.ap[0:16, half * SEC:(half + 1) * SEC],
                                   bank.ap[96:112, 0:SEC], AF.Identity),
            reads=(bank,), writes=(Csb,) if half == 1 else (),
            extra=tuple(Csb.ready) + tuple(Csb.readers) if half == 0 else ())
        if half == 1:
            par = ssec % 2
            dma_gp(bc_hbm[2 * par:2 * par + 1, :].rearrange(
                       "r (n t) -> r n t", t=SS),
                   Bsb.ap[0:16, 0:SS], reads=(Bsb,),
                   writes=(trk[("bcB", ssec % 2)],))
            dma_gp(bc_hbm[2 * par + 1:2 * par + 2, :].rearrange(
                       "r (n t) -> r n t", t=SS),
                   Csb.ap[0:16, 0:SS], reads=(Csb,),
                   writes=(trk[("bcC", ssec % 2)],))

    from collections import deque
    pend = deque()
    pend2 = deque()

    def pump(kk):
        for _ in range(kk):
            if pend:
                try:
                    next(pend[0])
                except StopIteration:
                    pend.popleft()
            if pend2:
                try:
                    next(pend2[0])
                except StopIteration:
                    pend2.popleft()

    def phase_a2(ssec):
        T0 = ssec * SS
        dtb = r_dt.bufs[ssec % 2]
        Bsb = r_Bsb.bufs[ssec % 2]
        Csb = r_Csb.bufs[ssec % 2]
        if taps:
            dma_sp(taps_out["tap_B"][0:NS, T0:T0 + SS], Bsb.ap[0:NS, 0:SS],
                   reads=(Bsb,))
            dma_sp(taps_out["tap_C"][0:NS, T0:T0 + SS], Csb.ap[0:NS, 0:SS],
                   reads=(Csb,))
        for gb in range(G // 2):
            gs = [2 * gb, 2 * gb + 1]
            sug_of = {}
            for gi, g in enumerate(gs):
                sug = r_sug.get()
                sug_of[g] = sug
                dma_sp(sug.ap[:], su_hbm[g * 128:(g + 1) * 128, T0:T0 + SS],
                       writes=(sug,), reads=(trk[("su", 2 * ssec)], trk[("su", 2 * ssec + 1)]))
                for hh in range(2):
                    bank = PS_Y[gi * 2 + hh]
                    pe(lambda e, g=g, hh=hh, bank=bank: e.matmul(
                        bank.ap[:, 0:512], c_Wdt.ap[0:64, g * 128:(g + 1) * 128],
                        dtb.ap[0:64, hh * 512:(hh + 1) * 512], start=True, stop=True),
                       reads=(c_Wdt, dtb), writes=(bank,))
                    sp1 = r_f32.get()
                    dve(lambda e, g=g, bank=bank, sp1=sp1: e.tensor_scalar(
                        sp1.ap[:, 0:512], bank.ap[:, 0:512],
                        c_bdt.ap[:, g:g + 1], 44.0, op0=OP.add, op1=OP.min),
                        reads=(bank, c_bdt), writes=(sp1,))
                    sp2 = r_f32.get()
                    act(lambda e, sp1=sp1, sp2=sp2: e.activation(
                        sp2.ap[:, 0:512], sp1.ap[:, 0:512], AF.Exp),
                        reads=(sp1,), writes=(sp2,))
                    act(lambda e, gi=gi, hh=hh, sp2=sp2: e.activation(
                        b_dl.ap[:, gi * SS + hh * 512:gi * SS + (hh + 1) * 512],
                        sp2.ap[:, 0:512], AF.Ln, bias=1.0),
                        reads=(sp2,), writes=(b_dl,))
                dve(lambda e, gi=gi, sug=sug: e.tensor_tensor(
                    b_du.ap[:, gi * SS:(gi + 1) * SS], sug.ap[:],
                    b_dl.ap[:, gi * SS:(gi + 1) * SS], OP.mult),
                    reads=(sug, b_dl), writes=(b_du,))
            if taps:
                for gi, g in enumerate(gs):
                    dma_sp(taps_out["tap_delta"][g * 128:(g + 1) * 128, T0:T0 + SS],
                           b_dl.ap[:, gi * SS:(gi + 1) * SS], reads=(b_dl,))

            par = ssec % 2
            for n in range(NS):
                if n % 2 == 0:
                    Brep = r_Brep.get()
                    ctx.dma("act", Brep.ap[:, 0:2 * SS],
                            bc_hbm[2 * par:2 * par + 1,
                                   n * SS:(n + 2) * SS].unsqueeze(1)
                            .broadcast_to((1, 128, 2 * SS)),
                            reads=(trk[("bcB", par)],), writes=(Brep,))
                    Crep = r_Crep.get()
                    ctx.dma("act", Crep.ap[:, 0:2 * SS],
                            bc_hbm[2 * par + 1:2 * par + 2,
                                   n * SS:(n + 2) * SS].unsqueeze(1)
                            .broadcast_to((1, 128, 2 * SS)),
                            reads=(trk[("bcC", par)],), writes=(Crep,))
                no = (n % 2) * SS
                for gi, g in enumerate(gs):
                    idx = g * NS + n
                    dA = r_dA.get()
                    act(lambda e, gi=gi, idx=idx, dA=dA: e.activation(
                        dA.ap[:], b_dl.ap[:, gi * SS:(gi + 1) * SS], AF.Exp,
                        scale=c_A.ap[:, idx:idx + 1]),
                        reads=(b_dl, c_A), writes=(dA,))
                    bt = r_bt.get()
                    dve(lambda e, gi=gi, bt=bt, Brep=Brep, no=no: e.tensor_tensor(
                        bt.ap[:], b_du.ap[:, gi * SS:(gi + 1) * SS],
                        Brep.ap[:, no:no + SS],
                        OP.mult), reads=(b_du, Brep), writes=(bt,))
                    hb = r_h.get()
                    dve(lambda e, dA=dA, bt=bt, hb=hb, idx=idx: e.tensor_tensor_scan(
                        hb.ap[:], dA.ap[:], bt.ap[:], st_h.ap[:, idx:idx + 1],
                        op0=OP.mult, op1=OP.add),
                        reads=(dA, bt, st_h), writes=(hb,))
                    dve(lambda e, hb=hb, idx=idx: e.tensor_copy(
                        st_h.ap[:, idx:idx + 1], hb.ap[:, SS - 1:SS]),
                        reads=(hb,), writes=(st_h,))
                    tmp = r_tmp.get()
                    dve(lambda e, hb=hb, tmp=tmp, Crep=Crep, no=no: e.tensor_tensor(
                        tmp.ap[:], hb.ap[:], Crep.ap[:, no:no + SS], OP.mult),
                        reads=(hb, Crep), writes=(tmp,))
                    for hh in range(2):
                        bank = PS_Y[gi * 2 + hh]
                        pe(lambda e, bank=bank, tmp=tmp, hh=hh, n=n: e.matmul(
                            bank.ap[:, 0:512], c_eye.ap[:],
                            tmp.ap[:, hh * 512:(hh + 1) * 512],
                            start=(n == 0), stop=False),
                            reads=(c_eye, tmp),
                            extra=tuple(bank.ready) + tuple(bank.readers) if n == 0 else (),
                            inc=True)
                    pump(1)
            for gi, g in enumerate(gs):
                for hh in range(2):
                    bank = PS_Y[gi * 2 + hh]
                    pe(lambda e, g=g, hh=hh, bank=bank, sug=sug_of[g]: e.matmul(
                        bank.ap[:, 0:512], c_Dd.ap[:, g * 128:(g + 1) * 128],
                        sug.ap[:, hh * 512:(hh + 1) * 512],
                        start=False, stop=True),
                       reads=(c_Dd, sug_of[g]), writes=(bank,))
            for gi, g in enumerate(gs):
                yv = r_yv.get()
                for hh in range(2):
                    bank = PS_Y[gi * 2 + hh]
                    act(lambda e, yv=yv, hh=hh, bank=bank: e.activation(
                        yv.ap[:, hh * 512:(hh + 1) * 512], bank.ap[:, 0:512],
                        AF.Identity),
                        reads=(bank,), writes=(yv,) if hh == 1 else (),
                        extra=tuple(yv.ready) + tuple(yv.readers) if hh == 0 else ())
                sres = r_sres.get()
                dma_sp(sres.ap[:], sres_hbm[g * 128:(g + 1) * 128, T0:T0 + SS],
                       writes=(sres,), reads=(trk[("sres", 2 * ssec)], trk[("sres", 2 * ssec + 1)]))
                yg = r_tmp.get()
                dve(lambda e, yv=yv, sres=sres, yg=yg: e.tensor_tensor(
                    yg.ap[:], yv.ap[:], sres.ap[:], OP.mult),
                    reads=(yv, sres), writes=(yg,))
                for hh in range(2):
                    sidx = 2 * ssec + hh
                    dma_gp(yg3_hbm[0:128, sidx * (G * SEC) + g * SEC:
                                   sidx * (G * SEC) + (g + 1) * SEC],
                           yg.ap[:, hh * SEC:(hh + 1) * SEC],
                           reads=(yg,), writes=(trk[("yg", sidx)],))
                if taps:
                    dma_sp(taps_out["tap_yg"][g * 128:(g + 1) * 128, T0:T0 + SS],
                           yg.ap[:], reads=(yg,))

    def phase_a3_ffn(s):
        t0 = s * SEC
        yield
        ygbufs = {}
        for wave in range(2):
            for k in range(G):
                kg = k // 4
                if k % 4 == 0:
                    wsl = r_wsl.get()
                    dof = wave * (G * 512) + kg * 2048
                    dma_sp(wsl.ap[:, 0:2048], wout_bf[0:128, dof:dof + 2048],
                           writes=(wsl,), reads=(trk["wout"],))
                    wcur = wsl
                    ygb = r_wsl.get()
                    yof = s * (G * SEC) + kg * 4 * SEC
                    dma_sp(ygb.ap[:, 0:2048], yg3_hbm[0:128, yof:yof + 2048],
                           writes=(ygb,), reads=(trk[("yg", s)],))
                    ygbufs[kg] = ygb
                ko = (k - kg * 4) * 512
                ygc = ygbufs[kg]
                for mi in range(4):
                    bank = PS_W[mi]
                    tok = pe(lambda e, bank=bank, w=wcur, mi=mi, ko=ko, k=k,
                             yg=ygc: e.matmul(
                        bank.ap[:, 0:SEC],
                        w.ap[:, ko + mi * 128:ko + (mi + 1) * 128],
                        yg.ap[:, ko:ko + SEC], start=(k == 0), stop=(k == G - 1)),
                       reads=(wcur, ygc),
                       extra=tuple(bank.ready) + tuple(bank.readers) if k == 0 else (),
                       inc=(mi == 3))
                if k % 4 == 3:
                    yield
            for mi in range(4):
                PS_W[mi].ready = [tok]
                PS_W[mi].readers = []
            for mi in range(4):
                m = wave * 4 + mi
                bank = PS_W[mi]
                ob = r_f32.get()
                act(lambda e, m=m, bank=bank, ob=ob: e.activation(
                    ob.ap[:, 0:SEC], bank.ap[:, 0:SEC], AF.Identity,
                    bias=c_bout.ap[:, m:m + 1]), reads=(bank, c_bout), writes=(ob,))
                t2r = r_small.get()
                dma_sp(t2r.ap[:, 0:SEC], t2_hbm[m * 128:(m + 1) * 128, t0:t0 + SEC],
                       writes=(t2r,), reads=(trk[("t2", s)],))
                e1 = r_f32.get()
                dve(lambda e, ob=ob, t2r=t2r, e1=e1: e.tensor_tensor(
                    e1.ap[:, 0:SEC], ob.ap[:, 0:SEC], t2r.ap[:, 0:SEC], OP.add),
                    reads=(ob, t2r), writes=(e1,))
                x3 = r_f32.get()
                dma_sp(x3.ap[:, 0:SEC], xT[m * 128:(m + 1) * 128, t0:t0 + SEC],
                       writes=(x3,))
                dve(lambda e, m=m, e1=e1, x3=x3: e.scalar_tensor_tensor(
                    b_x1.ap[:, m * SEC:(m + 1) * SEC], e1.ap[:, 0:SEC],
                    c_ga.ap[:, m:m + 1], x3.ap[:, 0:SEC],
                    op0=OP.mult, op1=OP.add), reads=(e1, c_ga, x3), writes=(b_x1,))
                yield
        if taps and s == 0:
            dma_sp(taps_out["tap_x1"].rearrange("(ft p) t -> p ft t", p=128)[:, :, :],
                   b_x1.ap[:].rearrange("p (ft t) -> p ft t", t=SEC), reads=(b_x1,))

        for hq in range(2):
            x1bf = r_xbf.get()
            dve(lambda e, hq=hq, x1bf=x1bf: e.tensor_copy(
                x1bf.ap[:].rearrange("p (ft t) -> p ft t", t=QC),
                b_x1.ap[:].rearrange("p (ft t) -> p ft t", t=QC)[:, hq::2, :]),
                reads=(b_x1,), writes=(x1bf,))
            sq4 = r_xbf.get()
            mu4 = r_rep.get()
            rr4 = r_rep.get()
            layernorm_stats(x1bf, lambda k: k * QC, FT, QC, sq4, mu4, rr4)
            for ft in range(FT):
                f1 = r_small.get()
                dve(lambda e, ft=ft, f1=f1, x1bf=x1bf, mu4=mu4: e.tensor_tensor(
                    f1.ap[:, 0:QC], x1bf.ap[:, ft * QC:(ft + 1) * QC],
                    mu4.ap[:, 0:QC], OP.subtract), reads=(x1bf, mu4), writes=(f1,))
                f2 = r_small.get()
                dve(lambda e, f1=f1, f2=f2, rr4=rr4: e.tensor_tensor(
                    f2.ap[:, 0:QC], f1.ap[:, 0:QC], rr4.ap[:, 0:QC], OP.mult),
                    reads=(f1, rr4), writes=(f2,))
                qb1 = r_f32.get()
                act(lambda e, ft=ft, f2=f2, qb1=qb1: e.activation(
                    qb1.ap[:, 0:QC], f2.ap[:, 0:QC], AF.Identity,
                    scale=c_s16cb.ap[:, ft:ft + 1], bias=c_b16sb.ap[:, ft:ft + 1]),
                    reads=(f2, c_s16cb, c_b16sb), writes=(qb1,))
                qb2 = r_f32.get()
                dve(lambda e, qb1=qb1, qb2=qb2: e.tensor_scalar(
                    qb2.ap[:, 0:QC], qb1.ap[:, 0:QC], MAGIC, 127.0,
                    op0=OP.subtract, op1=OP.min), reads=(qb1,), writes=(qb2,))
                dve(lambda e, ft=ft, hq=hq, qb2=qb2: e.tensor_scalar(
                    b_xq.ap[:, ft * SEC + hq * QC:ft * SEC + (hq + 1) * QC],
                    qb2.ap[:, 0:QC], -128.0, None, op0=OP.max),
                    reads=(qb2,), writes=(b_xq,))
                if ft % 2 == 1:
                    yield
        HC = SEC
        for hc in range(1):
            for wave in range(8):
                wbs = {}
                for k in range(FT):
                    kg = k // 4
                    if kg not in wbs:
                        wb = r_wsl.get()
                        dof = wave * (FT * 512) + kg * 2048
                        dma_sp(wb.ap[:, 0:2048], w1q_bf[0:128, dof:dof + 2048],
                               writes=(wb,), reads=(trk["w1q"],))
                        wbs[kg] = wb
                    wb = wbs[kg]
                    ko = (k - kg * 4) * 512
                    for mi in range(4):
                        bank = PS_W[mi]
                        tok = pe(lambda e, k=k, ko=ko, mi=mi, bank=bank, wb=wb,
                                 hc=hc: e.matmul(
                            bank.ap[:, 0:HC],
                            wb.ap[:, ko + mi * 128:ko + (mi + 1) * 128],
                            b_xq.ap[:, k * SEC + hc * HC:k * SEC + (hc + 1) * HC],
                            start=(k == 0), stop=(k == FT - 1)),
                           reads=(wb, b_xq),
                           extra=tuple(bank.ready) + tuple(bank.readers) if k == 0 else (),
                           inc=(mi == 3))
                    if k % 4 == 3:
                        yield
                for mi in range(4):
                    PS_W[mi].ready = [tok]
                    PS_W[mi].readers = []
                for mi in range(4):
                    m = wave * 4 + mi
                    bank = PS_W[mi]
                    tb = r_f32.get()
                    act(lambda e, m=m, hc=hc, bank=bank, tb=tb: e.activation(
                        tb.ap[:, 0:HC], bank.ap[:, 0:HC], AF.Identity,
                        scale=1.0 / 64.0, bias=c_b1m.ap[:, m:m + 1]),
                        reads=(bank, c_b1m), writes=(tb,))
                    tb2 = r_f32.get()
                    dve(lambda e, tb=tb, tb2=tb2: e.tensor_scalar(
                        tb2.ap[:, 0:HC], tb.ap[:, 0:HC], MAGIC, 127.0,
                        op0=OP.subtract, op1=OP.min), reads=(tb,), writes=(tb2,))
                    act(lambda e, m=m, hc=hc, tb2=tb2: e.activation(
                        b_h2q.ap[:, m * HC:(m + 1) * HC], tb2.ap[:, 0:HC],
                        AF.Relu), reads=(tb2,), writes=(b_h2q,))
                yield
            for wave in range(2):
                for k in range(HT):
                    kg = k // 4
                    if k % 4 == 0:
                        w2s = r_wsl.get()
                        dof = wave * (HT * 512) + kg * 2048
                        dma_sp(w2s.ap[:, 0:2048], w2q_bf[0:128, dof:dof + 2048],
                               writes=(w2s,), reads=(trk["w2q"],))
                        w2cur = w2s
                    ko = (k - kg * 4) * 512
                    for mi in range(4):
                        bank = PS_W[mi]
                        tok = pe(lambda e, k=k, ko=ko, mi=mi, bank=bank,
                                 w2s=w2cur: e.matmul(
                            bank.ap[:, 0:HC],
                            w2s.ap[:, ko + mi * 128:ko + (mi + 1) * 128],
                            b_h2q.ap[:, k * HC:(k + 1) * HC],
                            start=(k == 0), stop=(k == HT - 1)),
                           reads=(w2cur, b_h2q),
                           extra=tuple(bank.ready) + tuple(bank.readers) if k == 0 else (),
                           inc=(mi == 3))
                    if k % 4 == 3:
                        yield
                for mi in range(4):
                    PS_W[mi].ready = [tok]
                    PS_W[mi].readers = []
                for mi in range(4):
                    m = wave * 4 + mi
                    bank = PS_W[mi]
                    o2 = r_f32.get()
                    act(lambda e, m=m, bank=bank, o2=o2: e.activation(
                        o2.ap[:, 0:HC], bank.ap[:, 0:HC], AF.Identity,
                        scale=1.0 / 1024.0, bias=c_b2q.ap[:, m:m + 1]),
                        reads=(bank, c_b2q), writes=(o2,))
                    ot = r_f32.get()
                    dve(lambda e, m=m, o2=o2, ot=ot, hc=hc: e.scalar_tensor_tensor(
                        ot.ap[:, 0:HC], o2.ap[:, 0:HC], c_gb.ap[:, m:m + 1],
                        b_x1.ap[:, m * SEC + hc * HC:m * SEC + (hc + 1) * HC],
                        op0=OP.mult, op1=OP.add),
                        reads=(o2, c_gb, b_x1), writes=(ot,))
                    dma_gp(outT[m * 128:(m + 1) * 128, t0 + hc * HC:t0 + (hc + 1) * HC],
                           ot.ap[:, 0:HC], reads=(ot,))
                    yield

    def drain():
        while pend or pend2:
            pump(1)

    pend2.append(prep_weight(W_out, wout_bf, DI * DM // 128, False, trk["wout"], True))
    pend2.append(prep_weight(w1T, w1q_bf, DM * HID // 128, True, trk["w1q"], True))
    pend2.append(prep_weight(w2T, w2q_bf, HID * DM // 128, True, trk["w2q"], True))
    for _ in phase_a1(0):
        pass
    for _ in phase_a1(1):
        pass
    pend.append(phase_a1(2))
    pend.append(phase_a1(3))
    phase_a2(0)
    for ssec in range(1, NSS):
        drain()
        if 2 * ssec + 2 < NSEC:
            pend.append(phase_a1(2 * ssec + 2))
        if 2 * ssec + 3 < NSEC:
            pend.append(phase_a1(2 * ssec + 3))
        pend.append(phase_a3_ffn(2 * ssec - 2))
        pend.append(phase_a3_ffn(2 * ssec - 1))
        phase_a2(ssec)
    drain()
    pend.append(phase_a3_ffn(NSEC - 2))
    pend.append(phase_a3_ffn(NSEC - 1))
    drain()

    finals = [Tok(s, ctx.cnt[s]) for s in
              Ctx.SP_LANES + Ctx.ACT_LANES + Ctx.GP_LANES if ctx.cnt[s] > 0]
    ctx._emit("sp", lambda e: e.nop(), finals, None, 0)
    ctx._emit("act", lambda e: e.nop(),
              [Tok(s, ctx.cnt[s]) for s in Ctx.ACT_LANES if ctx.cnt[s] > 0],
              None, 0)

    semmap = {k: sems[k] for k in sems}

    with st, nc.Block() as block:

        @block.sync
        def _(sync):
            for thunk in ctx.q["sp"]:
                thunk(sync, semmap)

        @block.vector
        def _(vector):
            for thunk in ctx.q["dve"]:
                thunk(vector, semmap)

        @block.scalar
        def _(scalar):
            for thunk in ctx.q["act"]:
                thunk(scalar, semmap)

        @block.tensor
        def _(tensor):
            for thunk in ctx.q["pe"]:
                thunk(tensor, semmap)

        @block.gpsimd
        def _(gpsimd):
            for thunk in ctx.q["pool"]:
                thunk(gpsimd, semmap)

    return nc, sbufs_bytes[0]


def _vec_tiles(v, ntiles):
    return np.ascontiguousarray(v.reshape(ntiles, 128).T.astype(np.float32))


_prog_cache = {}


def _get_prog(taps=False):
    key = taps
    if key not in _prog_cache:
        _prog_cache[key] = build_program(taps=taps)
    return _prog_cache[key]


def make_inputs(args, taps=False):
    (x, ln_g, ln_b, W_in, b_in, conv_w, conv_b, W_x, W_dt, b_dt, A_log, Dv,
     W_out, b_out, w1, b1, w2, b2, c, wmod_w) = args
    n = 8
    f32 = np.float32
    def _tile_w(W):
        rows, cols = W.shape
        nk, nw = rows // 128, cols // 512
        return np.ascontiguousarray(
            W.astype(f32).reshape(nk, 128, nw, 512)
             .transpose(1, 2, 0, 3).reshape(128, rows * cols // 128))

    W_in_h = _tile_w(W_in)
    convw_h = np.ascontiguousarray(conv_w.T.reshape(G, 128, 4).transpose(1, 0, 2)
                                   .reshape(128, G * 4).astype(f32))
    Wx_h = np.zeros((DI, 128), f32)
    Wx_h[:, 0:64] = W_x[:, 0:64]
    Wx_h[:, 64:80] = W_x[:, 64:80]
    Wx_h[:, 96:112] = W_x[:, 80:96]
    Wdt_h = np.ascontiguousarray(W_dt.astype(f32))
    A_h = np.ascontiguousarray(
        A_log.reshape(G, 128, NS).transpose(1, 0, 2).reshape(128, G * NS).astype(f32))
    Dd_h = np.zeros((DI, 128), f32)
    idx = np.arange(DI)
    Dd_h[idx, idx % 128] = Dv.astype(f32)
    Wout_h = _tile_w(W_out)
    w1T_h = _tile_w(np.ascontiguousarray(w1.T))
    w2T_h = _tile_w(np.ascontiguousarray(w2.T))
    wmod_h = np.ascontiguousarray(wmod_w.astype(f32))
    eye_h = np.eye(128, dtype=f32)
    shared = {
        "W_in": W_in_h, "convw": convw_h, "Wx": Wx_h, "Wdt": Wdt_h,
        "A_log_t": A_h, "D_diag": Dd_h, "W_out": Wout_h, "w1T": w1T_h,
        "w2T": w2T_h, "wmod": wmod_h, "eye": eye_h,
        "v_gamma": _vec_tiles(ln_g, FT), "v_beta": _vec_tiles(ln_b, FT),
        "v_b_in_u": _vec_tiles(b_in[0:DI], G),
        "v_b_in_r": _vec_tiles(b_in[DI:2 * DI], G),
        "v_conv_b": _vec_tiles(conv_b, G), "v_b_dt": _vec_tiles(b_dt, G),
        "v_b_out": _vec_tiles(b_out, FT), "v_b1": _vec_tiles(b1, HT),
        "v_b2": _vec_tiles(b2, FT),
    }
    in_maps = []
    for b in range(n):
        m = dict(shared)
        xt = np.ascontiguousarray(x[b].T.astype(f32))
        m["xT"] = xt
        m["xT4"] = np.ascontiguousarray(
            xt.reshape(FT, 128, L // 256, 256).transpose(1, 2, 0, 3)
            .reshape(128, L * FT))
        m["v_c"] = _vec_tiles(c[b], FT)
        in_maps.append(m)
    return in_maps


def kernel(L0_attn_x, L0_attn_ln_gamma, L0_attn_ln_beta, L0_attn_W_in,
           L0_attn_b_in, L0_attn_conv_w, L0_attn_conv_b, L0_attn_W_x,
           L0_attn_W_dt, L0_attn_b_dt, L0_attn_A_log, L0_attn_D,
           L0_attn_W_out, L0_attn_b_out, L0_ffn_w1, L0_ffn_b1, L0_ffn_w2,
           L0_ffn_b2, L0_c, L0_wmod):
    args = (np.asarray(L0_attn_x), np.asarray(L0_attn_ln_gamma),
            np.asarray(L0_attn_ln_beta), np.asarray(L0_attn_W_in),
            np.asarray(L0_attn_b_in), np.asarray(L0_attn_conv_w),
            np.asarray(L0_attn_conv_b), np.asarray(L0_attn_W_x),
            np.asarray(L0_attn_W_dt), np.asarray(L0_attn_b_dt),
            np.asarray(L0_attn_A_log), np.asarray(L0_attn_D),
            np.asarray(L0_attn_W_out), np.asarray(L0_attn_b_out),
            np.asarray(L0_ffn_w1), np.asarray(L0_ffn_b1),
            np.asarray(L0_ffn_w2), np.asarray(L0_ffn_b2),
            np.asarray(L0_c), np.asarray(L0_wmod))
    in_maps = make_inputs(args)
    nc, _ = _get_prog(taps=False)
    res = run_bass_kernel_spmd(nc, in_maps, core_ids=list(range(8)))
    out = np.stack([np.ascontiguousarray(r["outT"].T) for r in res.results])
    return out.astype(np.float32)


if __name__ == "__main__":
    nc, nbytes = build_program()
    print("built ok, sbuf bytes/partition:", nbytes)


# version 25 (final)
# speedup vs baseline: 1.0777x; 1.0127x over previous
import contextlib
import numpy as np
import concourse.bass as bass
import concourse.mybir as mybir
from concourse.bass_utils import run_bass_kernel_spmd
from concourse.alu_op_type import AluOpType as OP

F32 = mybir.dt.float32
BF16 = mybir.dt.bfloat16
AF = mybir.ActivationFunctionType

MAGIC = float(3 * 2 ** 22)

L = 4096
DM = 1024
DI = 2048
NS = 16
DTR = 64
HID = 4096
SEC = 512
NSEC = L // SEC
SS = 1024
NSS = L // SS
G = DI // 128
FT = DM // 128
HT = HID // 128
EPS = 1e-5


class Tok:
    __slots__ = ("sem", "val")

    def __init__(self, sem, val):
        self.sem = sem
        self.val = val


class Buf:
    __slots__ = ("ap", "ready", "readers", "name")

    def __init__(self, ap, name=""):
        self.ap = ap
        self.ready = []
        self.readers = []
        self.name = name

    def __getitem__(self, idx):
        return self.ap[idx]


class Ctx:
    STREAMS = ("pe", "act", "dve", "pool", "sp", "gp")
    SP_LANES = tuple(f"spd{i}" for i in range(16))
    ACT_LANES = tuple(f"actd{i}" for i in range(8))
    GP_LANES = tuple(f"gpd{i}" for i in range(8))
    SEMS = STREAMS + SP_LANES + ACT_LANES + GP_LANES

    def __init__(self):
        self.q = {s: [] for s in self.STREAMS}
        self.q["gp"] = self.q["pool"]
        self.cnt = {s: 0 for s in self.SEMS}
        self.waited = {s: {} for s in self.STREAMS}
        self.waited["gp"] = self.waited["pool"]
        self.lane_i = {"sp": 0, "act": 0, "gp": 0}

    def _emit(self, stream, builder, waits, inc_sem, inc_n, prewait=None):
        ws = []
        for t in waits:
            if t is None:
                continue
            prev = self.waited[stream].get(t.sem, -1)
            if t.val <= prev:
                continue
            self.waited[stream][t.sem] = t.val
            ws.append((t.sem, t.val))
        if prewait is not None:
            s, v = prewait
            if self.waited[stream].get(s, -1) < v:
                self.waited[stream][s] = v
                ws.append((s, v))
        tok = None
        if inc_sem is not None:
            self.cnt[inc_sem] += inc_n
            tok = Tok(inc_sem, self.cnt[inc_sem])

        def thunk(eng, sems, ws=ws, builder=builder, inc_sem=inc_sem, inc_n=inc_n):
            for s, v in ws:
                eng.wait_ge(sems[s], v)
            inst = builder(eng)
            if inc_sem is not None:
                inst.then_inc(sems[inc_sem], inc_n)

        self.q[stream].append(thunk)
        return tok

    def op(self, stream, builder, reads=(), writes=(), extra=(), inc=True):
        waits = []
        for b in reads:
            if isinstance(b, Buf):
                waits.extend(b.ready)
        for b in writes:
            waits.extend(b.ready)
            waits.extend(b.readers)
        waits.extend(extra)
        sem = stream if inc else None
        tok = self._emit(stream, builder, waits, sem, 1 if inc else 0)
        if tok is not None:
            for b in writes:
                b.ready = [tok]
                b.readers = []
            for b in reads:
                if isinstance(b, Buf):
                    b.readers.append(tok)
        return tok

    def dma(self, stream, out, in_, reads=(), writes=(), extra=()):
        lanes = {"sp": self.SP_LANES, "act": self.ACT_LANES,
                 "gp": self.GP_LANES}[stream]
        lane = lanes[self.lane_i[stream] % len(lanes)]
        self.lane_i[stream] += 1
        if isinstance(out, Buf) and out not in writes:
            writes = tuple(writes) + (out,)
        if isinstance(in_, Buf) and in_ not in reads:
            reads = tuple(reads) + (in_,)
        waits = []
        for b in reads:
            if isinstance(b, Buf):
                waits.extend(b.ready)
        for b in writes:
            waits.extend(b.ready)
            waits.extend(b.readers)
        waits.extend(extra)
        out_ap = out.ap[:] if isinstance(out, Buf) else out
        in_ap = in_.ap[:] if isinstance(in_, Buf) else in_

        def builder(eng, o=out_ap, i=in_ap):
            return eng.dma_start(out=o, in_=i)

        tok = self._emit(stream, builder, waits, lane, 16,
                         prewait=(lane, self.cnt[lane]))
        for b in writes:
            b.ready = [tok]
            b.readers = []
        for b in reads:
            if isinstance(b, Buf):
                b.readers.append(tok)
        return tok

    def dma_multi(self, stream, pairs, wbuf, reads=(), extra=()):
        lanes = {"sp": self.SP_LANES, "act": self.ACT_LANES,
                 "gp": self.GP_LANES}[stream]
        old_waits = list(wbuf.ready) + list(wbuf.readers) + list(extra)
        for b in reads:
            if isinstance(b, Buf):
                old_waits.extend(b.ready)
        toks = []
        for out_ap, in_ap in pairs:
            lane = lanes[self.lane_i[stream] % len(lanes)]
            self.lane_i[stream] += 1

            def builder(eng, o=out_ap, i=in_ap):
                return eng.dma_start(out=o, in_=i)

            toks.append(self._emit(stream, builder, old_waits, lane, 16,
                                   prewait=(lane, self.cnt[lane])))
        wbuf.ready = list(toks)
        wbuf.readers = []
        for b in reads:
            if isinstance(b, Buf):
                b.readers.extend(toks)
        return toks


class Rot:

    def __init__(self, bufs):
        self.bufs = bufs
        self.i = 0

    def get(self):
        b = self.bufs[self.i % len(self.bufs)]
        self.i += 1
        return b


def build_program(taps=False):
    nc = bass.Bass("TRN2", target_bir_lowering=False)

    dram = {}

    def din(name, shape, dtype=F32):
        dram[name] = nc.dram_tensor(name, shape, dtype, kind="ExternalInput").ap()
        return dram[name]

    def dscratch(name, shape, dtype):
        dram[name] = nc.dram_tensor(name, shape, dtype, kind="Internal").ap()
        return dram[name]

    xT = din("xT", [DM, L])
    xT4 = din("xT4", [128, L * FT])
    W_in = din("W_in", [128, DM * 2 * DI // 128])
    convw = din("convw", [128, G * 4])
    Wx = din("Wx", [DI, 128])
    Wdt = din("Wdt", [DTR, DI])
    A_log_t = din("A_log_t", [128, G * NS])
    D_diag = din("D_diag", [DI, 128])
    W_out = din("W_out", [128, DI * DM // 128])
    w1T = din("w1T", [128, DM * HID // 128])
    w2T = din("w2T", [128, HID * DM // 128])
    wmod = din("wmod", [DM, 6 * DM])
    eye = din("eye", [128, 128])
    v_gamma = din("v_gamma", [128, FT])
    v_beta = din("v_beta", [128, FT])
    v_b_in_u = din("v_b_in_u", [128, G])
    v_b_in_r = din("v_b_in_r", [128, G])
    v_conv_b = din("v_conv_b", [128, G])
    v_b_dt = din("v_b_dt", [128, G])
    v_b_out = din("v_b_out", [128, FT])
    v_b1 = din("v_b1", [128, HT])
    v_b2 = din("v_b2", [128, FT])
    v_c = din("v_c", [128, FT])

    outT = nc.dram_tensor("outT", [DM, L], F32, kind="ExternalOutput").ap()

    win_bf = dscratch("win_bf", [128, FT * 2 * DI], BF16)
    wout_bf = dscratch("wout_bf", [128, G * DM], BF16)
    w1q_bf = dscratch("w1q_bf", [128, FT * HID], BF16)
    w2q_bf = dscratch("w2q_bf", [128, HT * DM], BF16)
    t2_hbm = dscratch("t2_hbm", [DM, L], BF16)
    yg3_hbm = dscratch("yg3_hbm", [128, NSEC * G * SEC], BF16)
    bc_hbm = dscratch("bc_hbm", [4, NS * SS], BF16)
    su_hbm = dscratch("su_hbm", [DI, L], BF16)
    sres_hbm = dscratch("sres_hbm", [DI, L], BF16)
    trk = {nm: Buf(None, nm) for nm in ("win", "wout", "w1q", "w2q")}
    for nm in ("t2", "su", "sres", "yg"):
        for s_ in range(NSEC):
            trk[(nm, s_)] = Buf(None, f"{nm}{s_}")
    for nm in ("bcB", "bcC"):
        for s_ in range(2):
            trk[(nm, s_)] = Buf(None, f"{nm}{s_}")

    taps_out = {}
    if taps:
        for nm, shape, tdt in [("tap_xn", [DM, L], BF16), ("tap_su", [DI, L], BF16),
                               ("tap_delta", [DI, L], BF16), ("tap_B", [NS, L], BF16),
                               ("tap_C", [NS, L], BF16), ("tap_yg", [DI, L], BF16),
                               ("tap_x1", [DM, SEC], BF16), ("tap_t2", [DM, L], BF16),
                               ("tap_x", [DM, L], F32), ("tap_modv", [128, 48], F32),
                               ("tap_sc", [128, FT], F32), ("tap_wq", [128, 128], BF16),
                               ("tap_scbf", [128, FT], BF16), ("tap_bank0", [128, 1], F32),
                               ("tap_snap", [128, FT], BF16)]:
            taps_out[nm] = nc.dram_tensor(nm, shape, tdt, kind="ExternalOutput").ap()

    st = contextlib.ExitStack()
    sbufs_bytes = [0]

    def sb(name, shape, dtype):
        t = st.enter_context(nc.sbuf_tensor(name, shape, dtype))
        esz = 4 if dtype == F32 else 2
        sbufs_bytes[0] += shape[1] * esz
        return Buf(t, name)

    def sbrot(name, n, shape, dtype):
        return Rot([sb(f"{name}{i}", shape, dtype) for i in range(n)])

    c_ones = sb("c_ones", [128, 1], BF16)
    c_eye = sb("c_eye", [128, 128], BF16)
    c_A = sb("c_A", [128, G * NS], F32)
    c_convw = sb("c_convw", [128, G * 4], F32)
    c_Dd = sb("c_Dd", [128, G * 128], BF16)
    c_Wx = sb("c_Wx", [128, G * 128], BF16)
    c_Wdt = sb("c_Wdt", [64, DI], BF16)
    c_gam = sb("c_gam", [128, FT], F32)
    c_bet = sb("c_bet", [128, FT], F32)
    c_biu = sb("c_biu", [128, G], F32)
    c_bir = sb("c_bir", [128, G], F32)
    c_bir_h = sb("c_bir_h", [128, G], F32)
    c_bconv = sb("c_bconv", [128, G], F32)
    c_bdt = sb("c_bdt", [128, G], F32)
    c_bout = sb("c_bout", [128, FT], F32)
    c_b1m = sb("c_b1m", [128, HT], F32)
    c_b2q = sb("c_b2q", [128, FT], F32)
    c_sa = sb("c_sa", [128, FT], F32)
    c_ca1 = sb("c_ca1", [128, FT], F32)
    c_ga = sb("c_ga", [128, FT], F32)
    c_gb = sb("c_gb", [128, FT], F32)
    c_s16cb = sb("c_s16cb", [128, FT], F32)
    c_b16sb = sb("c_b16sb", [128, FT], F32)
    st_h = sb("st_h", [128, G * NS], F32)
    st_halo = sb("st_halo", [128, G * 3], BF16)
    c_cst = sb("c_cst", [128, 7], F32)

    QC = SEC // 2
    r_x = sbrot("r_x", 1, [128, FT * QC], F32)
    r_xbf = sbrot("r_xbf", 2, [128, FT * QC], BF16)
    b_t2 = sb("b_t2", [128, FT * SEC], BF16)
    b_xn = sb("b_xn", [128, FT * SEC], BF16)
    r_uraw = sbrot("r_uraw", 2, [128, SEC + 3], BF16)
    b_su = sb("b_su", [128, G * SEC], BF16)
    r_wsl = sbrot("r_wsl", 2, [128, 2048], BF16)
    r_small = sbrot("r_small", 3, [128, SEC], BF16)
    r_f32 = sbrot("r_f32", 3, [128, SEC], F32)
    r_rep = sbrot("r_rep", 2, [128, QC], BF16)
    r_dt = sbrot("r_dt", 2, [64, SS], BF16)
    r_Bsb = sbrot("r_Bsb", 2, [16, SS], BF16)
    r_Csb = sbrot("r_Csb", 2, [16, SS], BF16)
    b_dl = sb("b_dl", [128, 2 * SS], BF16)
    b_du = sb("b_du", [128, 2 * SS], BF16)
    r_sug = sbrot("r_sug", 2, [128, SS], BF16)
    r_Brep = sbrot("r_Brep", 2, [128, 2 * SS], BF16)
    r_Crep = sbrot("r_Crep", 2, [128, 2 * SS], BF16)
    r_dA = sbrot("r_dA", 2, [128, SS], BF16)
    r_bt = sbrot("r_bt", 2, [128, SS], BF16)
    r_h = sbrot("r_h", 2, [128, SS], F32)
    r_tmp = sbrot("r_tmp", 3, [128, SS], BF16)
    r_yv = sbrot("r_yv", 1, [128, SS], BF16)
    r_sres = sbrot("r_sres", 1, [128, SS], BF16)
    b_x1 = sb("b_x1", [128, FT * SEC], BF16)
    b_xq = sb("b_xq", [128, FT * SEC], BF16)
    b_h2q = sb("b_h2q", [128, HT * SEC], BF16)
    r_wst = sbrot("r_wst", 2, [128, 512], F32)
    r_wbf = sbrot("r_wbf", 2, [128, 512], BF16)
    b_modv = sb("b_modv", [128, 48], F32)
    b_vc = sb("b_vc", [128, FT], F32)
    r_tiny = sbrot("r_tiny", 3, [128, FT], F32)
    r_stat = sbrot("r_stat", 3, [1, QC], F32)
    r_statb = sbrot("r_statb", 2, [1, QC], BF16)

    psum = [Buf(st.enter_context(nc.psum_tensor(f"ps{i}", [128, 512], F32)), f"ps{i}")
            for i in range(8)]
    PS_Y = psum[0:4]
    PS_W = psum[4:8]

    sems = {}
    for s in Ctx.SEMS:
        sems[s] = st.enter_context(nc.semaphore(f"sem_{s}"))

    ctx = Ctx()

    def act(builder, reads=(), writes=(), extra=()):
        return ctx.op("act", builder, reads, writes, extra)

    def dve(builder, reads=(), writes=(), extra=()):
        return ctx.op("dve", builder, reads, writes, extra)

    def pool(builder, reads=(), writes=(), extra=()):
        return ctx.op("pool", builder, reads, writes, extra)

    def pe(builder, reads=(), writes=(), extra=(), inc=True):
        return ctx.op("pe", builder, reads, writes, extra, inc=inc)

    def dma_sp(out, in_, reads=(), writes=(), extra=()):
        return ctx.dma("sp", out, in_, reads, writes, extra)

    def dma_act(out, in_, reads=(), writes=(), extra=()):
        return ctx.dma("act", out, in_, reads, writes, extra)

    def dma_gp(out, in_, reads=(), writes=(), extra=()):
        return ctx.dma("gp", out, in_, reads, writes, extra)

    def bcast(dst_buf, src_ap, cols, reads=()):
        in_b = src_ap.unsqueeze(1).broadcast_to((1, 128, cols))
        return ctx.dma("act", dst_buf.ap[:, 0:cols], in_b, reads=reads,
                       writes=(dst_buf,))

    cst_vals = [EPS, 1.0 / 64.0, 1.0 / 1024.0, -0.5, 0.5, 1.0, MAGIC]
    cst_tok = None
    for i, v in enumerate(cst_vals):
        cst_tok = dve(lambda e, i=i, v=v: e.memset(c_cst.ap[:, i:i + 1], v),
                      writes=(c_cst,))
        try:
            nc.const_aps.register(F32, float(v), c_cst.ap[:, i:i + 1])
        except Exception:
            pass
    ctx._emit("act", lambda e: e.nop(), [cst_tok], None, 0)
    ctx._emit("pool", lambda e: e.nop(), [cst_tok], None, 0)

    dma_sp(c_A, A_log_t)
    dma_sp(c_convw, convw)
    dma_sp(c_gam, v_gamma)
    dma_sp(c_bet, v_beta)
    dma_sp(c_biu, v_b_in_u)
    dma_sp(c_bir, v_b_in_r)
    dma_sp(c_bconv, v_conv_b)
    dma_sp(c_bdt, v_b_dt)
    dma_sp(c_bout, v_b_out)
    dma_sp(b_vc, v_c)

    eye_f = r_wst.get()
    dma_sp(eye_f.ap[:, 0:128], eye, writes=(eye_f,))
    dve(lambda e: e.tensor_copy(c_eye.ap[:], eye_f.ap[:, 0:128]),
        reads=(eye_f,), writes=(c_eye,))
    dve(lambda e: e.memset(c_ones.ap[:], 1.0), writes=(c_ones,))
    dve(lambda e: e.memset(st_h.ap[:], 0.0), writes=(st_h,))
    dve(lambda e: e.memset(st_halo.ap[:], 0.0), writes=(st_halo,))

    tA = r_wst.get()
    act(lambda e: e.activation(tA.ap[:, 0:G * NS], c_A.ap[:], AF.Exp),
        reads=(c_A,), writes=(tA,))
    dve(lambda e: e.tensor_scalar(c_A.ap[:], tA.ap[:, 0:G * NS], -1.0, None, op0=OP.mult),
        reads=(tA,), writes=(c_A,))

    dve(lambda e: e.tensor_scalar(c_bir_h.ap[:], c_bir.ap[:], 0.5, None, op0=OP.mult),
        reads=(c_bir,), writes=(c_bir_h,))

    for g in range(G):
        w = r_wst.get()
        dma_sp(w.ap[:, 0:128], D_diag[g * 128:(g + 1) * 128, :], writes=(w,))
        dve(lambda e, g=g, w=w: e.tensor_scalar(
            c_Dd.ap[:, g * 128:(g + 1) * 128], w.ap[:, 0:128], 1.0, None, op0=OP.mult),
            reads=(w,), writes=(c_Dd,))

    for g in range(G):
        w = r_wst.get()
        dma_sp(w.ap[:, 0:128], Wx[g * 128:(g + 1) * 128, :], writes=(w,))
        dve(lambda e, g=g, w=w: e.tensor_copy(
            c_Wx.ap[:, g * 128:(g + 1) * 128], w.ap[:, 0:128]),
            reads=(w,), writes=(c_Wx,))
    for hw in range(8):
        wdt_f = r_wst.get()
        dma_sp(wdt_f.ap[0:64, 0:256], Wdt[0:64, hw * 256:(hw + 1) * 256],
               writes=(wdt_f,))
        dve(lambda e, hw=hw, wdt_f=wdt_f: e.tensor_copy(
            c_Wdt.ap[0:64, hw * 256:(hw + 1) * 256], wdt_f.ap[0:64, 0:256]),
            reads=(wdt_f,), writes=(c_Wdt,))

    wb = r_wst.get()
    dma_sp(wb.ap[:, 0:HT], v_b1, writes=(wb,))
    q1 = r_f32.get()
    dve(lambda e, q1=q1, wb=wb: e.tensor_scalar(
        q1.ap[:, 0:HT], wb.ap[:, 0:HT], 64.0, MAGIC,
        op0=OP.mult, op1=OP.add), reads=(wb,), writes=(q1,))
    q2 = r_f32.get()
    dve(lambda e, q1=q1, q2=q2: e.tensor_scalar(
        q2.ap[:, 0:HT], q1.ap[:, 0:HT], MAGIC, 127.0,
        op0=OP.subtract, op1=OP.min), reads=(q1,), writes=(q2,))
    q3 = r_f32.get()
    dve(lambda e, q2=q2, q3=q3: e.tensor_scalar(
        q3.ap[:, 0:HT], q2.ap[:, 0:HT], -128.0, None,
        op0=OP.max), reads=(q2,), writes=(q3,))
    dve(lambda e, q3=q3: e.tensor_scalar(
        c_b1m.ap[:], q3.ap[:, 0:HT], 0.25, MAGIC,
        op0=OP.mult, op1=OP.add), reads=(q3,), writes=(c_b1m,))
    wb2 = r_wst.get()
    dma_sp(wb2.ap[:, 0:FT], v_b2, writes=(wb2,))
    q1 = r_tiny.get()
    dve(lambda e, q1=q1, wb2=wb2: e.tensor_scalar(
        q1.ap[:, 0:FT], wb2.ap[:, 0:FT], 64.0, MAGIC,
        op0=OP.mult, op1=OP.add), reads=(wb2,), writes=(q1,))
    q2 = r_tiny.get()
    dve(lambda e, q1=q1, q2=q2: e.tensor_scalar(
        q2.ap[:, 0:FT], q1.ap[:, 0:FT], MAGIC, 127.0,
        op0=OP.subtract, op1=OP.min), reads=(q1,), writes=(q2,))
    q3 = r_tiny.get()
    dve(lambda e, q2=q2, q3=q3: e.tensor_scalar(
        q3.ap[:, 0:FT], q2.ap[:, 0:FT], -128.0, None,
        op0=OP.max), reads=(q2,), writes=(q3,))
    dve(lambda e, q3=q3: e.tensor_scalar(
        c_b2q.ap[:], q3.ap[:, 0:FT], 1.0 / 64.0, None,
        op0=OP.mult), reads=(q3,), writes=(c_b2q,))

    tc = r_tiny.get()
    act(lambda e: e.activation(tc.ap[:, 0:FT], b_vc.ap[:], AF.Tanh, scale=0.5),
        reads=(b_vc,), writes=(tc,))
    scv = r_tiny.get()
    dve(lambda e: e.scalar_tensor_tensor(scv.ap[:, 0:FT], tc.ap[:, 0:FT], 1.0,
                                         b_vc.ap[:], op0=OP.add, op1=OP.mult),
        reads=(tc, b_vc), writes=(scv,))
    sc_bf = r_small.get()
    dve(lambda e: e.tensor_copy(sc_bf.ap[:, 0:FT], scv.ap[:, 0:FT]),
        reads=(scv,), writes=(sc_bf,))
    if taps:
        dma_sp(taps_out["tap_sc"], scv.ap[:, 0:FT], reads=(scv,))
        dma_sp(taps_out["tap_scbf"], sc_bf.ap[:, 0:FT], reads=(sc_bf,))
        snap = sb("dbg_snap", [128, FT], BF16)
        dve(lambda e, snap=snap: e.tensor_copy(snap.ap[:], sc_bf.ap[:, 0:FT]),
            reads=(sc_bf,), writes=(snap,))
        dma_sp(taps_out["tap_snap"], snap.ap[:], reads=(snap,))
    for m in range(48):
        bank = PS_W[m % 2]
        group_bufs = [sc_bf]
        for k in range(FT):
            w = r_wst.get()
            dma_sp(w.ap[:, 0:128], wmod[k * 128:(k + 1) * 128,
                                        m * 128:m * 128 + 128], writes=(w,))
            wq = r_wsl.get()
            dve(lambda e, w=w, wq=wq: e.tensor_copy(wq.ap[:, 0:128], w.ap[:, 0:128]),
                reads=(w,), writes=(wq,))
            group_bufs.append(wq)
            if taps and m == 0 and k == 0:
                dma_sp(taps_out["tap_wq"], wq.ap[:, 0:128], reads=(wq,))
            tok = pe(lambda e, wq=wq, k=k, bank=bank: e.matmul(
                bank.ap[:, 0:1], wq.ap[:, 0:128], sc_bf.ap[:, k:k + 1],
                start=(k == 0), stop=(k == FT - 1)),
                reads=(wq, sc_bf), writes=(bank,) if k == FT - 1 else (),
                extra=tuple(bank.ready) + tuple(bank.readers) if k == 0 else (),
                inc=True)
        if taps and m == 0:
            dbg = r_f32.get()
            act(lambda e, bank=bank, dbg=dbg: e.activation(
                dbg.ap[:, 0:1], bank.ap[:, 0:1], AF.Identity),
                reads=(bank,), writes=(dbg,))
            dma_sp(taps_out["tap_bank0"], dbg.ap[:, 0:1], reads=(dbg,))
        act(lambda e, m=m, bank=bank: e.activation(
            b_modv.ap[:, m:m + 1], bank.ap[:, 0:1], AF.Identity, scale=0.5),
            reads=(bank,), writes=(b_modv,))

    dve(lambda e: e.tensor_copy(c_sa.ap[:], b_modv.ap[:, 0:FT]),
        reads=(b_modv,), writes=(c_sa,))
    dve(lambda e: e.tensor_scalar(c_ca1.ap[:], b_modv.ap[:, FT:2 * FT], 1.0, None,
                                  op0=OP.add), reads=(b_modv,), writes=(c_ca1,))

    def clamp_gate(dst, col0):
        gneg = r_tiny.get()
        dve(lambda e: e.tensor_scalar(gneg.ap[:, 0:FT],
                                      b_modv.ap[:, col0:col0 + FT], -1.0, None,
                                      op0=OP.mult), reads=(b_modv,), writes=(gneg,))
        gabs = r_tiny.get()
        dve(lambda e: e.tensor_tensor(gabs.ap[:, 0:FT], gneg.ap[:, 0:FT],
                                      b_modv.ap[:, col0:col0 + FT], OP.max),
            reads=(gneg, b_modv), writes=(gabs,))
        gmag = r_tiny.get()
        dve(lambda e: e.tensor_scalar(gmag.ap[:, 0:FT], gabs.ap[:, 0:FT], 0.25, None,
                                      op0=OP.max), reads=(gabs,), writes=(gmag,))
        gge = r_tiny.get()
        dve(lambda e: e.tensor_scalar(gge.ap[:, 0:FT],
                                      b_modv.ap[:, col0:col0 + FT], 0.0, None,
                                      op0=OP.is_ge), reads=(b_modv,), writes=(gge,))
        gsgn = r_tiny.get()
        dve(lambda e: e.tensor_scalar(gsgn.ap[:, 0:FT], gge.ap[:, 0:FT], 2.0, -1.0,
                                      op0=OP.mult, op1=OP.add), reads=(gge,), writes=(gsgn,))
        dve(lambda e: e.tensor_tensor(dst.ap[:], gmag.ap[:, 0:FT], gsgn.ap[:, 0:FT],
                                      OP.mult), reads=(gmag, gsgn), writes=(dst,))

    clamp_gate(c_ga, 2 * FT)
    clamp_gate(c_gb, 5 * FT)
    cb1 = r_tiny.get()
    dve(lambda e: e.tensor_scalar(cb1.ap[:, 0:FT], b_modv.ap[:, 4 * FT:5 * FT],
                                  1.0, None, op0=OP.add), reads=(b_modv,), writes=(cb1,))
    dve(lambda e: e.tensor_scalar(c_s16cb.ap[:], cb1.ap[:, 0:FT], 16.0, None,
                                  op0=OP.mult), reads=(cb1,), writes=(c_s16cb,))
    dve(lambda e: e.tensor_scalar(c_b16sb.ap[:], b_modv.ap[:, 3 * FT:4 * FT],
                                  16.0, MAGIC, op0=OP.mult, op1=OP.add),
        reads=(b_modv,), writes=(c_b16sb,))
    if taps:
        dma_sp(taps_out["tap_modv"], b_modv.ap[:], reads=(b_modv,))

    def prep_weight(src_t, dst, total, quant, track, do_yield=False):
        CW = 512
        piece = 0
        for c0 in range(0, total, CW):
            w = r_wst.get()
            dma_sp(w.ap[:, 0:CW], src_t[0:128, c0:c0 + CW], writes=(w,))
            wbf = r_wbf.get()
            if quant:
                t1 = r_wst.get()
                act(lambda e, w=w, t1=t1: e.activation(
                    t1.ap[:, 0:CW], w.ap[:, 0:CW], AF.Identity,
                    scale=64.0, bias=c_cst.ap[:, 6:7]),
                    reads=(w, c_cst), writes=(t1,))
                dve(lambda e, t1=t1: e.tensor_scalar(
                    t1.ap[:, 0:CW], t1.ap[:, 0:CW], MAGIC, 127.0,
                    op0=OP.subtract, op1=OP.min), reads=(t1,), writes=(t1,))
                dve(lambda e, t1=t1, wbf=wbf: e.tensor_scalar(
                    wbf.ap[:, 0:CW], t1.ap[:, 0:CW], -128.0, None,
                    op0=OP.max), reads=(t1,), writes=(wbf,))
            else:
                act(lambda e, w=w, wbf=wbf: e.activation(
                    wbf.ap[:, 0:CW], w.ap[:, 0:CW], AF.Identity),
                    reads=(w,), writes=(wbf,))
            dma_sp(dst[0:128, c0:c0 + CW], wbf.ap[:, 0:CW],
                   reads=(wbf,), writes=(track,))
            piece += 1
            if do_yield and piece % 2 == 0:
                yield

    for _ in prep_weight(W_in, win_bf, DM * 2 * DI // 128, False, trk["win"]):
        pass

    def layernorm_stats(src_bf, col0, n_feat_tiles, width, sq_buf, mu_rep, r_rep_):
        nfeat = float(n_feat_tiles * 128)
        cols = [col0(k) for k in range(n_feat_tiles)]
        col0 = lambda k, cols=cols: cols[k]
        for k in range(n_feat_tiles):
            dve(lambda e, k=k: e.tensor_tensor(
                sq_buf.ap[:, k * width:(k + 1) * width],
                src_bf.ap[:, col0(k):col0(k) + width],
                src_bf.ap[:, col0(k):col0(k) + width], OP.mult),
                reads=(src_bf,), writes=(sq_buf,) if k == n_feat_tiles - 1 else (),
                extra=tuple(sq_buf.ready) + tuple(sq_buf.readers) if k == 0 else ())
        bank_s = PS_W[0]
        bank_q = PS_W[1]
        for k in range(n_feat_tiles):
            t_s = pe(lambda e, k=k: e.matmul(
                bank_s.ap[0:1, 0:width], c_ones.ap[:],
                src_bf.ap[:, col0(k):col0(k) + width],
                start=(k == 0), stop=(k == n_feat_tiles - 1)),
                reads=(c_ones, src_bf),
                writes=(bank_s,) if k == n_feat_tiles - 1 else (),
                extra=tuple(bank_s.ready) + tuple(bank_s.readers) if k == 0 else (),
                inc=(k == n_feat_tiles - 1))
        src_bf.readers.append(t_s)
        for k in range(n_feat_tiles):
            t_q = pe(lambda e, k=k: e.matmul(
                bank_q.ap[0:1, 0:width], c_ones.ap[:],
                sq_buf.ap[:, k * width:(k + 1) * width],
                start=(k == 0), stop=(k == n_feat_tiles - 1)),
                reads=(c_ones, sq_buf),
                writes=(bank_q,) if k == n_feat_tiles - 1 else (),
                extra=tuple(bank_q.ready) + tuple(bank_q.readers) if k == 0 else (),
                inc=(k == n_feat_tiles - 1))
        sq_buf.readers.append(t_q)
        mu = r_stat.get()
        dve(lambda e: e.tensor_scalar(mu.ap[0:1, 0:width], bank_s.ap[0:1, 0:width],
                                      1.0 / nfeat, None, op0=OP.mult),
            reads=(bank_s,), writes=(mu,))
        mub = r_statb.get()
        dve(lambda e: e.tensor_copy(mub.ap[0:1, 0:width], mu.ap[0:1, 0:width]),
            reads=(mu,), writes=(mub,))
        m2 = r_stat.get()
        dve(lambda e: e.tensor_tensor(m2.ap[0:1, 0:width], mu.ap[0:1, 0:width],
                                      mu.ap[0:1, 0:width], OP.mult),
            reads=(mu,), writes=(m2,))
        var = r_stat.get()
        dve(lambda e: e.scalar_tensor_tensor(var.ap[0:1, 0:width],
                                             bank_q.ap[0:1, 0:width], 1.0 / nfeat,
                                             m2.ap[0:1, 0:width],
                                             op0=OP.mult, op1=OP.subtract),
            reads=(bank_q, m2), writes=(var,))
        lnv = r_stat.get()
        act(lambda e: e.activation(lnv.ap[0:1, 0:width], var.ap[0:1, 0:width],
                                   AF.Ln, bias=EPS), reads=(var,), writes=(lnv,))
        rin = r_stat.get()
        act(lambda e: e.activation(rin.ap[0:1, 0:width], lnv.ap[0:1, 0:width],
                                   AF.Exp, scale=-0.5), reads=(lnv,), writes=(rin,))
        rb = r_statb.get()
        dve(lambda e: e.tensor_copy(rb.ap[0:1, 0:width], rin.ap[0:1, 0:width]),
            reads=(rin,), writes=(rb,))
        bcast(mu_rep, mub.ap[0:1, 0:width], width, reads=(mub,))
        bcast(r_rep_, rb.ap[0:1, 0:width], width, reads=(rb,))

    def phase_a1(s):
        t0 = s * SEC
        for hq in range(2):
            tq = t0 + hq * QC
            xb = r_x.get()
            qi = tq // QC
            dma_sp(xb, xT4[0:128, qi * (FT * QC):(qi + 1) * (FT * QC)])
            if taps:
                dma_sp(taps_out["tap_x"].rearrange("(ft p) t -> p ft t", p=128)[:, :, tq:tq + QC],
                       xb.ap[:].rearrange("p (ft t) -> p ft t", t=QC), reads=(xb,))
            xbf = r_xbf.get()
            dve(lambda e, xb=xb, xbf=xbf: e.tensor_copy(xbf.ap[:], xb.ap[:]),
                reads=(xb,), writes=(xbf,))
            sq = r_xbf.get()
            mu1 = r_rep.get()
            rr1 = r_rep.get()
            layernorm_stats(xbf, lambda k: k * QC, FT, QC, sq, mu1, rr1)
            for ft in range(FT):
                d1 = r_small.get()
                dve(lambda e, ft=ft, d1=d1, xb=xb, mu1=mu1: e.tensor_tensor(
                    d1.ap[:, 0:QC], xb.ap[:, ft * QC:(ft + 1) * QC],
                    mu1.ap[:, 0:QC], OP.subtract), reads=(xb, mu1), writes=(d1,))
                d2 = r_small.get()
                dve(lambda e, d1=d1, d2=d2, rr1=rr1: e.tensor_tensor(
                    d2.ap[:, 0:QC], d1.ap[:, 0:QC], rr1.ap[:, 0:QC], OP.mult),
                    reads=(d1, rr1), writes=(d2,))
                act(lambda e, ft=ft, hq=hq, d2=d2: e.activation(
                    b_t2.ap[:, ft * SEC + hq * QC:ft * SEC + (hq + 1) * QC],
                    d2.ap[:, 0:QC], AF.Identity,
                    scale=c_ca1.ap[:, ft:ft + 1], bias=c_sa.ap[:, ft:ft + 1]),
                    reads=(d2, c_ca1, c_sa), writes=(b_t2,))
            sq2 = r_xbf.get()
            mu2 = r_rep.get()
            rr2 = r_rep.get()
            layernorm_stats(b_t2, lambda k: k * SEC + hq * QC, FT, QC, sq2, mu2, rr2)
            for ft in range(FT):
                e1 = r_small.get()
                dve(lambda e, ft=ft, hq=hq, e1=e1, mu2=mu2: e.tensor_tensor(
                    e1.ap[:, 0:QC],
                    b_t2.ap[:, ft * SEC + hq * QC:ft * SEC + (hq + 1) * QC],
                    mu2.ap[:, 0:QC], OP.subtract), reads=(b_t2, mu2), writes=(e1,))
                e2 = r_small.get()
                dve(lambda e, e1=e1, e2=e2, rr2=rr2: e.tensor_tensor(
                    e2.ap[:, 0:QC], e1.ap[:, 0:QC], rr2.ap[:, 0:QC], OP.mult),
                    reads=(e1, rr2), writes=(e2,))
                act(lambda e, ft=ft, hq=hq, e2=e2: e.activation(
                    b_xn.ap[:, ft * SEC + hq * QC:ft * SEC + (hq + 1) * QC],
                    e2.ap[:, 0:QC], AF.Identity,
                    scale=c_gam.ap[:, ft:ft + 1], bias=c_bet.ap[:, ft:ft + 1]),
                    reads=(e2, c_gam, c_bet), writes=(b_xn,))
            yield
        dma_gp(t2_hbm.rearrange("(ft p) t -> p ft t", p=128)[:, :, t0:t0 + SEC],
               b_t2.ap[:].rearrange("p (ft t) -> p ft t", t=SEC), reads=(b_t2,),
               writes=(trk[("t2", s)],))
        if taps:
            dma_sp(taps_out["tap_t2"].rearrange("(ft p) t -> p ft t", p=128)[:, :, t0:t0 + SEC],
                   b_t2.ap[:].rearrange("p (ft t) -> p ft t", t=SEC), reads=(b_t2,))
        if taps:
            dma_sp(taps_out["tap_xn"].rearrange("(ft p) t -> p ft t", p=128)[:, :, t0:t0 + SEC],
                   b_xn.ap[:].rearrange("p (ft t) -> p ft t", t=SEC), reads=(b_xn,))

        for wave in range(8):
            for kg in range(2):
                wb = r_wsl.get()
                dof = wave * (FT * 512) + kg * 2048
                dma_sp(wb.ap[:, 0:2048], win_bf[0:128, dof:dof + 2048],
                       writes=(wb,), reads=(trk["win"],))
                for k in range(kg * 4, kg * 4 + 4):
                    ko = (k - kg * 4) * 512
                    for mi in range(4):
                        bank = PS_W[mi]
                        tok = pe(lambda e, k=k, ko=ko, mi=mi, bank=bank, wb=wb: e.matmul(
                            bank.ap[:, 0:SEC],
                            wb.ap[:, ko + mi * 128:ko + (mi + 1) * 128],
                            b_xn.ap[:, k * SEC:(k + 1) * SEC],
                            start=(k == 0), stop=(k == FT - 1)),
                           reads=(wb, b_xn),
                           extra=tuple(bank.ready) + tuple(bank.readers) if k == 0 else (),
                           inc=(mi == 3))
            for mi in range(4):
                PS_W[mi].ready = [tok]
                PS_W[mi].readers = []
            for mi in range(4):
                m = wave * 4 + mi
                bank = PS_W[mi]
                if m < G:
                    g = m
                    ub = r_uraw.get()
                    act(lambda e, g=g, bank=bank, ub=ub: e.activation(
                        ub.ap[:, 3:SEC + 3], bank.ap[:, 0:SEC], AF.Identity,
                        bias=c_biu.ap[:, g:g + 1]),
                        reads=(bank, c_biu), writes=(ub,))
                    _htok = dve(lambda e, g=g, ub=ub: e.tensor_copy(
                        ub.ap[:, 0:3], st_halo.ap[:, g * 3:(g + 1) * 3]),
                        reads=(st_halo,), extra=tuple(ub.ready))
                    ub.ready = list(ub.ready) + [_htok]
                    a1b = r_small.get()
                    dve(lambda e, g=g, ub=ub, a1b=a1b: e.tensor_scalar(
                        a1b.ap[:, 0:SEC], ub.ap[:, 0:SEC],
                        c_convw.ap[:, g * 4:g * 4 + 1], c_bconv.ap[:, g:g + 1],
                        op0=OP.mult, op1=OP.add), reads=(ub, c_convw, c_bconv),
                        writes=(a1b,))
                    a2b = r_small.get()
                    dve(lambda e, g=g, ub=ub, a1b=a1b, a2b=a2b: e.scalar_tensor_tensor(
                        a2b.ap[:, 0:SEC], ub.ap[:, 1:1 + SEC],
                        c_convw.ap[:, g * 4 + 1:g * 4 + 2], a1b.ap[:, 0:SEC],
                        op0=OP.mult, op1=OP.add), reads=(ub, c_convw, a1b),
                        writes=(a2b,))
                    a3b = r_small.get()
                    dve(lambda e, g=g, ub=ub, a2b=a2b, a3b=a3b: e.scalar_tensor_tensor(
                        a3b.ap[:, 0:SEC], ub.ap[:, 2:2 + SEC],
                        c_convw.ap[:, g * 4 + 2:g * 4 + 3], a2b.ap[:, 0:SEC],
                        op0=OP.mult, op1=OP.add), reads=(ub, c_convw, a2b),
                        writes=(a3b,))
                    zb = r_small.get()
                    dve(lambda e, g=g, ub=ub, a3b=a3b, zb=zb: e.scalar_tensor_tensor(
                        zb.ap[:, 0:SEC], ub.ap[:, 3:3 + SEC],
                        c_convw.ap[:, g * 4 + 3:g * 4 + 4], a3b.ap[:, 0:SEC],
                        op0=OP.mult, op1=OP.add), reads=(ub, c_convw, a3b),
                        writes=(zb,))
                    dve(lambda e, g=g, ub=ub: e.tensor_copy(
                        st_halo.ap[:, g * 3:(g + 1) * 3], ub.ap[:, SEC:SEC + 3]),
                        reads=(ub,), writes=(st_halo,))
                    act(lambda e, g=g, zb=zb: e.activation(
                        b_su.ap[:, g * SEC:(g + 1) * SEC], zb.ap[:, 0:SEC],
                        AF.Silu), reads=(zb,), writes=(b_su,))
                else:
                    rti = m - G
                    srt = r_small.get()
                    act(lambda e, rti=rti, bank=bank, srt=srt: e.activation(
                        srt.ap[:, 0:SEC], bank.ap[:, 0:SEC], AF.Silu,
                        bias=c_bir.ap[:, rti:rti + 1]),
                        reads=(bank, c_bir), writes=(srt,))
                    dma_gp(sres_hbm[rti * 128:(rti + 1) * 128, t0:t0 + SEC],
                           srt.ap[:, 0:SEC], reads=(srt,), writes=(trk[("sres", s)],))
            yield

        for h in range(4):
            dma_gp(su_hbm.rearrange("(g p) t -> p g t", p=128)[:, h * 4:(h + 1) * 4,
                                                              t0:t0 + SEC],
                   b_su.ap[:, h * 4 * SEC:(h + 1) * 4 * SEC].rearrange(
                       "p (g t) -> p g t", t=SEC), reads=(b_su,),
                   writes=(trk[("su", s)],))
        yield
        if taps:
            dma_sp(taps_out["tap_su"].rearrange("(g p) t -> p g t", p=128)[:, :, t0:t0 + SEC],
                   b_su.ap[:].rearrange("p (g t) -> p g t", t=SEC), reads=(b_su,))

        bank = PS_W[0]
        for k in range(G):
            tok = pe(lambda e, k=k, bank=bank: e.matmul(
                bank.ap[:, 0:SEC], c_Wx.ap[:, k * 128:(k + 1) * 128],
                b_su.ap[:, k * SEC:(k + 1) * SEC], start=(k == 0), stop=(k == G - 1)),
               reads=(c_Wx, b_su), writes=(bank,) if k == G - 1 else (),
               extra=tuple(bank.ready) + tuple(bank.readers) if k == 0 else (),
               inc=(k == G - 1))
        b_su.readers.append(tok)
        ssec = s // 2
        half = s % 2
        dtb = r_dt.bufs[ssec % 2]
        Bsb = r_Bsb.bufs[ssec % 2]
        Csb = r_Csb.bufs[ssec % 2]
        act(lambda e: e.activation(dtb.ap[0:64, half * SEC:(half + 1) * SEC],
                                   bank.ap[0:64, 0:SEC], AF.Identity),
            reads=(bank,), writes=(dtb,) if half == 1 else (),
            extra=tuple(dtb.ready) + tuple(dtb.readers) if half == 0 else ())
        act(lambda e: e.activation(Bsb.ap[0:16, half * SEC:(half + 1) * SEC],
                                   bank.ap[64:80, 0:SEC], AF.Identity),
            reads=(bank,), writes=(Bsb,) if half == 1 else (),
            extra=tuple(Bsb.ready) + tuple(Bsb.readers) if half == 0 else ())
        act(lambda e: e.activation(Csb.ap[0:16, half * SEC:(half + 1) * SEC],
                                   bank.ap[96:112, 0:SEC], AF.Identity),
            reads=(bank,), writes=(Csb,) if half == 1 else (),
            extra=tuple(Csb.ready) + tuple(Csb.readers) if half == 0 else ())
        if half == 1:
            par = ssec % 2
            dma_gp(bc_hbm[2 * par:2 * par + 1, :].rearrange(
                       "r (n t) -> r n t", t=SS),
                   Bsb.ap[0:16, 0:SS], reads=(Bsb,),
                   writes=(trk[("bcB", ssec % 2)],))
            dma_gp(bc_hbm[2 * par + 1:2 * par + 2, :].rearrange(
                       "r (n t) -> r n t", t=SS),
                   Csb.ap[0:16, 0:SS], reads=(Csb,),
                   writes=(trk[("bcC", ssec % 2)],))

    from collections import deque
    pend = deque()
    pend2 = deque()

    def pump(kk):
        for _ in range(kk):
            if pend:
                try:
                    next(pend[0])
                except StopIteration:
                    pend.popleft()
            if pend2:
                try:
                    next(pend2[0])
                except StopIteration:
                    pend2.popleft()

    def phase_a2(ssec):
        T0 = ssec * SS
        dtb = r_dt.bufs[ssec % 2]
        Bsb = r_Bsb.bufs[ssec % 2]
        Csb = r_Csb.bufs[ssec % 2]
        if taps:
            dma_sp(taps_out["tap_B"][0:NS, T0:T0 + SS], Bsb.ap[0:NS, 0:SS],
                   reads=(Bsb,))
            dma_sp(taps_out["tap_C"][0:NS, T0:T0 + SS], Csb.ap[0:NS, 0:SS],
                   reads=(Csb,))
        sug_of = {}

        def dt_gi(gb2, gi):
            g = 2 * gb2 + gi
            sug = r_sug.get()
            sug_of[g] = sug
            dma_sp(sug.ap[:], su_hbm[g * 128:(g + 1) * 128, T0:T0 + SS],
                   writes=(sug,), reads=(trk[("su", 2 * ssec)], trk[("su", 2 * ssec + 1)]))
            for hh in range(2):
                bank = PS_Y[gi * 2 + hh]
                pe(lambda e, g=g, hh=hh, bank=bank: e.matmul(
                    bank.ap[:, 0:512], c_Wdt.ap[0:64, g * 128:(g + 1) * 128],
                    dtb.ap[0:64, hh * 512:(hh + 1) * 512], start=True, stop=True),
                   reads=(c_Wdt, dtb), writes=(bank,))
                sp1 = r_f32.get()
                dve(lambda e, g=g, bank=bank, sp1=sp1: e.tensor_scalar(
                    sp1.ap[:, 0:512], bank.ap[:, 0:512],
                    c_bdt.ap[:, g:g + 1], 44.0, op0=OP.add, op1=OP.min),
                    reads=(bank, c_bdt), writes=(sp1,))
                sp2 = r_f32.get()
                act(lambda e, sp1=sp1, sp2=sp2: e.activation(
                    sp2.ap[:, 0:512], sp1.ap[:, 0:512], AF.Exp),
                    reads=(sp1,), writes=(sp2,))
                act(lambda e, gi=gi, hh=hh, sp2=sp2: e.activation(
                    b_dl.ap[:, gi * SS + hh * 512:gi * SS + (hh + 1) * 512],
                    sp2.ap[:, 0:512], AF.Ln, bias=1.0),
                    reads=(sp2,), writes=(b_dl,))
            dve(lambda e, gi=gi, sug=sug: e.tensor_tensor(
                b_du.ap[:, gi * SS:(gi + 1) * SS], sug.ap[:],
                b_dl.ap[:, gi * SS:(gi + 1) * SS], OP.mult),
                reads=(sug, b_dl), writes=(b_du,))
            if taps:
                dma_sp(taps_out["tap_delta"][g * 128:(g + 1) * 128, T0:T0 + SS],
                       b_dl.ap[:, gi * SS:(gi + 1) * SS], reads=(b_dl,))

        for gb in range(G // 2):
            gs = [2 * gb, 2 * gb + 1]
            if gb == 0:
                for gi0 in range(2):
                    dt_gi(gb, gi0)

            par = ssec % 2
            for n in range(NS):
                if n % 2 == 0:
                    Brep = r_Brep.get()
                    ctx.dma("act", Brep.ap[:, 0:2 * SS],
                            bc_hbm[2 * par:2 * par + 1,
                                   n * SS:(n + 2) * SS].unsqueeze(1)
                            .broadcast_to((1, 128, 2 * SS)),
                            reads=(trk[("bcB", par)],), writes=(Brep,))
                    Crep = r_Crep.get()
                    ctx.dma("act", Crep.ap[:, 0:2 * SS],
                            bc_hbm[2 * par + 1:2 * par + 2,
                                   n * SS:(n + 2) * SS].unsqueeze(1)
                            .broadcast_to((1, 128, 2 * SS)),
                            reads=(trk[("bcC", par)],), writes=(Crep,))
                no = (n % 2) * SS
                for gi, g in enumerate(gs):
                    idx = g * NS + n
                    dA = r_dA.get()
                    act(lambda e, gi=gi, idx=idx, dA=dA: e.activation(
                        dA.ap[:], b_dl.ap[:, gi * SS:(gi + 1) * SS], AF.Exp,
                        scale=c_A.ap[:, idx:idx + 1]),
                        reads=(b_dl, c_A), writes=(dA,))
                    bt = r_bt.get()
                    dve(lambda e, gi=gi, bt=bt, Brep=Brep, no=no: e.tensor_tensor(
                        bt.ap[:], b_du.ap[:, gi * SS:(gi + 1) * SS],
                        Brep.ap[:, no:no + SS],
                        OP.mult), reads=(b_du, Brep), writes=(bt,))
                    hb = r_h.get()
                    dve(lambda e, dA=dA, bt=bt, hb=hb, idx=idx: e.tensor_tensor_scan(
                        hb.ap[:], dA.ap[:], bt.ap[:], st_h.ap[:, idx:idx + 1],
                        op0=OP.mult, op1=OP.add),
                        reads=(dA, bt, st_h), writes=(hb,))
                    dve(lambda e, hb=hb, idx=idx: e.tensor_copy(
                        st_h.ap[:, idx:idx + 1], hb.ap[:, SS - 1:SS]),
                        reads=(hb,), writes=(st_h,))
                    tmp = r_tmp.get()
                    dve(lambda e, hb=hb, tmp=tmp, Crep=Crep, no=no: e.tensor_tensor(
                        tmp.ap[:], hb.ap[:], Crep.ap[:, no:no + SS], OP.mult),
                        reads=(hb, Crep), writes=(tmp,))
                    for hh in range(2):
                        bank = PS_Y[gi * 2 + hh]
                        pe(lambda e, bank=bank, tmp=tmp, hh=hh, n=n: e.matmul(
                            bank.ap[:, 0:512], c_eye.ap[:],
                            tmp.ap[:, hh * 512:(hh + 1) * 512],
                            start=(n == 0), stop=False),
                            reads=(c_eye, tmp),
                            extra=tuple(bank.ready) + tuple(bank.readers) if n == 0 else (),
                            inc=True)
                    pump(1)
            for gi, g in enumerate(gs):
                for hh in range(2):
                    bank = PS_Y[gi * 2 + hh]
                    pe(lambda e, g=g, hh=hh, bank=bank, sug=sug_of[g]: e.matmul(
                        bank.ap[:, 0:512], c_Dd.ap[:, g * 128:(g + 1) * 128],
                        sug.ap[:, hh * 512:(hh + 1) * 512],
                        start=False, stop=True),
                       reads=(c_Dd, sug_of[g]), writes=(bank,))
            for gi, g in enumerate(gs):
                yv = r_yv.get()
                for hh in range(2):
                    bank = PS_Y[gi * 2 + hh]
                    act(lambda e, yv=yv, hh=hh, bank=bank: e.activation(
                        yv.ap[:, hh * 512:(hh + 1) * 512], bank.ap[:, 0:512],
                        AF.Identity),
                        reads=(bank,), writes=(yv,) if hh == 1 else (),
                        extra=tuple(yv.ready) + tuple(yv.readers) if hh == 0 else ())
                sres = r_sres.get()
                dma_sp(sres.ap[:], sres_hbm[g * 128:(g + 1) * 128, T0:T0 + SS],
                       writes=(sres,), reads=(trk[("sres", 2 * ssec)], trk[("sres", 2 * ssec + 1)]))
                yg = r_tmp.get()
                dve(lambda e, yv=yv, sres=sres, yg=yg: e.tensor_tensor(
                    yg.ap[:], yv.ap[:], sres.ap[:], OP.mult),
                    reads=(yv, sres), writes=(yg,))
                for hh in range(2):
                    sidx = 2 * ssec + hh
                    dma_gp(yg3_hbm[0:128, sidx * (G * SEC) + g * SEC:
                                   sidx * (G * SEC) + (g + 1) * SEC],
                           yg.ap[:, hh * SEC:(hh + 1) * SEC],
                           reads=(yg,), writes=(trk[("yg", sidx)],))
                if taps:
                    dma_sp(taps_out["tap_yg"][g * 128:(g + 1) * 128, T0:T0 + SS],
                           yg.ap[:], reads=(yg,))
                if gb + 1 < G // 2:
                    dt_gi(gb + 1, gi)

    def phase_a3_ffn(s):
        t0 = s * SEC
        yield
        ygbufs = {}
        for wave in range(2):
            for k in range(G):
                kg = k // 4
                if k % 4 == 0:
                    wsl = r_wsl.get()
                    dof = wave * (G * 512) + kg * 2048
                    dma_sp(wsl.ap[:, 0:2048], wout_bf[0:128, dof:dof + 2048],
                           writes=(wsl,), reads=(trk["wout"],))
                    wcur = wsl
                    ygb = r_wsl.get()
                    yof = s * (G * SEC) + kg * 4 * SEC
                    dma_sp(ygb.ap[:, 0:2048], yg3_hbm[0:128, yof:yof + 2048],
                           writes=(ygb,), reads=(trk[("yg", s)],))
                    ygbufs[kg] = ygb
                ko = (k - kg * 4) * 512
                ygc = ygbufs[kg]
                for mi in range(4):
                    bank = PS_W[mi]
                    tok = pe(lambda e, bank=bank, w=wcur, mi=mi, ko=ko, k=k,
                             yg=ygc: e.matmul(
                        bank.ap[:, 0:SEC],
                        w.ap[:, ko + mi * 128:ko + (mi + 1) * 128],
                        yg.ap[:, ko:ko + SEC], start=(k == 0), stop=(k == G - 1)),
                       reads=(wcur, ygc),
                       extra=tuple(bank.ready) + tuple(bank.readers) if k == 0 else (),
                       inc=(mi == 3))
                if k % 4 == 3:
                    yield
            for mi in range(4):
                PS_W[mi].ready = [tok]
                PS_W[mi].readers = []
            for mi in range(4):
                m = wave * 4 + mi
                bank = PS_W[mi]
                ob = r_f32.get()
                act(lambda e, m=m, bank=bank, ob=ob: e.activation(
                    ob.ap[:, 0:SEC], bank.ap[:, 0:SEC], AF.Identity,
                    bias=c_bout.ap[:, m:m + 1]), reads=(bank, c_bout), writes=(ob,))
                t2r = r_small.get()
                dma_sp(t2r.ap[:, 0:SEC], t2_hbm[m * 128:(m + 1) * 128, t0:t0 + SEC],
                       writes=(t2r,), reads=(trk[("t2", s)],))
                e1 = r_f32.get()
                dve(lambda e, ob=ob, t2r=t2r, e1=e1: e.tensor_tensor(
                    e1.ap[:, 0:SEC], ob.ap[:, 0:SEC], t2r.ap[:, 0:SEC], OP.add),
                    reads=(ob, t2r), writes=(e1,))
                x3 = r_f32.get()
                dma_sp(x3.ap[:, 0:SEC], xT[m * 128:(m + 1) * 128, t0:t0 + SEC],
                       writes=(x3,))
                dve(lambda e, m=m, e1=e1, x3=x3: e.scalar_tensor_tensor(
                    b_x1.ap[:, m * SEC:(m + 1) * SEC], e1.ap[:, 0:SEC],
                    c_ga.ap[:, m:m + 1], x3.ap[:, 0:SEC],
                    op0=OP.mult, op1=OP.add), reads=(e1, c_ga, x3), writes=(b_x1,))
                yield
        if taps and s == 0:
            dma_sp(taps_out["tap_x1"].rearrange("(ft p) t -> p ft t", p=128)[:, :, :],
                   b_x1.ap[:].rearrange("p (ft t) -> p ft t", t=SEC), reads=(b_x1,))

        for hq in range(2):
            x1bf = r_xbf.get()
            dve(lambda e, hq=hq, x1bf=x1bf: e.tensor_copy(
                x1bf.ap[:].rearrange("p (ft t) -> p ft t", t=QC),
                b_x1.ap[:].rearrange("p (ft t) -> p ft t", t=QC)[:, hq::2, :]),
                reads=(b_x1,), writes=(x1bf,))
            sq4 = r_xbf.get()
            mu4 = r_rep.get()
            rr4 = r_rep.get()
            layernorm_stats(x1bf, lambda k: k * QC, FT, QC, sq4, mu4, rr4)
            for ft in range(FT):
                f1 = r_small.get()
                dve(lambda e, ft=ft, f1=f1, x1bf=x1bf, mu4=mu4: e.tensor_tensor(
                    f1.ap[:, 0:QC], x1bf.ap[:, ft * QC:(ft + 1) * QC],
                    mu4.ap[:, 0:QC], OP.subtract), reads=(x1bf, mu4), writes=(f1,))
                f2 = r_small.get()
                dve(lambda e, f1=f1, f2=f2, rr4=rr4: e.tensor_tensor(
                    f2.ap[:, 0:QC], f1.ap[:, 0:QC], rr4.ap[:, 0:QC], OP.mult),
                    reads=(f1, rr4), writes=(f2,))
                qb1 = r_f32.get()
                act(lambda e, ft=ft, f2=f2, qb1=qb1: e.activation(
                    qb1.ap[:, 0:QC], f2.ap[:, 0:QC], AF.Identity,
                    scale=c_s16cb.ap[:, ft:ft + 1], bias=c_b16sb.ap[:, ft:ft + 1]),
                    reads=(f2, c_s16cb, c_b16sb), writes=(qb1,))
                qb2 = r_f32.get()
                dve(lambda e, qb1=qb1, qb2=qb2: e.tensor_scalar(
                    qb2.ap[:, 0:QC], qb1.ap[:, 0:QC], MAGIC, 127.0,
                    op0=OP.subtract, op1=OP.min), reads=(qb1,), writes=(qb2,))
                dve(lambda e, ft=ft, hq=hq, qb2=qb2: e.tensor_scalar(
                    b_xq.ap[:, ft * SEC + hq * QC:ft * SEC + (hq + 1) * QC],
                    qb2.ap[:, 0:QC], -128.0, None, op0=OP.max),
                    reads=(qb2,), writes=(b_xq,))
                if ft % 2 == 1:
                    yield
        HC = SEC
        for hc in range(1):
            for wave in range(8):
                wbs = {}
                for k in range(FT):
                    kg = k // 4
                    if kg not in wbs:
                        wb = r_wsl.get()
                        dof = wave * (FT * 512) + kg * 2048
                        dma_sp(wb.ap[:, 0:2048], w1q_bf[0:128, dof:dof + 2048],
                               writes=(wb,), reads=(trk["w1q"],))
                        wbs[kg] = wb
                    wb = wbs[kg]
                    ko = (k - kg * 4) * 512
                    for mi in range(4):
                        bank = PS_W[mi]
                        tok = pe(lambda e, k=k, ko=ko, mi=mi, bank=bank, wb=wb,
                                 hc=hc: e.matmul(
                            bank.ap[:, 0:HC],
                            wb.ap[:, ko + mi * 128:ko + (mi + 1) * 128],
                            b_xq.ap[:, k * SEC + hc * HC:k * SEC + (hc + 1) * HC],
                            start=(k == 0), stop=(k == FT - 1)),
                           reads=(wb, b_xq),
                           extra=tuple(bank.ready) + tuple(bank.readers) if k == 0 else (),
                           inc=(mi == 3))
                    if k % 4 == 3:
                        yield
                for mi in range(4):
                    PS_W[mi].ready = [tok]
                    PS_W[mi].readers = []
                for mi in range(4):
                    m = wave * 4 + mi
                    bank = PS_W[mi]
                    tb = r_f32.get()
                    act(lambda e, m=m, hc=hc, bank=bank, tb=tb: e.activation(
                        tb.ap[:, 0:HC], bank.ap[:, 0:HC], AF.Identity,
                        scale=1.0 / 64.0, bias=c_b1m.ap[:, m:m + 1]),
                        reads=(bank, c_b1m), writes=(tb,))
                    tb2 = r_f32.get()
                    dve(lambda e, tb=tb, tb2=tb2: e.tensor_scalar(
                        tb2.ap[:, 0:HC], tb.ap[:, 0:HC], MAGIC, 127.0,
                        op0=OP.subtract, op1=OP.min), reads=(tb,), writes=(tb2,))
                    act(lambda e, m=m, hc=hc, tb2=tb2: e.activation(
                        b_h2q.ap[:, m * HC:(m + 1) * HC], tb2.ap[:, 0:HC],
                        AF.Relu), reads=(tb2,), writes=(b_h2q,))
                yield
            for wave in range(2):
                for k in range(HT):
                    kg = k // 4
                    if k % 4 == 0:
                        w2s = r_wsl.get()
                        dof = wave * (HT * 512) + kg * 2048
                        dma_sp(w2s.ap[:, 0:2048], w2q_bf[0:128, dof:dof + 2048],
                               writes=(w2s,), reads=(trk["w2q"],))
                        w2cur = w2s
                    ko = (k - kg * 4) * 512
                    for mi in range(4):
                        bank = PS_W[mi]
                        tok = pe(lambda e, k=k, ko=ko, mi=mi, bank=bank,
                                 w2s=w2cur: e.matmul(
                            bank.ap[:, 0:HC],
                            w2s.ap[:, ko + mi * 128:ko + (mi + 1) * 128],
                            b_h2q.ap[:, k * HC:(k + 1) * HC],
                            start=(k == 0), stop=(k == HT - 1)),
                           reads=(w2cur, b_h2q),
                           extra=tuple(bank.ready) + tuple(bank.readers) if k == 0 else (),
                           inc=(mi == 3))
                    if k % 4 == 3:
                        yield
                for mi in range(4):
                    PS_W[mi].ready = [tok]
                    PS_W[mi].readers = []
                for mi in range(4):
                    m = wave * 4 + mi
                    bank = PS_W[mi]
                    o2 = r_f32.get()
                    act(lambda e, m=m, bank=bank, o2=o2: e.activation(
                        o2.ap[:, 0:HC], bank.ap[:, 0:HC], AF.Identity,
                        scale=1.0 / 1024.0, bias=c_b2q.ap[:, m:m + 1]),
                        reads=(bank, c_b2q), writes=(o2,))
                    ot = r_f32.get()
                    dve(lambda e, m=m, o2=o2, ot=ot, hc=hc: e.scalar_tensor_tensor(
                        ot.ap[:, 0:HC], o2.ap[:, 0:HC], c_gb.ap[:, m:m + 1],
                        b_x1.ap[:, m * SEC + hc * HC:m * SEC + (hc + 1) * HC],
                        op0=OP.mult, op1=OP.add),
                        reads=(o2, c_gb, b_x1), writes=(ot,))
                    dma_gp(outT[m * 128:(m + 1) * 128, t0 + hc * HC:t0 + (hc + 1) * HC],
                           ot.ap[:, 0:HC], reads=(ot,))
                    yield

    def drain():
        while pend or pend2:
            pump(1)

    pend2.append(prep_weight(W_out, wout_bf, DI * DM // 128, False, trk["wout"], True))
    pend2.append(prep_weight(w1T, w1q_bf, DM * HID // 128, True, trk["w1q"], True))
    pend2.append(prep_weight(w2T, w2q_bf, HID * DM // 128, True, trk["w2q"], True))
    for _ in phase_a1(0):
        pass
    for _ in phase_a1(1):
        pass
    pend.append(phase_a1(2))
    pend.append(phase_a1(3))
    phase_a2(0)
    for ssec in range(1, NSS):
        drain()
        if 2 * ssec + 2 < NSEC:
            pend.append(phase_a1(2 * ssec + 2))
        if 2 * ssec + 3 < NSEC:
            pend.append(phase_a1(2 * ssec + 3))
        pend.append(phase_a3_ffn(2 * ssec - 2))
        pend.append(phase_a3_ffn(2 * ssec - 1))
        phase_a2(ssec)
    drain()
    pend.append(phase_a3_ffn(NSEC - 2))
    pend.append(phase_a3_ffn(NSEC - 1))
    drain()

    finals = [Tok(s, ctx.cnt[s]) for s in
              Ctx.SP_LANES + Ctx.ACT_LANES + Ctx.GP_LANES if ctx.cnt[s] > 0]
    ctx._emit("sp", lambda e: e.nop(), finals, None, 0)
    ctx._emit("act", lambda e: e.nop(),
              [Tok(s, ctx.cnt[s]) for s in Ctx.ACT_LANES if ctx.cnt[s] > 0],
              None, 0)

    semmap = {k: sems[k] for k in sems}

    with st, nc.Block() as block:

        @block.sync
        def _(sync):
            for thunk in ctx.q["sp"]:
                thunk(sync, semmap)

        @block.vector
        def _(vector):
            for thunk in ctx.q["dve"]:
                thunk(vector, semmap)

        @block.scalar
        def _(scalar):
            for thunk in ctx.q["act"]:
                thunk(scalar, semmap)

        @block.tensor
        def _(tensor):
            for thunk in ctx.q["pe"]:
                thunk(tensor, semmap)

        @block.gpsimd
        def _(gpsimd):
            for thunk in ctx.q["pool"]:
                thunk(gpsimd, semmap)

    return nc, sbufs_bytes[0]


def _vec_tiles(v, ntiles):
    return np.ascontiguousarray(v.reshape(ntiles, 128).T.astype(np.float32))


_prog_cache = {}


def _get_prog(taps=False):
    key = taps
    if key not in _prog_cache:
        _prog_cache[key] = build_program(taps=taps)
    return _prog_cache[key]


def make_inputs(args, taps=False):
    (x, ln_g, ln_b, W_in, b_in, conv_w, conv_b, W_x, W_dt, b_dt, A_log, Dv,
     W_out, b_out, w1, b1, w2, b2, c, wmod_w) = args
    n = 8
    f32 = np.float32
    def _tile_w(W):
        rows, cols = W.shape
        nk, nw = rows // 128, cols // 512
        return np.ascontiguousarray(
            W.astype(f32).reshape(nk, 128, nw, 512)
             .transpose(1, 2, 0, 3).reshape(128, rows * cols // 128))

    W_in_h = _tile_w(W_in)
    convw_h = np.ascontiguousarray(conv_w.T.reshape(G, 128, 4).transpose(1, 0, 2)
                                   .reshape(128, G * 4).astype(f32))
    Wx_h = np.zeros((DI, 128), f32)
    Wx_h[:, 0:64] = W_x[:, 0:64]
    Wx_h[:, 64:80] = W_x[:, 64:80]
    Wx_h[:, 96:112] = W_x[:, 80:96]
    Wdt_h = np.ascontiguousarray(W_dt.astype(f32))
    A_h = np.ascontiguousarray(
        A_log.reshape(G, 128, NS).transpose(1, 0, 2).reshape(128, G * NS).astype(f32))
    Dd_h = np.zeros((DI, 128), f32)
    idx = np.arange(DI)
    Dd_h[idx, idx % 128] = Dv.astype(f32)
    Wout_h = _tile_w(W_out)
    w1T_h = _tile_w(np.ascontiguousarray(w1.T))
    w2T_h = _tile_w(np.ascontiguousarray(w2.T))
    wmod_h = np.ascontiguousarray(wmod_w.astype(f32))
    eye_h = np.eye(128, dtype=f32)
    shared = {
        "W_in": W_in_h, "convw": convw_h, "Wx": Wx_h, "Wdt": Wdt_h,
        "A_log_t": A_h, "D_diag": Dd_h, "W_out": Wout_h, "w1T": w1T_h,
        "w2T": w2T_h, "wmod": wmod_h, "eye": eye_h,
        "v_gamma": _vec_tiles(ln_g, FT), "v_beta": _vec_tiles(ln_b, FT),
        "v_b_in_u": _vec_tiles(b_in[0:DI], G),
        "v_b_in_r": _vec_tiles(b_in[DI:2 * DI], G),
        "v_conv_b": _vec_tiles(conv_b, G), "v_b_dt": _vec_tiles(b_dt, G),
        "v_b_out": _vec_tiles(b_out, FT), "v_b1": _vec_tiles(b1, HT),
        "v_b2": _vec_tiles(b2, FT),
    }
    in_maps = []
    for b in range(n):
        m = dict(shared)
        xt = np.ascontiguousarray(x[b].T.astype(f32))
        m["xT"] = xt
        m["xT4"] = np.ascontiguousarray(
            xt.reshape(FT, 128, L // 256, 256).transpose(1, 2, 0, 3)
            .reshape(128, L * FT))
        m["v_c"] = _vec_tiles(c[b], FT)
        in_maps.append(m)
    return in_maps


def kernel(L0_attn_x, L0_attn_ln_gamma, L0_attn_ln_beta, L0_attn_W_in,
           L0_attn_b_in, L0_attn_conv_w, L0_attn_conv_b, L0_attn_W_x,
           L0_attn_W_dt, L0_attn_b_dt, L0_attn_A_log, L0_attn_D,
           L0_attn_W_out, L0_attn_b_out, L0_ffn_w1, L0_ffn_b1, L0_ffn_w2,
           L0_ffn_b2, L0_c, L0_wmod):
    args = (np.asarray(L0_attn_x), np.asarray(L0_attn_ln_gamma),
            np.asarray(L0_attn_ln_beta), np.asarray(L0_attn_W_in),
            np.asarray(L0_attn_b_in), np.asarray(L0_attn_conv_w),
            np.asarray(L0_attn_conv_b), np.asarray(L0_attn_W_x),
            np.asarray(L0_attn_W_dt), np.asarray(L0_attn_b_dt),
            np.asarray(L0_attn_A_log), np.asarray(L0_attn_D),
            np.asarray(L0_attn_W_out), np.asarray(L0_attn_b_out),
            np.asarray(L0_ffn_w1), np.asarray(L0_ffn_b1),
            np.asarray(L0_ffn_w2), np.asarray(L0_ffn_b2),
            np.asarray(L0_c), np.asarray(L0_wmod))
    in_maps = make_inputs(args)
    nc, _ = _get_prog(taps=False)
    res = run_bass_kernel_spmd(nc, in_maps, core_ids=list(range(8)))
    out = np.stack([np.ascontiguousarray(r["outT"].T) for r in res.results])
    return out.astype(np.float32)


if __name__ == "__main__":
    nc, nbytes = build_program()
    print("built ok, sbuf bytes/partition:", nbytes)
